# version 03
# baseline (speedup 1.0000x reference)
import math
import numpy as np
import ml_dtypes
import concourse.bass as bass
import concourse.mybir as mybir
from concourse.bass_utils import run_bass_kernel_spmd

F32 = mybir.dt.float32
BF16 = mybir.dt.bfloat16
AF = mybir.ActivationFunctionType
ALU = mybir.AluOpType
AX = mybir.AxisListType
bfdt = ml_dtypes.bfloat16

B, S, D = 2, 4096, 4096
HQ, HKV, DH = 32, 8, 128
GRP = HQ // HKV
NT = 8
TL = 1024
R = 512
NE, KD = 8, 128
EPS = 1e-5
ISQ = 1.0 / math.sqrt(DH)
NEG = -1e9

DEBUG_OUTPUTS = False
BUILD_PHASES = 99
PHASE2_CUT = "full"


class Prog:
    ENGS = ("sync", "gpsimd", "tensor", "vector", "scalar")
    ENG_SEMS = ("PE", "DV", "AC", "PO")
    DMA_SEMS = ("CONST", "XL0", "XL1", "KV0", "KV1", "MN", "MS",
                "KS0", "KS1", "VS", "OS0", "OS1",
                "WP0", "WP1", "WS0", "WS1", "WS2", "WS3", "WM", "DBG")
    CC_SEMS = ("CCM", "CCK")
    ALL_SEMS = ENG_SEMS + DMA_SEMS + CC_SEMS

    def __init__(self, nc, es):
        self.nc = nc
        self.q = {e: [] for e in self.ENGS}
        self.sem = {}
        self.cnt = {}
        self._es = es

    def mksem(self, name):
        self.sem[name] = self._es.enter_context(self.nc.semaphore(name))
        self.cnt[name] = 0

    def wait(self, eng, sem, val):
        if val <= 0:
            return
        h = self.sem[sem]
        self.q[eng].append(lambda e, h=h, v=val: e.wait_ge(h, v))

    def op(self, eng, fn, inc=None, n=1):
        if inc is None:
            self.q[eng].append(fn)
            return None
        self.cnt[inc] += n
        v = self.cnt[inc]
        h = self.sem[inc]
        self.q[eng].append(lambda e, fn=fn, h=h, n=n: fn(e).then_inc(h, n))
        return v

    def val(self, sem):
        return self.cnt[sem]

    def barrier(self):
        marks = [(s, self.val(s)) for s in self.ENG_SEMS + self.DMA_SEMS]
        for eng in self.ENGS:
            for sem, v in marks:
                self.wait(eng, sem, v)

    def finalize(self):
        with self.nc.Block() as block:
            for eng in self.ENGS:
                ops = self.q[eng]

                def run(e, ops=ops):
                    for f in ops:
                        f(e)

                getattr(block, eng)(run)


def build_program():
    from contextlib import ExitStack

    nc = bass.Bass("TRN2", target_bir_lowering=False, debug=False)
    es = ExitStack()

    def dram(name, shape, dtype, kind):
        return nc.dram_tensor(name, shape, dtype, kind=kind).ap()

    x_in = dram("x_in", [TL, D], F32, "ExternalInput")
    wq_h = dram("wq_h", [HQ, 128, D], F32, "ExternalInput")
    wk_h = dram("wk_h", [HKV, 128, D], F32, "ExternalInput")
    wv_h = dram("wv_h", [32, 2, 128, 512], F32, "ExternalInput")
    wo_h = dram("wo_h", [8, 32, 128, 512], F32, "ExternalInput")
    worg_h = dram("worg_h", [8, 32, 128, 512], F32, "ExternalInput")
    dmat_h = dram("dmat_h", [4, 128, 32 * 128], F32, "ExternalInput")
    umat_h = dram("umat_h", [8, 4, 128, 512], F32, "ExternalInput")
    wqm_h = dram("wqm_h", [128, 32 * 128], F32, "ExternalInput")
    keys_t = dram("keys_t", [KD, NE], F32, "ExternalInput")
    rope_c = dram("rope_c", [128, TL], F32, "ExternalInput")
    rope_s = dram("rope_s", [128, TL], F32, "ExternalInput")
    mask_in = dram("mask_in", [128, 512], BF16, "ExternalInput")
    ident_in = dram("ident_in", [128, 128], BF16, "ExternalInput")
    onec_in = dram("onec_in", [128, 1], BF16, "ExternalInput")
    oner_in = dram("oner_in", [1, 128], BF16, "ExternalInput")
    one1_in = dram("one1_in", [1, 1], BF16, "ExternalInput")
    sel_in = dram("sel_in", [NE, 512], BF16, "ExternalInput")

    y_out = dram("y_out", [TL, D], F32, "ExternalOutput")

    kv_ci = dram("kv_ci", [2048, 1024], BF16, "Internal")
    kv_co = dram("kv_co", [4 * 2048, 1024], BF16, "Internal")
    mn_ci = dram("mn_ci", [128, 32], F32, "Internal")
    mn_co = dram("mn_co", [128, 32], F32, "Internal")
    y1_d = dram("y1_d", [TL, D], F32, "Internal")
    RG = [[0, 1, 2, 3], [4, 5, 6, 7]]

    if DEBUG_OUTPUTS:
        dbg_xn = dram("dbg_xn", [32, 128, TL], BF16, "ExternalOutput")
        dbg_coef = dram("dbg_coef", [128, 4], F32, "ExternalOutput")
        dbg_q = dram("dbg_q", [HQ, 128, TL], BF16, "ExternalOutput")
        dbg_kv = dram("dbg_kv", [4 * 2048, 1024], BF16, "ExternalOutput")
        dbg_att = dram("dbg_att", [HQ, 128, TL], BF16, "ExternalOutput")
        dbg_y1 = dram("dbg_y1", [TL, D], F32, "ExternalOutput")

    sb = lambda name, shape, dt: es.enter_context(nc.sbuf_tensor(name, shape, dt))
    xn_pool = sb("xn_pool", [128, 32768], BF16)
    qa_pool = sb("qa_pool", [128, 32768], BF16)
    f32_pool = sb("f32_pool", [128, 8192], F32)
    wpool = sb("wpool", [128, 8192], BF16)
    ptsb = sb("ptsb", [128, 1024], BF16)
    stage = sb("stage", [128, 1536], BF16)
    ident = sb("ident", [128, 128], BF16)
    onec = sb("onec", [128, 1], BF16)
    oner = sb("oner", [1, 128], BF16)
    one1 = sb("one1", [1, 1], BF16)
    mask = sb("mask", [128, 512], BF16)
    ropec = sb("ropec", [128, TL], F32)
    ropes = sb("ropes", [128, TL], F32)
    selm = sb("selm", [NE, 512], BF16)
    keyst = sb("keyst", [KD, NE], BF16)
    ra = sb("ra", [128, 512], F32)
    rb = sb("rb", [128, 512], F32)
    rt1 = sb("rt1", [128, 512], F32)
    rt2 = sb("rt2", [128, 512], F32)
    colsum = sb("colsum", [128, 32], F32)
    meansb = sb("meansb", [128, 32], F32)
    meanbf = sb("meanbf", [128, 32], BF16)
    stat = sb("stat", [128, 16], F32)
    sums4 = sb("sums4", [128, 32], F32)
    tot4 = sb("tot4", [128, 4], F32)
    rcp4 = sb("rcp4", [128, 4], F32)
    bias4 = sb("bias4", [128, 4], F32)
    scol = sb("scol", [128, 4], F32)
    trowb = sb("trowb", [1, 128], BF16)
    kmrow = sb("kmrow", [1, 32], F32)

    pb = [es.enter_context(nc.psum_tensor(f"pb{i}", [128, 512], F32)) for i in range(8)]

    P = Prog(nc, es)
    for s in Prog.ALL_SEMS:
        P.mksem(s)

    for dst, src in ((ident, ident_in), (onec, onec_in), (oner, oner_in),
                     (one1, one1_in), (mask, mask_in), (ropec, rope_c),
                     (ropes, rope_s), (selm, sel_in)):
        P.op("sync", lambda e, d=dst, s=src: e.dma_start(out=d[:], in_=s[:]), "CONST", 16)
    P.op("gpsimd", lambda e: e.dma_start(out=keyst[:], in_=keys_t[:]), "WM", 16)
    P.barrier()

    def phase1():
        tp_free = [0, 0]
        norm_done = [0] * NT
        stage_free = [0, 0]
        xld = [0] * NT
        for qt in range(NT):
            par = qt % 2
            xbuf = f32_pool[:, par * 4096:(par + 1) * 4096]
            st = lambda i, par=par: stat[:, par * 8 + i: par * 8 + i + 1]
            if qt >= 2:
                P.wait("sync", "AC", norm_done[qt - 2])
            xld[qt] = P.op("sync", lambda e, d=xbuf, q=qt: e.dma_start(
                out=d, in_=x_in[q * 128:(q + 1) * 128, :]), f"XL{par}", 16)
            P.wait("vector", f"XL{par}", xld[qt])
            sink = qa_pool[:, 8192:12288]
            P.wait("scalar", f"XL{par}", xld[qt])
            a0 = P.op("scalar", lambda e, x=xbuf, st=st, s=sink: e.activation(
                out=s, in_=x, func=AF.Square, accum_out=st(2)), "AC")
            v = P.op("vector", lambda e, x=xbuf, st=st: e.reduce_sum(
                st(0), x, axis=AX.X, negate=True), "DV")
            P.wait("vector", "DV", v)
            v = P.op("vector", lambda e, st=st: e.tensor_scalar_mul(st(1), st(0), 1.0 / D), "DV")
            P.wait("vector", "DV", v)
            v = P.op("vector", lambda e, st=st: e.tensor_mul(st(3), st(1), st(1)), "DV")
            P.wait("vector", "AC", a0)
            P.wait("vector", "DV", v)
            v = P.op("vector", lambda e, st=st: e.tensor_scalar(
                st(4), st(2), 1.0 / D, EPS, ALU.mult, ALU.add), "DV")
            P.wait("vector", "DV", v)
            v = P.op("vector", lambda e, st=st: e.tensor_sub(st(5), st(4), st(3)), "DV")
            P.wait("scalar", "DV", v)
            a = P.op("scalar", lambda e, st=st: e.activation(out=st(6), in_=st(5), func=AF.Sqrt), "AC")
            P.wait("vector", "AC", a)
            v = P.op("vector", lambda e, st=st: e.reciprocal(out=st(7), in_=st(6)), "DV")
            P.wait("vector", "DV", v)
            v = P.op("vector", lambda e, st=st: e.tensor_mul(st(3), st(1), st(7)), "DV")
            xst = qa_pool[:, par * 4096:(par + 1) * 4096]
            P.wait("scalar", "DV", v)
            P.wait("scalar", "PE", stage_free[par])
            a = P.op("scalar", lambda e, x=xbuf, o=xst, st=st: e.activation(
                out=o, in_=x, func=AF.Identity, scale=st(7), bias=st(3)), "AC")
            norm_done[qt] = a
            for g in range(8):
                bank = pb[g % 2]
                P.wait("tensor", "AC", a)
                P.wait("tensor", "DV", tp_free[g % 2])
                for u in range(4):
                    dc = g * 4 + u
                    pe_v = P.op("tensor", lambda e, b=bank, u=u, x=xst, dc=dc: e.matmul(
                        b[:, u * 128:(u + 1) * 128], x[:, dc * 128:(dc + 1) * 128],
                        ident[:], start=True, stop=True), "PE" if u == 3 else None)
                P.wait("vector", "PE", pe_v)
                for u in range(4):
                    dc = g * 4 + u
                    dst = xn_pool[:, dc * 1024 + qt * 128: dc * 1024 + (qt + 1) * 128]
                    ev = P.op("vector", lambda e, b=bank, u=u, d=dst: e.tensor_copy(
                        out=d, in_=b[:, u * 128:(u + 1) * 128]), "DV" if u == 3 else None)
                tp_free[g % 2] = ev
            stage_free[par] = pe_v
        P.wait("vector", "DV", max(tp_free[0], tp_free[1]))
        for dc in range(32):
            cv = P.op("vector", lambda e, dc=dc: e.reduce_sum(
                colsum[:, dc:dc + 1], xn_pool[:, dc * 1024:(dc + 1) * 1024], axis=AX.X),
                "DV" if dc == 31 else None)
        P.wait("sync", "DV", cv)
        stv = P.op("sync", lambda e: e.dma_start(out=mn_ci[:], in_=colsum[:]), "MS", 16)
        P.wait("gpsimd", "MS", stv)
        ar = P.op("gpsimd", lambda e: e.collective_compute(
            "AllReduce", ALU.add, RG, [mn_ci[:]], [mn_co[:]]), "CCM", 1)
        return ar

    mean_ar = phase1()
    P.barrier()

    if DEBUG_OUTPUTS:
        for dc in range(32):
            P.op("sync", lambda e, dc=dc: e.dma_start(
                out=dbg_xn[dc], in_=xn_pool[:, dc * 1024:(dc + 1) * 1024]), "DBG", 16)
        P.barrier()

    if BUILD_PHASES <= 1:
        P.finalize()
        return nc, es

    rope_state = {"p1": 0, "v4": 0, "stage_st": {}}

    def rope_evac(bank, th, dst_bf, pe_v, extra_dst_wait=None):
        C = ropec[:, th * 512:(th + 1) * 512]
        Ssg = ropes[:, th * 512:(th + 1) * 512]
        P.wait("vector", "PE", pe_v)
        P.wait("vector", "PO", rope_state["p1"])
        v1 = P.op("vector", lambda e, b=bank: e.tensor_copy(out=rb[64:128, :], in_=b[0:64, :]), "DV")
        v2 = P.op("vector", lambda e, b=bank: e.tensor_copy(out=rb[0:64, :], in_=b[64:128, :]), "DV")
        P.wait("scalar", "PE", pe_v)
        P.wait("scalar", "DV", max(v2, rope_state["v4"]))
        a1 = P.op("scalar", lambda e, b=bank: e.copy(out=ra[:], in_=b[:]), "AC")
        P.wait("vector", "AC", a1)
        v3 = P.op("vector", lambda e, C=C: e.tensor_mul(rt1[:], ra[:], C), "DV")
        P.wait("gpsimd", "DV", max(v2, rope_state["v4"]))
        p1 = P.op("gpsimd", lambda e, Ssg=Ssg: e.tensor_tensor(
            out=rt2[:], in0=rb[:], in1=Ssg, op=ALU.mult), "PO")
        rope_state["p1"] = p1
        P.wait("vector", "PO", p1)
        if extra_dst_wait is not None:
            sem, val = extra_dst_wait
            P.wait("vector", sem, val)
        v4 = P.op("vector", lambda e, d=dst_bf: e.tensor_add(d, rt1[:], rt2[:]), "DV")
        rope_state["v4"] = v4
        return v2, v4

    def phase2():
        wfree = [0, 0]
        bank_free = [0] * 4
        kst_by_th = {0: 0, 1: 0}
        for kvh in range(HKV):
            par = kvh % 2
            wbuf = wpool[:, par * 4096:(par + 1) * 4096]
            P.wait("gpsimd", "PE", wfree[par])
            wld = P.op("gpsimd", lambda e, d=wbuf, k=kvh: e.dma_start(
                out=d, in_=wk_h[k]), f"WP{par}", 16)
            P.wait("tensor", f"WP{par}", wld)
            for th in range(2):
                bslot = (kvh * 2 + th) % 4
                bank = pb[4 + bslot]
                P.wait("tensor", "DV", bank_free[bslot])
                for ic in range(32):
                    mm = P.op("tensor", lambda e, b=bank, w=wbuf, ic=ic, th=th: e.matmul(
                        b[:], w[:, ic * 128:(ic + 1) * 128],
                        xn_pool[:, ic * 1024 + th * 512: ic * 1024 + (th + 1) * 512],
                        start=(ic == 0), stop=(ic == 31)), "PE" if ic == 31 else None)
                if th == 1:
                    wfree[par] = mm
                dst = stage[:, th * 512:(th + 1) * 512]
                if PHASE2_CUT == "kmm":
                    P.wait("vector", "PE", mm)
                    done = P.op("vector", lambda e, b=bank, d=dst: e.tensor_copy(out=d, in_=b[:]), "DV")
                    bank_free[bslot] = done
                    continue
                cons, done = rope_evac(bank, th, dst, mm,
                                       extra_dst_wait=(f"KS{th}", kst_by_th[th]))
                bank_free[bslot] = done
                if PHASE2_CUT == "krope":
                    continue
                P.wait("sync", "DV", done)
                sv = P.op("sync", lambda e, k=kvh, th=th, d=dst: e.dma_start(
                    out=kv_ci[k * 128:(k + 1) * 128, th * 512:(th + 1) * 512], in_=d), f"KS{th}", 16)
                kst_by_th[th] = sv
        if PHASE2_CUT == "k":
            return 0
        vw_free = [P.val("PE")] * 4
        vbank_free = [0] * 4
        vst = 0
        for t in range(NT):
            for dvh in range(2):
                bslot = (t * 2 + dvh) % 4
                bank = pb[bslot]
                first = True
                for ic in range(32):
                    slot = ic % 4
                    wvbuf = wpool[:, slot * 512:(slot + 1) * 512]
                    P.wait("gpsimd", "PE", vw_free[slot])
                    wv = P.op("gpsimd", lambda e, d=wvbuf, ic=ic, dvh=dvh: e.dma_start(
                        out=d, in_=wv_h[ic, dvh]), f"WS{slot}", 16)
                    P.wait("tensor", f"WS{slot}", wv)
                    if first:
                        P.wait("tensor", "AC", vbank_free[bslot])
                        first = False
                    mm = P.op("tensor", lambda e, b=bank, ic=ic, t=t, w=wvbuf: e.matmul(
                        b[:], xn_pool[:, ic * 1024 + t * 128: ic * 1024 + (t + 1) * 128],
                        w, start=(ic == 0), stop=(ic == 31)), "PE")
                    vw_free[slot] = mm
                vdst = stage[:, 1024:1536]
                P.wait("scalar", "PE", mm)
                P.wait("scalar", "VS", vst)
                av = P.op("scalar", lambda e, b=bank, d=vdst: e.copy(out=d, in_=b[:]), "AC")
                vbank_free[bslot] = av
                P.wait("sync", "AC", av)
                vst = P.op("sync", lambda e, t=t, dvh=dvh, d=vdst: e.dma_start(
                    out=kv_ci[1024 + t * 128:1024 + (t + 1) * 128, dvh * 512:(dvh + 1) * 512],
                    in_=d), "VS", 16)
        if PHASE2_CUT == "kv":
            return 0
        P.wait("gpsimd", "KS0", P.val("KS0"))
        P.wait("gpsimd", "KS1", P.val("KS1"))
        P.wait("gpsimd", "VS", P.val("VS"))
        ag = P.op("gpsimd", lambda e: e.collective_compute(
            "AllGather", ALU.bypass, RG, [kv_ci[:]], [kv_co[:]]), "CCK", 1)
        return ag

    kv_ag = phase2()
    P.barrier()

    if BUILD_PHASES <= 2:
        P.finalize()
        return nc, es

    def phase3():
        wfree = [0, 0]
        bank_free = [0] * 4
        for h in range(HQ):
            par = h % 2
            wbuf = wpool[:, par * 4096:(par + 1) * 4096]
            P.wait("gpsimd", "PE", wfree[par])
            wld = P.op("gpsimd", lambda e, d=wbuf, h=h: e.dma_start(
                out=d, in_=wq_h[h]), f"WP{par}", 16)
            P.wait("tensor", f"WP{par}", wld)
            for th in range(2):
                bslot = (h * 2 + th) % 4
                bank = pb[4 + bslot]
                P.wait("tensor", "DV", bank_free[bslot])
                for ic in range(32):
                    mm = P.op("tensor", lambda e, b=bank, w=wbuf, ic=ic, th=th: e.matmul(
                        b[:], w[:, ic * 128:(ic + 1) * 128],
                        xn_pool[:, ic * 1024 + th * 512: ic * 1024 + (th + 1) * 512],
                        start=(ic == 0), stop=(ic == 31)), "PE" if ic == 31 else None)
                if th == 1:
                    wfree[par] = mm
                dst = qa_pool[:, h * 1024 + th * 512: h * 1024 + (th + 1) * 512]
                cons, done = rope_evac(bank, th, dst, mm)
                bank_free[bslot] = done
        return done

    q_done = phase3()
    P.barrier()

    if DEBUG_OUTPUTS:
        for h in range(HQ):
            P.op("sync", lambda e, h=h: e.dma_start(
                out=dbg_q[h], in_=qa_pool[:, h * 1024:(h + 1) * 1024]), "DBG", 16)
        P.wait("sync", "CCK", kv_ag)
        P.op("sync", lambda e: e.dma_start(out=dbg_kv[:], in_=kv_co[:]), "DBG", 16)
        P.barrier()

    if BUILD_PHASES <= 3:
        P.finalize()
        return nc, es

    def phase_coef():
        P.wait("sync", "CCM", mean_ar)
        mld = P.op("sync", lambda e: e.dma_start(out=meansb[:], in_=mn_co[:]), "MN", 16)
        P.wait("vector", "MN", mld)
        v = P.op("vector", lambda e: e.tensor_scalar_mul(meansb[:], meansb[:], 1.0 / S), "DV")
        P.wait("vector", "DV", v)
        v = P.op("vector", lambda e: e.tensor_copy(out=meanbf[:], in_=meansb[:]), "DV")
        wld = P.op("gpsimd", lambda e: e.dma_start(out=wpool[:, 0:4096], in_=wqm_h[:]), "WM", 16)
        P.wait("tensor", "DV", v)
        P.wait("tensor", "WM", wld)
        for ic in range(32):
            mm = P.op("tensor", lambda e, ic=ic: e.matmul(
                pb[7][0:1, 0:128], meanbf[:, ic:ic + 1],
                wpool[:, ic * 128:(ic + 1) * 128], start=(ic == 0), stop=(ic == 31)),
                "PE" if ic == 31 else None)
        P.wait("scalar", "PE", mm)
        a = P.op("scalar", lambda e: e.copy(out=trowb[:], in_=pb[7][0:1, 0:128]), "AC")
        P.wait("tensor", "AC", a)
        mm = P.op("tensor", lambda e: e.matmul(pb[6][:, 0:1], trowb[:], one1[:],
                                               start=True, stop=True), "PE")
        P.wait("vector", "PE", mm)
        v = P.op("vector", lambda e: e.tensor_copy(out=meanbf[:, 0:1], in_=pb[6][:, 0:1]), "DV")
        P.wait("tensor", "DV", v)
        mm = P.op("tensor", lambda e: e.matmul(pb[6][0:1, 16:24], meanbf[:, 0:1], keyst[:],
                                               start=True, stop=True), "PE")
        P.wait("vector", "PE", mm)
        v = P.op("vector", lambda e: e.tensor_scalar_mul(
            kmrow[0:1, 0:8], pb[6][0:1, 16:24], 1.0 / math.sqrt(KD)), "DV")
        P.wait("vector", "DV", v)
        v = P.op("vector", lambda e: e.reduce_max(kmrow[0:1, 8:9], kmrow[0:1, 0:8],
                                                  axis=AX.X, negate=True), "DV")
        P.wait("scalar", "DV", v)
        a = P.op("scalar", lambda e: e.activation(out=kmrow[0:1, 10:18], in_=kmrow[0:1, 0:8],
                                                  func=AF.Exp, bias=kmrow[0:1, 8:9], scale=1.0,
                                                  accum_out=kmrow[0:1, 9:10]), "AC")
        P.wait("vector", "AC", a)
        v = P.op("vector", lambda e: e.reciprocal(out=kmrow[0:1, 8:9], in_=kmrow[0:1, 9:10]), "DV")
        P.wait("vector", "DV", v)
        v = P.op("vector", lambda e: e.tensor_scalar_mul(
            kmrow[0:1, 10:18], kmrow[0:1, 10:18], kmrow[0:1, 8:9]), "DV")
        P.wait("vector", "DV", v)
        v = P.op("vector", lambda e: e.tensor_copy(out=trowb[0:1, 0:8], in_=kmrow[0:1, 10:18]), "DV")
        P.wait("tensor", "DV", v)
        mm = P.op("tensor", lambda e: e.matmul(pb[6][0:8, 8:9], trowb[0:1, 0:8], one1[:],
                                               start=True, stop=True), "PE")
        P.wait("vector", "PE", mm)
        v = P.op("vector", lambda e: e.tensor_copy(out=meanbf[0:8, 1:2], in_=pb[6][0:8, 8:9]), "DV")
        P.wait("tensor", "DV", v)
        for rc in range(4):
            mm = P.op("tensor", lambda e, rc=rc: e.matmul(
                pb[6][:, 2 + rc:3 + rc], selm[:, rc * 128:(rc + 1) * 128], meanbf[0:8, 1:2],
                start=True, stop=True), "PE" if rc == 3 else None)
        P.wait("vector", "PE", mm)
        v = P.op("vector", lambda e: e.tensor_copy(out=scol[:], in_=pb[6][:, 2:6]), "DV")
        return v

    coef_done = phase_coef()
    P.barrier()

    if DEBUG_OUTPUTS:
        P.op("sync", lambda e: e.dma_start(out=dbg_coef[:], in_=scol[:]), "DBG", 16)
        P.barrier()

    def phase4():
        h_sb = wpool[:, 4096:8192]
        hfree = [0, 0]
        last_rc_mm = 0
        for rc in range(4):
            P.wait("gpsimd", "PE", last_rc_mm)
            wld = P.op("gpsimd", lambda e, rc=rc: e.dma_start(
                out=wpool[:, 0:4096], in_=dmat_h[rc]), "WP0", 16)
            P.wait("tensor", "WP0", wld)
            for th in range(2):
                bank = pb[6 + th]
                P.wait("tensor", "DV", hfree[th])
                for ic in range(32):
                    mm = P.op("tensor", lambda e, b=bank, ic=ic, th=th: e.matmul(
                        b[:], wpool[:, ic * 128:(ic + 1) * 128],
                        xn_pool[:, ic * 1024 + th * 512: ic * 1024 + (th + 1) * 512],
                        start=(ic == 0), stop=(ic == 31)), "PE" if ic == 31 else None)
                last_rc_mm = mm
                P.wait("vector", "PE", mm)
                hv = P.op("vector", lambda e, b=bank, rc=rc, th=th: e.tensor_scalar_mul(
                    h_sb[:, rc * 1024 + th * 512: rc * 1024 + (th + 1) * 512],
                    b[:], scol[:, rc:rc + 1]), "DV")
                hfree[th] = hv
        h_done = hv
        wslot_free = [last_rc_mm] * 4
        bank_ev = [0] * NT
        bank_ev[6] = hfree[0]
        bank_ev[7] = hfree[1]
        xl_free = [0, 0]
        out_free = [0, 0]
        y1_last = 0
        for oc in range(8):
            for ic in range(36):
                slot = ic % 4
                wbuf = wpool[:, slot * 512:(slot + 1) * 512]
                P.wait("gpsimd", "PE", wslot_free[slot])
                if ic < 32:
                    wv = P.op("gpsimd", lambda e, d=wbuf, oc=oc, ic=ic: e.dma_start(
                        out=d, in_=worg_h[oc, ic]), f"WS{slot}", 16)
                else:
                    wv = P.op("gpsimd", lambda e, d=wbuf, oc=oc, ic=ic: e.dma_start(
                        out=d, in_=umat_h[oc, ic - 32]), f"WS{slot}", 16)
                P.wait("tensor", f"WS{slot}", wv)
                if oc == 0 and ic == 32:
                    P.wait("tensor", "DV", h_done)
                for t in range(NT):
                    if ic == 0:
                        P.wait("tensor", "DV", bank_ev[t])
                    if ic < 32:
                        lhs = xn_pool[:, ic * 1024 + t * 128: ic * 1024 + (t + 1) * 128]
                    else:
                        rc = ic - 32
                        lhs = h_sb[:, rc * 1024 + t * 128: rc * 1024 + (t + 1) * 128]
                    mm = P.op("tensor", lambda e, b=pb[t], l=lhs, w=wbuf, ic=ic: e.matmul(
                        b[:], l, w, start=(ic == 0), stop=(ic == 35)),
                        "PE" if t == NT - 1 else None)
                wslot_free[slot] = mm
            for t in range(NT):
                par = t % 2
                xs = f32_pool[:, par * 512:(par + 1) * 512]
                od = f32_pool[:, 4096 + par * 512: 4096 + (par + 1) * 512]
                P.wait("sync", "DV", xl_free[par])
                xv = P.op("sync", lambda e, d=xs, t=t, oc=oc: e.dma_start(
                    out=d, in_=x_in[t * 128:(t + 1) * 128, oc * 512:(oc + 1) * 512]), f"XL{par}", 16)
                P.wait("vector", "PE", mm)
                P.wait("vector", f"XL{par}", xv)
                P.wait("vector", f"OS{par}", out_free[par])
                ev = P.op("vector", lambda e, b=pb[t], x=xs, o=od: e.tensor_add(o, b[:], x), "DV")
                xl_free[par] = ev
                bank_ev[t] = ev
                P.wait("sync", "DV", ev)
                sv = P.op("sync", lambda e, o=od, t=t, oc=oc: e.dma_start(
                    out=y1_d[t * 128:(t + 1) * 128, oc * 512:(oc + 1) * 512], in_=o), f"OS{par}", 16)
                out_free[par] = sv
                y1_last = sv
        return y1_last

    y1_last = phase4()
    P.barrier()

    if DEBUG_OUTPUTS:
        P.op("sync", lambda e: e.dma_start(out=dbg_y1[:], in_=y1_d[:]), "DBG", 16)
        P.barrier()

    if BUILD_PHASES <= 4:
        P.finalize()
        return nc, es

    def phase5():
        ksink = xn_pool[:, 16384:16384 + 4096]
        P.wait("sync", "CCK", kv_ag)
        kv_consumed = [0] * HKV

        def load_kv(kvh, wait_pe_val):
            par = kvh % 2
            P.wait("sync", "PE", wait_pe_val)
            last = 0
            for jj in range(4):
                kin = kv_co[jj * 2048 + kvh * 128: jj * 2048 + (kvh + 1) * 128, :]
                kout = xn_pool[:, par * 4096:(par + 1) * 4096].rearrange(
                    "p (kb j i) -> p kb j i", kb=8, j=4)[:, :, jj, :]
                last = P.op("sync", lambda e, o=kout,
                            i=kin.rearrange("p (kb i) -> p kb i", kb=8):
                            e.dma_start(out=o, in_=i), f"KV{par}", 16)
                vin = kv_co[jj * 2048 + 1024: jj * 2048 + 2048, kvh * 128:(kvh + 1) * 128]
                vout = xn_pool[:, 8192 + par * 4096: 8192 + (par + 1) * 4096].rearrange(
                    "p (kb j dv) -> p kb j dv", kb=8, j=4)[:, :, jj, :]
                last = P.op("sync", lambda e, o=vout,
                            i=vin.rearrange("(kb p) dv -> p kb dv", p=128):
                            e.dma_start(out=o, in_=i), f"KV{par}", 16)
            return last

        kld = [0] * HKV
        kld[0] = load_kv(0, 0)

        tr_last_all = 0
        sums_read = [0, 0, 0, 0]
        last_exp = 0
        pt_evac_done = [0, 0]
        ptsb_free = [0, 0]
        av_evac = [0, 0]
        sc_exp = [0, 0]
        for kvh in range(HKV):
            par = kvh % 2
            K_sb = xn_pool[:, par * 4096:(par + 1) * 4096]
            V_sb = xn_pool[:, 8192 + par * 4096: 8192 + (par + 1) * 4096]
            if kvh + 1 < HKV:
                kld[kvh + 1] = load_kv(kvh + 1, kv_consumed[kvh - 1] if kvh >= 1 else 0)
            P.wait("gpsimd", f"KV{par}", kld[kvh])
            P.wait("gpsimd", "PE", tr_last_all)
            pk = P.op("gpsimd", lambda e, K=K_sb: e.tensor_tensor(
                out=ksink[:], in0=K, in1=K, op=ALU.mult), "PO")
            vmax_prev = 0
            for kb in range(8):
                mb = pb[6 + kb % 2]
                P.wait("tensor", "PO", pk)
                P.wait("tensor", "DV", vmax_prev)
                mm = P.op("tensor", lambda e, b=mb, kb=kb: e.matmul(
                    b[0:1, 0:512], onec[:], ksink[:, kb * 512:(kb + 1) * 512],
                    start=True, stop=True), "PE")
                P.wait("vector", "PE", mm)
                vmax_prev = P.op("vector", lambda e, b=mb, kb=kb: e.reduce_max(
                    kmrow[0:1, kb:kb + 1], b[0:1, 0:512], axis=AX.X), "DV")
            P.wait("vector", "DV", vmax_prev)
            km_v = P.op("vector", lambda e: e.reduce_max(
                kmrow[0:1, 8:9], kmrow[0:1, 0:8], axis=AX.X), "DV")
            knorm_mm = mm
            bias_copy = 0
            for hp in range(4):
                h = kvh * GRP + hp
                P.wait("gpsimd", "PE", knorm_mm)
                pq = P.op("gpsimd", lambda e, h=h: e.tensor_tensor(
                    out=ksink[:, 0:1024], in0=qa_pool[:, h * 1024:(h + 1) * 1024],
                    in1=qa_pool[:, h * 1024:(h + 1) * 1024], op=ALU.mult), "PO")
                P.wait("tensor", "PO", pq)
                P.wait("tensor", "DV", max(bias_copy, km_v))
                for th in range(2):
                    mmq = P.op("tensor", lambda e, th=th, b=pb[6 + th]: e.matmul(
                        b[0:1, 0:512], onec[:], ksink[:, th * 512:(th + 1) * 512],
                        start=True, stop=True), "PE" if th == 1 else None)
                knorm_mm = mmq
                P.wait("vector", "PE", mmq)
                v = P.op("vector", lambda e: e.reduce_max(
                    kmrow[0:1, 9:10], pb[6][0:1, 0:512], axis=AX.X), "DV")
                P.wait("vector", "DV", v)
                v = P.op("vector", lambda e: e.reduce_max(
                    kmrow[0:1, 13:14], pb[7][0:1, 0:512], axis=AX.X), "DV")
                P.wait("vector", "DV", v)
                v = P.op("vector", lambda e: e.tensor_max(
                    kmrow[0:1, 9:10], kmrow[0:1, 9:10], kmrow[0:1, 13:14]), "DV")
                P.wait("vector", "DV", max(v, km_v))
                v = P.op("vector", lambda e: e.tensor_mul(
                    kmrow[0:1, 10:11], kmrow[0:1, 8:9], kmrow[0:1, 9:10]), "DV")
                P.wait("scalar", "DV", v)
                aa = P.op("scalar", lambda e: e.activation(
                    out=kmrow[0:1, 11:12], in_=kmrow[0:1, 10:11],
                    func=AF.Sqrt, scale=1.2155), "AC")
                P.wait("vector", "AC", aa)
                v = P.op("vector", lambda e: e.tensor_scalar_mul(
                    kmrow[0:1, 12:13], kmrow[0:1, 11:12], -ISQ), "DV")
                P.wait("vector", "DV", v)
                v = P.op("vector", lambda e: e.tensor_copy(
                    out=trowb[0:1, 1:2], in_=kmrow[0:1, 12:13]), "DV")
                P.wait("tensor", "DV", v)
                mmb = P.op("tensor", lambda e: e.matmul(
                    pb[6][:, 2:3], oner[:], trowb[0:1, 1:2], start=True, stop=True), "PE")
                knorm_mm = mmb
                P.wait("vector", "PE", mmb)
                P.wait("vector", "AC", last_exp)
                bias_copy = P.op("vector", lambda e, hp=hp: e.tensor_copy(
                    out=bias4[:, hp:hp + 1], in_=pb[6][:, 2:3]), "DV")
            bias_done = bias_copy
            for qt in range(NT):
                pnorm_done = [0, 0, 0, 0]
                for hp in range(4):
                    h = kvh * GRP + hp
                    Pbuf = xn_pool[:, 16384 + hp * 4096: 16384 + (hp + 1) * 4096]
                    for kb in range(qt + 1):
                        sbank = pb[kb % 2]
                        P.wait("tensor", "AC", sc_exp[kb % 2])
                        if kb == qt:
                            P.op("tensor", lambda e, b=sbank, h=h, qt=qt, kb=kb, K=K_sb: e.matmul(
                                b[:], qa_pool[:, h * 1024 + qt * 128: h * 1024 + (qt + 1) * 128],
                                K[:, kb * 512:(kb + 1) * 512], start=True, stop=False))
                            mm = P.op("tensor", lambda e, b=sbank: e.matmul(
                                b[:], ident[:], mask[:], start=False, stop=True), "PE")
                        else:
                            mm = P.op("tensor", lambda e, b=sbank, h=h, qt=qt, kb=kb, K=K_sb: e.matmul(
                                b[:], qa_pool[:, h * 1024 + qt * 128: h * 1024 + (qt + 1) * 128],
                                K[:, kb * 512:(kb + 1) * 512], start=True, stop=True), "PE")
                        P.wait("scalar", "PE", mm)
                        P.wait("scalar", "DV", bias_done)
                        P.wait("scalar", "DV", sums_read[hp])
                        P.wait("scalar", "PE", tr_last_all)
                        a = P.op("scalar", lambda e, b=sbank, P_=Pbuf, kb=kb, hp=hp: e.activation(
                            out=P_[:, kb * 512:(kb + 1) * 512], in_=b[:], func=AF.Exp,
                            scale=ISQ, bias=bias4[:, hp:hp + 1],
                            accum_out=sums4[:, hp * 8 + kb: hp * 8 + kb + 1]), "AC")
                        sc_exp[kb % 2] = a
                        last_exp = a
                    P.wait("vector", "AC", a)
                    v = P.op("vector", lambda e, hp=hp, qt=qt: e.reduce_sum(
                        tot4[:, hp:hp + 1], sums4[:, hp * 8: hp * 8 + qt + 1], axis=AX.X), "DV")
                    sums_read[hp] = v
                    P.wait("vector", "DV", v)
                    v = P.op("vector", lambda e, hp=hp: e.reciprocal(
                        out=rcp4[:, hp:hp + 1], in_=tot4[:, hp:hp + 1]), "DV")
                    P.wait("vector", "DV", v)
                    v = P.op("vector", lambda e, P_=Pbuf, qt=qt, hp=hp: e.tensor_scalar_mul(
                        P_[:, 0:(qt + 1) * 512], P_[:, 0:(qt + 1) * 512], rcp4[:, hp:hp + 1]), "DV")
                    pnorm_done[hp] = v
                av_bank = pb[4 + qt % 2]
                P.wait("tensor", "DV", pnorm_done[3])
                P.wait("tensor", "AC", av_evac[qt % 2])
                first_av = True
                for kb in range(qt + 1):
                    for jj in range(4):
                        ptpar = (kb * 4 + jj) % 2
                        ptbank = pb[2 + ptpar]
                        P.wait("tensor", "AC", pt_evac_done[ptpar])
                        for hp in range(4):
                            Pbuf = xn_pool[:, 16384 + hp * 4096: 16384 + (hp + 1) * 4096]
                            mm = P.op("tensor",
                                      lambda e, b=ptbank, P_=Pbuf, kb=kb, jj=jj, hp=hp: e.matmul(
                                          b[:, hp * 128:(hp + 1) * 128],
                                          P_[:, kb * 512 + jj * 128: kb * 512 + jj * 128 + 128],
                                          ident[:], start=True, stop=True),
                                      "PE" if hp == 3 else None)
                        P.wait("scalar", "PE", mm)
                        P.wait("scalar", "PE", ptsb_free[ptpar])
                        aev = P.op("scalar", lambda e, b=ptbank, p=ptpar: e.copy(
                            out=ptsb[:, p * 512:(p + 1) * 512], in_=b[:]), "AC")
                        pt_evac_done[ptpar] = aev
                        P.wait("tensor", "AC", aev)
                        mm = P.op("tensor", lambda e, b=av_bank, V=V_sb, kb=kb, jj=jj,
                                  st=first_av, sp=(kb == qt and jj == 3), p=ptpar: e.matmul(
                            b[:], V[:, (kb * 4 + jj) * 128:(kb * 4 + jj) * 128 + 128],
                            ptsb[:, p * 512:(p + 1) * 512], start=st, stop=sp), "PE")
                        ptsb_free[ptpar] = mm
                        first_av = False
                tr_last_all = mm
                if qt == NT - 1:
                    kv_consumed[kvh] = mm
                P.wait("scalar", "PE", mm)
                for hp in range(4):
                    h = kvh * GRP + hp
                    aev = P.op("scalar", lambda e, b=av_bank, hp=hp, h=h, qt=qt: e.copy(
                        out=qa_pool[:, h * 1024 + qt * 128: h * 1024 + (qt + 1) * 128],
                        in_=b[:, hp * 128:(hp + 1) * 128]), "AC" if hp == 3 else None)
                av_evac[qt % 2] = aev
        return P.val("AC")

    att_done = phase5()
    P.barrier()

    if DEBUG_OUTPUTS:
        for h in range(HQ):
            P.op("sync", lambda e, h=h: e.dma_start(
                out=dbg_att[h], in_=qa_pool[:, h * 1024:(h + 1) * 1024]), "DBG", 16)
        P.barrier()

    if BUILD_PHASES <= 5:
        P.finalize()
        return nc, es

    def phase6():
        wslot_free = [0] * 4
        bank_ev = [0] * NT
        yl_free = [0, 0]
        out_free = [0, 0]
        P.wait("sync", "OS0", P.val("OS0"))
        P.wait("sync", "OS1", P.val("OS1"))
        last = 0
        for oc in range(8):
            for ic in range(32):
                slot = ic % 4
                wbuf = wpool[:, slot * 512:(slot + 1) * 512]
                P.wait("gpsimd", "PE", wslot_free[slot])
                wv = P.op("gpsimd", lambda e, d=wbuf, oc=oc, ic=ic: e.dma_start(
                    out=d, in_=wo_h[oc, ic]), f"WS{slot}", 16)
                P.wait("tensor", f"WS{slot}", wv)
                for t in range(NT):
                    if ic == 0:
                        P.wait("tensor", "DV", bank_ev[t])
                    lhs = qa_pool[:, ic * 1024 + t * 128: ic * 1024 + (t + 1) * 128]
                    mm = P.op("tensor", lambda e, b=pb[t], l=lhs, w=wbuf, ic=ic: e.matmul(
                        b[:], l, w, start=(ic == 0), stop=(ic == 31)),
                        "PE" if t == NT - 1 else None)
                wslot_free[slot] = mm
            for t in range(NT):
                par = t % 2
                ys = f32_pool[:, par * 512:(par + 1) * 512]
                od = f32_pool[:, 4096 + par * 512: 4096 + (par + 1) * 512]
                P.wait("sync", "DV", yl_free[par])
                yv = P.op("sync", lambda e, d=ys, t=t, oc=oc: e.dma_start(
                    out=d, in_=y1_d[t * 128:(t + 1) * 128, oc * 512:(oc + 1) * 512]), f"XL{par}", 16)
                P.wait("vector", "PE", mm)
                P.wait("vector", f"XL{par}", yv)
                P.wait("vector", f"OS{par}", out_free[par])
                ev = P.op("vector", lambda e, b=pb[t], x=ys, o=od: e.tensor_add(o, b[:], x), "DV")
                yl_free[par] = ev
                bank_ev[t] = ev
                P.wait("sync", "DV", ev)
                sv = P.op("sync", lambda e, o=od, t=t, oc=oc: e.dma_start(
                    out=y_out[t * 128:(t + 1) * 128, oc * 512:(oc + 1) * 512], in_=o), f"OS{par}", 16)
                out_free[par] = sv
                last = sv
        return last

    phase6()
    P.wait("sync", "OS0", P.val("OS0"))
    P.wait("sync", "OS1", P.val("OS1"))
    P.finalize()
    return nc, es


def _host_prep(inputs):
    x = np.asarray(inputs["L0_attn_hidden_states"], np.float32)
    wq = np.asarray(inputs["L0_attn_wq"], np.float32)
    wk = np.asarray(inputs["L0_attn_wk"], np.float32)
    wv = np.asarray(inputs["L0_attn_wv"], np.float32)
    wo = np.asarray(inputs["L0_attn_wo"], np.float32)
    worg = np.asarray(inputs["L0_ffn_W_org"], np.float32)
    ldown = np.asarray(inputs["L0_ffn_lora_down"], np.float32)
    lup = np.asarray(inputs["L0_ffn_lora_up"], np.float32)
    lkeys = np.asarray(inputs["L0_ffn_lora_keys"], np.float32)
    wqm = np.asarray(inputs["L0_ffn_Wq"], np.float32)

    wq_h = np.ascontiguousarray(
        wq.reshape(HQ, 128, 32, 128).transpose(0, 3, 2, 1).reshape(HQ, 128, D))
    wk_h = np.ascontiguousarray(
        wk.reshape(HKV, 128, 32, 128).transpose(0, 3, 2, 1).reshape(HKV, 128, D))
    wv_h = np.ascontiguousarray(wv.reshape(2, 512, 32, 128).transpose(2, 0, 3, 1))
    wo_h = np.ascontiguousarray(wo.reshape(8, 512, 32, 128).transpose(0, 2, 3, 1))
    worg_h = np.ascontiguousarray(worg.reshape(8, 512, 32, 128).transpose(0, 2, 3, 1))
    dmat = ldown.transpose(1, 0, 2).reshape(D, R)
    dmat_h = np.ascontiguousarray(
        dmat.reshape(32, 128, 4, 128).transpose(2, 1, 0, 3).reshape(4, 128, 32 * 128))
    umat = lup.reshape(R, D)
    umat_h = np.ascontiguousarray(umat.reshape(4, 128, 8, 512).transpose(2, 0, 1, 3))
    wqm_h = np.ascontiguousarray(
        wqm.reshape(128, 32, 128).transpose(2, 1, 0).reshape(128, 32 * 128))
    keys_t = np.ascontiguousarray(lkeys.T)

    ident = np.eye(128, dtype=bfdt)
    onec = np.ones((128, 1), dtype=bfdt)
    oner = np.ones((1, 128), dtype=bfdt)
    one1 = np.ones((1, 1), dtype=bfdt)
    sel = np.zeros((NE, 512), dtype=np.float32)
    for rc in range(4):
        for p in range(128):
            sel[2 * rc + p // 64, rc * 128 + p] = 1.0
    sel = sel.astype(bfdt)

    inv_freq = 1.0 / (10000.0 ** (np.arange(0, DH, 2, dtype=np.float32) / DH))

    in_maps = []
    for c in range(8):
        b, j = c // 4, c % 4
        xr = x[b].reshape(NT, 4, 128, D)[:, j]
        x_core = np.ascontiguousarray(xr.reshape(TL, D))
        pos = (np.arange(NT)[:, None] * 512 + j * 128 +
               np.arange(128)[None, :]).reshape(TL).astype(np.float32)
        ang = pos[None, :] * inv_freq[:, None]
        ropec_a = np.empty((128, TL), np.float32)
        ropes_a = np.empty((128, TL), np.float32)
        ropec_a[0:64] = np.cos(ang); ropec_a[64:128] = np.cos(ang)
        ropes_a[0:64] = -np.sin(ang); ropes_a[64:128] = np.sin(ang)
        qi = np.arange(128)[:, None]
        ji = np.arange(512)[None, :]
        mask_a = np.where(j * 128 + qi >= ji, 0.0, NEG).astype(bfdt)
        in_maps.append({
            "x_in": x_core, "wq_h": wq_h, "wk_h": wk_h, "wv_h": wv_h,
            "wo_h": wo_h, "worg_h": worg_h, "dmat_h": dmat_h, "umat_h": umat_h,
            "wqm_h": wqm_h, "keys_t": keys_t, "rope_c": ropec_a, "rope_s": ropes_a,
            "mask_in": mask_a, "ident_in": ident, "onec_in": onec, "oner_in": oner,
            "one1_in": one1, "sel_in": sel,
        })
    return in_maps


def _assemble(results):
    y = np.empty((B, S, D), np.float32)
    for c in range(8):
        b, j = c // 4, c % 4
        y[b].reshape(NT, 4, 128, D)[:, j] = results[c]["y_out"].reshape(NT, 128, D)
    return y


def kernel(**inputs):
    nc, es = build_program()
    in_maps = _host_prep(inputs)
    try:
        res = run_bass_kernel_spmd(nc, in_maps, core_ids=list(range(8)))
    finally:
        es.close()
    return _assemble(res.results)


# version 05
# speedup vs baseline: 1.0501x; 1.0501x over previous
import math
import numpy as np
import ml_dtypes
import concourse.bass as bass
import concourse.mybir as mybir
from concourse.bass_utils import run_bass_kernel_spmd

F32 = mybir.dt.float32
BF16 = mybir.dt.bfloat16
AF = mybir.ActivationFunctionType
ALU = mybir.AluOpType
AX = mybir.AxisListType
bfdt = ml_dtypes.bfloat16

B, S, D = 2, 4096, 4096
HQ, HKV, DH = 32, 8, 128
GRP = HQ // HKV
NT = 8
TL = 1024
R = 512
NE, KD = 8, 128
EPS = 1e-5
ISQ = 1.0 / math.sqrt(DH)
NEG = -1e9

DEBUG_OUTPUTS = False
BUILD_PHASES = 99
PHASE2_CUT = "full"


class Prog:
    ENGS = ("sync", "gpsimd", "tensor", "vector", "scalar")
    ENG_SEMS = ("PE", "DV", "AC", "PO")
    DMA_SEMS = ("CONST", "XL0", "XL1", "XL2", "XL3", "KV0", "KV1", "MN", "MS",
                "KS0", "KS1", "VS0", "VS1", "OS0", "OS1", "OS2", "OS3",
                "WP0", "WP1", "WS0", "WS1", "WS2", "WS3", "WM", "DBG")
    CC_SEMS = ("CCM", "CCK")
    ALL_SEMS = ENG_SEMS + DMA_SEMS + CC_SEMS

    def __init__(self, nc, es):
        self.nc = nc
        self.q = {e: [] for e in self.ENGS}
        self.sem = {}
        self.cnt = {}
        self._es = es

    def mksem(self, name):
        self.sem[name] = self._es.enter_context(self.nc.semaphore(name))
        self.cnt[name] = 0

    def wait(self, eng, sem, val):
        if val <= 0:
            return
        h = self.sem[sem]
        self.q[eng].append(lambda e, h=h, v=val: e.wait_ge(h, v))

    def op(self, eng, fn, inc=None, n=1):
        if inc is None:
            self.q[eng].append(fn)
            return None
        self.cnt[inc] += n
        v = self.cnt[inc]
        h = self.sem[inc]
        self.q[eng].append(lambda e, fn=fn, h=h, n=n: fn(e).then_inc(h, n))
        return v

    def val(self, sem):
        return self.cnt[sem]

    def barrier(self):
        marks = [(s, self.val(s)) for s in self.ENG_SEMS + self.DMA_SEMS]
        for eng in self.ENGS:
            for sem, v in marks:
                self.wait(eng, sem, v)

    def finalize(self):
        with self.nc.Block() as block:
            for eng in self.ENGS:
                ops = self.q[eng]

                def run(e, ops=ops):
                    for f in ops:
                        f(e)

                getattr(block, eng)(run)


def build_program():
    from contextlib import ExitStack

    nc = bass.Bass("TRN2", target_bir_lowering=False, debug=False)
    es = ExitStack()

    def dram(name, shape, dtype, kind):
        return nc.dram_tensor(name, shape, dtype, kind=kind).ap()

    x_in = dram("x_in", [TL, D], F32, "ExternalInput")
    wq_h = dram("wq_h", [HQ, 128, D], F32, "ExternalInput")
    wk_h = dram("wk_h", [HKV, 128, D], F32, "ExternalInput")
    wv_h = dram("wv_h", [2, 8, 128, 2048], F32, "ExternalInput")
    wo_h = dram("wo_h", [8, 8, 128, 2048], F32, "ExternalInput")
    worg_h = dram("worg_h", [8, 8, 128, 2048], F32, "ExternalInput")
    dmat_h = dram("dmat_h", [4, 128, 32 * 128], F32, "ExternalInput")
    umat_h = dram("umat_h", [8, 128, 2048], F32, "ExternalInput")
    wqm_h = dram("wqm_h", [128, 32 * 128], F32, "ExternalInput")
    keys_t = dram("keys_t", [KD, NE], F32, "ExternalInput")
    rope_c = dram("rope_c", [128, TL], F32, "ExternalInput")
    rope_s = dram("rope_s", [128, TL], F32, "ExternalInput")
    mask_in = dram("mask_in", [128, 512], BF16, "ExternalInput")
    ident_in = dram("ident_in", [128, 128], BF16, "ExternalInput")
    onec_in = dram("onec_in", [128, 1], BF16, "ExternalInput")
    oner_in = dram("oner_in", [1, 128], BF16, "ExternalInput")
    one1_in = dram("one1_in", [1, 1], BF16, "ExternalInput")
    sel_in = dram("sel_in", [NE, 512], BF16, "ExternalInput")

    y_out = dram("y_out", [TL, D], F32, "ExternalOutput")

    kv_ci = dram("kv_ci", [2048, 1024], BF16, "Internal")
    kv_co = dram("kv_co", [4 * 2048, 1024], BF16, "Internal")
    mn_ci = dram("mn_ci", [128, 32], F32, "Internal")
    mn_co = dram("mn_co", [128, 32], F32, "Internal")
    y1_d = dram("y1_d", [TL, D], F32, "Internal")
    RG = [[0, 1, 2, 3], [4, 5, 6, 7]]

    if DEBUG_OUTPUTS:
        dbg_xn = dram("dbg_xn", [32, 128, TL], BF16, "ExternalOutput")
        dbg_coef = dram("dbg_coef", [128, 4], F32, "ExternalOutput")
        dbg_q = dram("dbg_q", [HQ, 128, TL], BF16, "ExternalOutput")
        dbg_kv = dram("dbg_kv", [4 * 2048, 1024], BF16, "ExternalOutput")
        dbg_att = dram("dbg_att", [HQ, 128, TL], BF16, "ExternalOutput")
        dbg_y1 = dram("dbg_y1", [TL, D], F32, "ExternalOutput")

    sb = lambda name, shape, dt: es.enter_context(nc.sbuf_tensor(name, shape, dt))
    xn_pool = sb("xn_pool", [128, 32768], BF16)
    qa_pool = sb("qa_pool", [128, 32768], BF16)
    f32_pool = sb("f32_pool", [128, 8192], F32)
    wpool = sb("wpool", [128, 8192], BF16)
    ptsb = sb("ptsb", [128, 1024], BF16)
    stage = sb("stage", [128, 1536], BF16)
    ident = sb("ident", [128, 128], BF16)
    onec = sb("onec", [128, 1], BF16)
    oner = sb("oner", [1, 128], BF16)
    one1 = sb("one1", [1, 1], BF16)
    mask = sb("mask", [128, 512], BF16)
    ropec = sb("ropec", [128, TL], F32)
    ropes = sb("ropes", [128, TL], F32)
    selm = sb("selm", [NE, 512], BF16)
    keyst = sb("keyst", [KD, NE], BF16)
    ra = f32_pool[:, 0:512]
    rb = f32_pool[:, 512:1024]
    rt1 = f32_pool[:, 1024:1536]
    rt2 = f32_pool[:, 1536:2048]
    colsum = sb("colsum", [128, 32], F32)
    meansb = sb("meansb", [128, 32], F32)
    meanbf = sb("meanbf", [128, 32], BF16)
    stat = sb("stat", [128, 16], F32)
    sums4 = sb("sums4", [128, 32], F32)
    tot4 = sb("tot4", [128, 4], F32)
    rcp4 = sb("rcp4", [128, 4], F32)
    bias4 = sb("bias4", [128, 4], F32)
    scol = sb("scol", [128, 4], F32)
    trowb = sb("trowb", [1, 128], BF16)
    kmrow = sb("kmrow", [1, 32], F32)

    pb = [es.enter_context(nc.psum_tensor(f"pb{i}", [128, 512], F32)) for i in range(8)]

    P = Prog(nc, es)
    for s in Prog.ALL_SEMS:
        P.mksem(s)

    for dst, src in ((ident, ident_in), (onec, onec_in), (oner, oner_in),
                     (one1, one1_in), (mask, mask_in), (ropec, rope_c),
                     (ropes, rope_s), (selm, sel_in)):
        P.op("sync", lambda e, d=dst, s=src: e.dma_start(out=d[:], in_=s[:]), "CONST", 16)
    P.op("gpsimd", lambda e: e.dma_start(out=keyst[:], in_=keys_t[:]), "WM", 16)
    P.barrier()

    def phase1():
        tp_free = [0, 0]
        norm_done = [0] * NT
        stage_free = [0, 0]
        xld = [0] * NT
        for qt in range(NT):
            par = qt % 2
            xbuf = f32_pool[:, par * 4096:(par + 1) * 4096]
            st = lambda i, par=par: stat[:, par * 8 + i: par * 8 + i + 1]
            if qt >= 2:
                P.wait("sync", "AC", norm_done[qt - 2])
            xld[qt] = P.op("sync", lambda e, d=xbuf, q=qt: e.dma_start(
                out=d, in_=x_in[q * 128:(q + 1) * 128, :]), f"XL{par}", 16)
            P.wait("vector", f"XL{par}", xld[qt])
            sink = qa_pool[:, 8192:12288]
            P.wait("scalar", f"XL{par}", xld[qt])
            a0 = P.op("scalar", lambda e, x=xbuf, st=st, s=sink: e.activation(
                out=s, in_=x, func=AF.Square, accum_out=st(2)), "AC")
            v = P.op("vector", lambda e, x=xbuf, st=st: e.reduce_sum(
                st(0), x, axis=AX.X, negate=True), "DV")
            P.wait("vector", "DV", v)
            v = P.op("vector", lambda e, st=st: e.tensor_scalar_mul(st(1), st(0), 1.0 / D), "DV")
            P.wait("vector", "DV", v)
            v = P.op("vector", lambda e, st=st: e.tensor_mul(st(3), st(1), st(1)), "DV")
            P.wait("vector", "AC", a0)
            P.wait("vector", "DV", v)
            v = P.op("vector", lambda e, st=st: e.tensor_scalar(
                st(4), st(2), 1.0 / D, EPS, ALU.mult, ALU.add), "DV")
            P.wait("vector", "DV", v)
            v = P.op("vector", lambda e, st=st: e.tensor_sub(st(5), st(4), st(3)), "DV")
            P.wait("scalar", "DV", v)
            a = P.op("scalar", lambda e, st=st: e.activation(out=st(6), in_=st(5), func=AF.Sqrt), "AC")
            P.wait("vector", "AC", a)
            v = P.op("vector", lambda e, st=st: e.reciprocal(out=st(7), in_=st(6)), "DV")
            P.wait("vector", "DV", v)
            v = P.op("vector", lambda e, st=st: e.tensor_mul(st(3), st(1), st(7)), "DV")
            xst = qa_pool[:, par * 4096:(par + 1) * 4096]
            P.wait("scalar", "DV", v)
            P.wait("scalar", "PE", stage_free[par])
            a = P.op("scalar", lambda e, x=xbuf, o=xst, st=st: e.activation(
                out=o, in_=x, func=AF.Identity, scale=st(7), bias=st(3)), "AC")
            norm_done[qt] = a
            for g in range(8):
                bank = pb[g % 2]
                P.wait("tensor", "AC", a)
                P.wait("tensor", "DV", tp_free[g % 2])
                for u in range(4):
                    dc = g * 4 + u
                    pe_v = P.op("tensor", lambda e, b=bank, u=u, x=xst, dc=dc: e.matmul(
                        b[:, u * 128:(u + 1) * 128], x[:, dc * 128:(dc + 1) * 128],
                        ident[:], start=True, stop=True), "PE" if u == 3 else None)
                P.wait("vector", "PE", pe_v)
                for u in range(4):
                    dc = g * 4 + u
                    dst = xn_pool[:, dc * 1024 + qt * 128: dc * 1024 + (qt + 1) * 128]
                    ev = P.op("vector", lambda e, b=bank, u=u, d=dst: e.tensor_copy(
                        out=d, in_=b[:, u * 128:(u + 1) * 128]), "DV" if u == 3 else None)
                tp_free[g % 2] = ev
            stage_free[par] = pe_v
        P.wait("vector", "DV", max(tp_free[0], tp_free[1]))
        for dc in range(32):
            cv = P.op("vector", lambda e, dc=dc: e.reduce_sum(
                colsum[:, dc:dc + 1], xn_pool[:, dc * 1024:(dc + 1) * 1024], axis=AX.X),
                "DV" if dc == 31 else None)
        P.wait("sync", "DV", cv)
        stv = P.op("sync", lambda e: e.dma_start(out=mn_ci[:], in_=colsum[:]), "MS", 16)
        P.wait("gpsimd", "MS", stv)
        ar = P.op("gpsimd", lambda e: e.collective_compute(
            "AllReduce", ALU.add, RG, [mn_ci[:]], [mn_co[:]]), "CCM", 1)
        return ar

    mean_ar = phase1()
    P.barrier()

    if DEBUG_OUTPUTS:
        for dc in range(32):
            P.op("sync", lambda e, dc=dc: e.dma_start(
                out=dbg_xn[dc], in_=xn_pool[:, dc * 1024:(dc + 1) * 1024]), "DBG", 16)
        P.barrier()

    if BUILD_PHASES <= 1:
        P.finalize()
        return nc, es

    rope_state = {"p1": 0, "v4": 0, "stage_st": {}}

    def rope_evac(bank, th, dst_bf, pe_v, extra_dst_wait=None):
        C = ropec[:, th * 512:(th + 1) * 512]
        Ssg = ropes[:, th * 512:(th + 1) * 512]
        P.wait("vector", "PE", pe_v)
        P.wait("vector", "PO", rope_state["p1"])
        v1 = P.op("vector", lambda e, b=bank: e.tensor_copy(out=rb[64:128, :], in_=b[0:64, :]), "DV")
        v2 = P.op("vector", lambda e, b=bank: e.tensor_copy(out=rb[0:64, :], in_=b[64:128, :]), "DV")
        P.wait("scalar", "PE", pe_v)
        P.wait("scalar", "DV", max(v2, rope_state["v4"]))
        a1 = P.op("scalar", lambda e, b=bank: e.copy(out=ra[:], in_=b[:]), "AC")
        P.wait("vector", "AC", a1)
        v3 = P.op("vector", lambda e, C=C: e.tensor_mul(rt1[:], ra[:], C), "DV")
        P.wait("gpsimd", "DV", max(v2, rope_state["v4"]))
        p1 = P.op("gpsimd", lambda e, Ssg=Ssg: e.tensor_tensor(
            out=rt2[:], in0=rb[:], in1=Ssg, op=ALU.mult), "PO")
        rope_state["p1"] = p1
        P.wait("vector", "DV", v3)
        P.wait("vector", "PO", p1)
        if extra_dst_wait is not None:
            sem, val = extra_dst_wait
            P.wait("vector", sem, val)
        v4 = P.op("vector", lambda e, d=dst_bf: e.tensor_add(d, rt1[:], rt2[:]), "DV")
        rope_state["v4"] = v4
        return v2, v4

    def phase2():
        wfree = [0, 0]
        bank_free = [0] * 4
        kst_by_th = {0: 0, 1: 0}
        for kvh in range(HKV):
            par = kvh % 2
            wbuf = wpool[:, par * 4096:(par + 1) * 4096]
            P.wait("gpsimd", "PE", wfree[par])
            wld = P.op("gpsimd", lambda e, d=wbuf, k=kvh: e.dma_start(
                out=d, in_=wk_h[k]), f"WP{par}", 16)
            P.wait("tensor", f"WP{par}", wld)
            for th in range(2):
                bslot = (kvh * 2 + th) % 4
                bank = pb[4 + bslot]
                P.wait("tensor", "DV", bank_free[bslot])
                for ic in range(32):
                    mm = P.op("tensor", lambda e, b=bank, w=wbuf, ic=ic, th=th: e.matmul(
                        b[:], w[:, ic * 128:(ic + 1) * 128],
                        xn_pool[:, ic * 1024 + th * 512: ic * 1024 + (th + 1) * 512],
                        start=(ic == 0), stop=(ic == 31)), "PE" if ic == 31 else None)
                if th == 1:
                    wfree[par] = mm
                dst = stage[:, th * 512:(th + 1) * 512]
                if PHASE2_CUT == "kmm":
                    P.wait("vector", "PE", mm)
                    done = P.op("vector", lambda e, b=bank, d=dst: e.tensor_copy(out=d, in_=b[:]), "DV")
                    bank_free[bslot] = done
                    continue
                cons, done = rope_evac(bank, th, dst, mm,
                                       extra_dst_wait=(f"KS{th}", kst_by_th[th]))
                bank_free[bslot] = done
                if PHASE2_CUT == "krope":
                    continue
                P.wait("sync", "DV", done)
                sv = P.op("sync", lambda e, k=kvh, th=th, d=dst: e.dma_start(
                    out=kv_ci[k * 128:(k + 1) * 128, th * 512:(th + 1) * 512], in_=d), f"KS{th}", 16)
                kst_by_th[th] = sv
        if PHASE2_CUT == "k":
            return 0
        P.wait("tensor", "DV", max(bank_free))
        P.wait("scalar", "KS0", P.val("KS0"))
        P.wait("scalar", "KS1", P.val("KS1"))
        vslot_free = [P.val("PE")] * 2
        vbank_free = [0] * NT
        vst_slot = [0, 0]
        for dvh in range(2):
            for g in range(8):
                slot = g % 2
                gbuf = wpool[:, slot * 2048:(slot + 1) * 2048]
                P.wait("gpsimd", "PE", vslot_free[slot])
                wv = P.op("gpsimd", lambda e, d=gbuf, dvh=dvh, g=g: e.dma_start(
                    out=d, in_=wv_h[dvh, g]), f"WS{slot}", 16)
                P.wait("tensor", f"WS{slot}", wv)
                for gi in range(4):
                    ic = g * 4 + gi
                    for t in range(NT):
                        if ic == 0:
                            P.wait("tensor", "AC", vbank_free[t])
                        mm = P.op("tensor", lambda e, b=pb[t], ic=ic, t=t, w=gbuf, gi=gi: e.matmul(
                            b[:], xn_pool[:, ic * 1024 + t * 128: ic * 1024 + (t + 1) * 128],
                            w[:, gi * 512:(gi + 1) * 512],
                            start=(ic == 0), stop=(ic == 31)), "PE" if t == NT - 1 else None)
                vslot_free[slot] = mm
            for t in range(NT):
                sl = t % 2
                vdst = stage[:, sl * 512:(sl + 1) * 512]
                P.wait("scalar", "PE", mm)
                P.wait("scalar", f"VS{sl}", vst_slot[sl])
                av = P.op("scalar", lambda e, b=pb[t], d=vdst: e.copy(out=d, in_=b[:]), "AC")
                vbank_free[t] = av
                P.wait("sync", "AC", av)
                vst_slot[sl] = P.op("sync", lambda e, t=t, dvh=dvh, d=vdst: e.dma_start(
                    out=kv_ci[1024 + t * 128:1024 + (t + 1) * 128, dvh * 512:(dvh + 1) * 512],
                    in_=d), f"VS{sl}", 16)
        if PHASE2_CUT == "kv":
            return 0
        P.wait("gpsimd", "KS0", P.val("KS0"))
        P.wait("gpsimd", "KS1", P.val("KS1"))
        P.wait("gpsimd", "VS0", P.val("VS0"))
        P.wait("gpsimd", "VS1", P.val("VS1"))
        ag = P.op("gpsimd", lambda e: e.collective_compute(
            "AllGather", ALU.bypass, RG, [kv_ci[:]], [kv_co[:]]), "CCK", 1)
        return ag

    kv_ag = phase2()
    P.barrier()

    if BUILD_PHASES <= 2:
        P.finalize()
        return nc, es

    def phase3():
        wfree = [0, 0]
        bank_free = [0] * 4
        for h in range(HQ):
            par = h % 2
            wbuf = wpool[:, par * 4096:(par + 1) * 4096]
            P.wait("gpsimd", "PE", wfree[par])
            wld = P.op("gpsimd", lambda e, d=wbuf, h=h: e.dma_start(
                out=d, in_=wq_h[h]), f"WP{par}", 16)
            P.wait("tensor", f"WP{par}", wld)
            for th in range(2):
                bslot = (h * 2 + th) % 4
                bank = pb[4 + bslot]
                P.wait("tensor", "DV", bank_free[bslot])
                for ic in range(32):
                    mm = P.op("tensor", lambda e, b=bank, w=wbuf, ic=ic, th=th: e.matmul(
                        b[:], w[:, ic * 128:(ic + 1) * 128],
                        xn_pool[:, ic * 1024 + th * 512: ic * 1024 + (th + 1) * 512],
                        start=(ic == 0), stop=(ic == 31)), "PE" if ic == 31 else None)
                if th == 1:
                    wfree[par] = mm
                dst = qa_pool[:, h * 1024 + th * 512: h * 1024 + (th + 1) * 512]
                cons, done = rope_evac(bank, th, dst, mm)
                bank_free[bslot] = done
        return done

    q_done = phase3()
    P.barrier()

    if DEBUG_OUTPUTS:
        for h in range(HQ):
            P.op("sync", lambda e, h=h: e.dma_start(
                out=dbg_q[h], in_=qa_pool[:, h * 1024:(h + 1) * 1024]), "DBG", 16)
        P.wait("sync", "CCK", kv_ag)
        P.op("sync", lambda e: e.dma_start(out=dbg_kv[:], in_=kv_co[:]), "DBG", 16)
        P.barrier()

    if BUILD_PHASES <= 3:
        P.finalize()
        return nc, es

    def phase_coef():
        P.wait("sync", "CCM", mean_ar)
        mld = P.op("sync", lambda e: e.dma_start(out=meansb[:], in_=mn_co[:]), "MN", 16)
        P.wait("vector", "MN", mld)
        v = P.op("vector", lambda e: e.tensor_scalar_mul(meansb[:], meansb[:], 1.0 / S), "DV")
        P.wait("vector", "DV", v)
        v = P.op("vector", lambda e: e.tensor_copy(out=meanbf[:], in_=meansb[:]), "DV")
        wld = P.op("gpsimd", lambda e: e.dma_start(out=wpool[:, 0:4096], in_=wqm_h[:]), "WM", 16)
        P.wait("tensor", "DV", v)
        P.wait("tensor", "WM", wld)
        for ic in range(32):
            mm = P.op("tensor", lambda e, ic=ic: e.matmul(
                pb[7][0:1, 0:128], meanbf[:, ic:ic + 1],
                wpool[:, ic * 128:(ic + 1) * 128], start=(ic == 0), stop=(ic == 31)),
                "PE" if ic == 31 else None)
        P.wait("scalar", "PE", mm)
        a = P.op("scalar", lambda e: e.copy(out=trowb[:], in_=pb[7][0:1, 0:128]), "AC")
        P.wait("tensor", "AC", a)
        mm = P.op("tensor", lambda e: e.matmul(pb[6][:, 0:1], trowb[:], one1[:],
                                               start=True, stop=True), "PE")
        P.wait("vector", "PE", mm)
        v = P.op("vector", lambda e: e.tensor_copy(out=meanbf[:, 0:1], in_=pb[6][:, 0:1]), "DV")
        P.wait("tensor", "DV", v)
        mm = P.op("tensor", lambda e: e.matmul(pb[6][0:1, 16:24], meanbf[:, 0:1], keyst[:],
                                               start=True, stop=True), "PE")
        P.wait("vector", "PE", mm)
        v = P.op("vector", lambda e: e.tensor_scalar_mul(
            kmrow[0:1, 0:8], pb[6][0:1, 16:24], 1.0 / math.sqrt(KD)), "DV")
        P.wait("vector", "DV", v)
        v = P.op("vector", lambda e: e.reduce_max(kmrow[0:1, 8:9], kmrow[0:1, 0:8],
                                                  axis=AX.X, negate=True), "DV")
        P.wait("scalar", "DV", v)
        a = P.op("scalar", lambda e: e.activation(out=kmrow[0:1, 10:18], in_=kmrow[0:1, 0:8],
                                                  func=AF.Exp, bias=kmrow[0:1, 8:9], scale=1.0,
                                                  accum_out=kmrow[0:1, 9:10]), "AC")
        P.wait("vector", "AC", a)
        v = P.op("vector", lambda e: e.reciprocal(out=kmrow[0:1, 8:9], in_=kmrow[0:1, 9:10]), "DV")
        P.wait("vector", "DV", v)
        v = P.op("vector", lambda e: e.tensor_scalar_mul(
            kmrow[0:1, 10:18], kmrow[0:1, 10:18], kmrow[0:1, 8:9]), "DV")
        P.wait("vector", "DV", v)
        v = P.op("vector", lambda e: e.tensor_copy(out=trowb[0:1, 0:8], in_=kmrow[0:1, 10:18]), "DV")
        P.wait("tensor", "DV", v)
        mm = P.op("tensor", lambda e: e.matmul(pb[6][0:8, 8:9], trowb[0:1, 0:8], one1[:],
                                               start=True, stop=True), "PE")
        P.wait("vector", "PE", mm)
        v = P.op("vector", lambda e: e.tensor_copy(out=meanbf[0:8, 1:2], in_=pb[6][0:8, 8:9]), "DV")
        P.wait("tensor", "DV", v)
        for rc in range(4):
            mm = P.op("tensor", lambda e, rc=rc: e.matmul(
                pb[6][:, 2 + rc:3 + rc], selm[:, rc * 128:(rc + 1) * 128], meanbf[0:8, 1:2],
                start=True, stop=True), "PE" if rc == 3 else None)
        P.wait("vector", "PE", mm)
        v = P.op("vector", lambda e: e.tensor_copy(out=scol[:], in_=pb[6][:, 2:6]), "DV")
        return v

    coef_done = phase_coef()
    P.barrier()

    if DEBUG_OUTPUTS:
        P.op("sync", lambda e: e.dma_start(out=dbg_coef[:], in_=scol[:]), "DBG", 16)
        P.barrier()

    def emit_outpass(ngroups, stop_ic, wsrc, lhs_fn, add_src, store_dst,
                     init_bank_ev=None, init_slot_free=0, extra_mm_wait=None):
        wslot_free = [init_slot_free, init_slot_free]
        bank_ev = [0] * NT
        if init_bank_ev:
            for kk, vv in init_bank_ev.items():
                bank_ev[kk] = vv
        xl_free = [0] * 4
        out_free = [0] * 4
        xv = [0] * NT
        last_sv = 0
        def stage_load(oc, t):
            sl = t % 4
            xs = f32_pool[:, sl * 512:(sl + 1) * 512]
            P.wait("sync", "DV", xl_free[sl])
            xv[t] = P.op("sync", lambda e, d=xs, t=t, oc=oc, a=add_src: e.dma_start(
                out=d, in_=a[t * 128:(t + 1) * 128, oc * 512:(oc + 1) * 512]), f"XL{sl}", 16)

        for oc in range(8):
            for t in range(4):
                stage_load(oc, t)
            for g in range(ngroups):
                slot = g % 2
                gbuf = wpool[:, slot * 2048:(slot + 1) * 2048]
                P.wait("gpsimd", "PE", wslot_free[slot])
                wv = P.op("gpsimd", lambda e, d=gbuf, oc=oc, g=g: e.dma_start(
                    out=d, in_=wsrc(oc, g)), f"WS{slot}", 16)
                P.wait("tensor", f"WS{slot}", wv)
                for gi in range(4):
                    ic = g * 4 + gi
                    if ic > stop_ic:
                        break
                    if extra_mm_wait is not None and ic == extra_mm_wait[2] * 4:
                        if oc == 0:
                            P.wait("tensor", extra_mm_wait[0], extra_mm_wait[1])
                    for t in range(NT):
                        if ic == 0:
                            P.wait("tensor", "DV", bank_ev[t])
                        mm = P.op("tensor",
                                  lambda e, b=pb[t], l=lhs_fn(ic, t), w=gbuf, gi=gi, ic=ic: e.matmul(
                                      b[:], l, w[:, gi * 512:(gi + 1) * 512],
                                      start=(ic == 0), stop=(ic == stop_ic)),
                                  "PE" if t == NT - 1 else None)
                wslot_free[slot] = mm
            for t in range(NT):
                sl = t % 4
                xs = f32_pool[:, sl * 512:(sl + 1) * 512]
                od = f32_pool[:, 4096 + sl * 512: 4096 + (sl + 1) * 512]
                P.wait("vector", "PE", mm)
                P.wait("vector", f"XL{sl}", xv[t])
                P.wait("vector", f"OS{sl}", out_free[sl])
                ev = P.op("vector", lambda e, b=pb[t], x=xs, o=od: e.tensor_add(o, b[:], x), "DV")
                xl_free[sl] = ev
                bank_ev[t] = ev
                P.wait("sync", "DV", ev)
                sv = P.op("sync", lambda e, o=od, t=t, oc=oc, s=store_dst: e.dma_start(
                    out=s[t * 128:(t + 1) * 128, oc * 512:(oc + 1) * 512], in_=o), f"OS{sl}", 16)
                out_free[sl] = sv
                last_sv = sv
                if t + 4 < NT:
                    stage_load(oc, t + 4)
        return last_sv

    def phase4():
        h_sb = wpool[:, 4096:8192]
        hfree = [0, 0]
        last_rc_mm = 0
        for rc in range(4):
            P.wait("gpsimd", "PE", last_rc_mm)
            wld = P.op("gpsimd", lambda e, rc=rc: e.dma_start(
                out=wpool[:, 0:4096], in_=dmat_h[rc]), "WP0", 16)
            P.wait("tensor", "WP0", wld)
            for th in range(2):
                bank = pb[6 + th]
                P.wait("tensor", "DV", hfree[th])
                for ic in range(32):
                    mm = P.op("tensor", lambda e, b=bank, ic=ic, th=th: e.matmul(
                        b[:], wpool[:, ic * 128:(ic + 1) * 128],
                        xn_pool[:, ic * 1024 + th * 512: ic * 1024 + (th + 1) * 512],
                        start=(ic == 0), stop=(ic == 31)), "PE" if ic == 31 else None)
                last_rc_mm = mm
                P.wait("vector", "PE", mm)
                hv = P.op("vector", lambda e, b=bank, rc=rc, th=th: e.tensor_scalar_mul(
                    h_sb[:, rc * 1024 + th * 512: rc * 1024 + (th + 1) * 512],
                    b[:], scol[:, rc:rc + 1]), "DV")
                hfree[th] = hv
        h_done = hv
        return emit_outpass(
            ngroups=9, stop_ic=35,
            wsrc=lambda oc, g: worg_h[oc, g] if g < 8 else umat_h[oc],
            lhs_fn=lambda ic, t: (xn_pool[:, ic * 1024 + t * 128: ic * 1024 + (t + 1) * 128]
                                  if ic < 32 else
                                  h_sb[:, (ic - 32) * 1024 + t * 128: (ic - 32) * 1024 + (t + 1) * 128]),
            add_src=x_in, store_dst=y1_d,
            init_bank_ev={6: hfree[0], 7: hfree[1]},
            init_slot_free=last_rc_mm,
            extra_mm_wait=("DV", h_done, 8))

    y1_last = phase4()
    P.barrier()

    if DEBUG_OUTPUTS:
        P.op("sync", lambda e: e.dma_start(out=dbg_y1[:], in_=y1_d[:]), "DBG", 16)
        P.barrier()

    if BUILD_PHASES <= 4:
        P.finalize()
        return nc, es

    def phase5():
        ksink = xn_pool[:, 16384:16384 + 4096]
        P.wait("sync", "CCK", kv_ag)
        kv_consumed = [0] * HKV

        def load_kv(kvh, wait_pe_val):
            par = kvh % 2
            P.wait("sync", "PE", wait_pe_val)
            last = 0
            for jj in range(4):
                kin = kv_co[jj * 2048 + kvh * 128: jj * 2048 + (kvh + 1) * 128, :]
                kout = xn_pool[:, par * 4096:(par + 1) * 4096].rearrange(
                    "p (kb j i) -> p kb j i", kb=8, j=4)[:, :, jj, :]
                last = P.op("sync", lambda e, o=kout,
                            i=kin.rearrange("p (kb i) -> p kb i", kb=8):
                            e.dma_start(out=o, in_=i), f"KV{par}", 16)
                vin = kv_co[jj * 2048 + 1024: jj * 2048 + 2048, kvh * 128:(kvh + 1) * 128]
                vout = xn_pool[:, 8192 + par * 4096: 8192 + (par + 1) * 4096].rearrange(
                    "p (kb j dv) -> p kb j dv", kb=8, j=4)[:, :, jj, :]
                last = P.op("sync", lambda e, o=vout,
                            i=vin.rearrange("(kb p) dv -> p kb dv", p=128):
                            e.dma_start(out=o, in_=i), f"KV{par}", 16)
            return last

        kld = [0] * HKV
        kld[0] = load_kv(0, 0)

        SCB = [0, 1, 6]
        st5 = {
            "sc_rr": 0,
            "sc_exp": [0, 0, 0],
            "sums_read": [0, 0, 0, 0],
            "last_exp": 0,
            "pt_evac": [0, 0],
            "ptsb_free": [0, 0],
            "pt_rr": 0,
            "av_evac": [0, 0],
            "tr_last": 0,
            "pb7_free": 0,
            "region_free": {},
            "pbuf_free": [0, 0],
            "norm_done": {},
        }

        def praw(upar, hpp):
            off = 16384 + upar * 8192 + hpp * 4096
            return xn_pool[:, off:off + 4096]

        def emit_unit(kvh, qt, pair, upar, K_sb, pending):
            heads = (kvh * GRP + pair * 2, kvh * GRP + pair * 2 + 1)
            nsc = 2 * (qt + 1)
            ci = 0
            emitted = 0
            rcpv = [0, 0]
            exp_last = [0, 0]
            for kb in range(qt + 1):
                for hpp in range(2):
                    h = heads[hpp]
                    slot = st5["sc_rr"]
                    st5["sc_rr"] = (slot + 1) % 3
                    bank = pb[SCB[slot]]
                    P.wait("tensor", "AC", st5["sc_exp"][slot])
                    rf = st5["region_free"].pop(SCB[slot], 0)
                    P.wait("tensor", "DV", rf)
                    if kb == qt:
                        P.op("tensor", lambda e, b=bank, h=h, qt=qt, kb=kb, K=K_sb: e.matmul(
                            b[:], qa_pool[:, h * 1024 + qt * 128: h * 1024 + (qt + 1) * 128],
                            K[:, kb * 512:(kb + 1) * 512], start=True, stop=False))
                        mm = P.op("tensor", lambda e, b=bank: e.matmul(
                            b[:], ident[:], mask[:], start=False, stop=True), "PE")
                    else:
                        mm = P.op("tensor", lambda e, b=bank, h=h, qt=qt, kb=kb, K=K_sb: e.matmul(
                            b[:], qa_pool[:, h * 1024 + qt * 128: h * 1024 + (qt + 1) * 128],
                            K[:, kb * 512:(kb + 1) * 512], start=True, stop=True), "PE")
                    hp = pair * 2 + hpp
                    Pb = praw(upar, hpp)
                    P.wait("scalar", "PE", mm)
                    if kb == 0:
                        P.wait("scalar", "DV", st5["bias_done"])
                        P.wait("scalar", "DV", st5["sums_read"][hp])
                        P.wait("scalar", "PE", st5["pbuf_free"][upar])
                    a = P.op("scalar", lambda e, b=bank, P_=Pb, kb=kb, hp=hp: e.activation(
                        out=P_[:, kb * 512:(kb + 1) * 512], in_=b[:], func=AF.Exp,
                        scale=ISQ, bias=bias4[:, hp:hp + 1],
                        accum_out=sums4[:, hp * 8 + kb: hp * 8 + kb + 1]), "AC")
                    st5["sc_exp"][slot] = a
                    st5["last_exp"] = a
                    exp_last[hpp] = a
                    emitted += 1
                    want = (len(pending) * emitted) // nsc
                    while ci < want:
                        pending[ci]()
                        ci += 1
            while ci < len(pending):
                pending[ci]()
                ci += 1
            nrm = 0
            for hpp in range(2):
                hp = pair * 2 + hpp
                Pb = praw(upar, hpp)
                P.wait("vector", "AC", exp_last[hpp])
                v = P.op("vector", lambda e, hp=hp, qt=qt: e.reduce_sum(
                    tot4[:, hp:hp + 1], sums4[:, hp * 8: hp * 8 + qt + 1], axis=AX.X), "DV")
                st5["sums_read"][hp] = v
                P.wait("vector", "DV", v)
                v = P.op("vector", lambda e, hp=hp: e.reciprocal(
                    out=rcp4[:, hp:hp + 1], in_=tot4[:, hp:hp + 1]), "DV")
                P.wait("vector", "DV", v)
                nrm = P.op("vector", lambda e, P_=Pb, qt=qt, hp=hp: e.tensor_scalar_mul(
                    P_[:, 0:(qt + 1) * 512], P_[:, 0:(qt + 1) * 512], rcp4[:, hp:hp + 1]), "DV")
            st5["norm_done"][(qt, pair)] = nrm

        def trav_chunks(kvh, qt, pair, upar, V_sb):
            heads = (kvh * GRP + pair * 2, kvh * GRP + pair * 2 + 1)
            av_bank = pb[4 + upar]
            state = {"first": True}

            def mk(kb, jj, last):
                def emit():
                    reg = st5["pt_rr"]
                    st5["pt_rr"] = (reg + 1) % 2
                    ptbank = pb[2 + reg]
                    pcol = 0
                    if kb == 0 and jj == 0:
                        P.wait("tensor", "DV", st5["norm_done"].pop((qt, pair)))
                        P.wait("tensor", "AC", st5["av_evac"][upar])
                    P.wait("tensor", "AC", st5["pt_evac"][reg])
                    rf = st5["region_free"].pop(2 + reg, 0)
                    P.wait("tensor", "DV", rf)
                    for hpp in range(2):
                        mm = P.op("tensor",
                                  lambda e, b=ptbank, kb=kb, jj=jj, hpp=hpp, pc=pcol: e.matmul(
                                      b[:, pc + hpp * 128: pc + hpp * 128 + 128],
                                      praw(upar, hpp)[:, kb * 512 + jj * 128:
                                                      kb * 512 + jj * 128 + 128],
                                      ident[:], start=True, stop=True),
                                  "PE" if hpp == 1 else None)
                    if last:
                        st5["pbuf_free"][upar] = mm
                    P.wait("scalar", "PE", mm)
                    P.wait("scalar", "PE", st5["ptsb_free"][reg])
                    aev = P.op("scalar", lambda e, b=ptbank, r=reg, pc=pcol: e.copy(
                        out=ptsb[:, r * 256:(r + 1) * 256], in_=b[:, pc:pc + 256]), "AC")
                    st5["pt_evac"][reg] = aev
                    P.wait("tensor", "AC", aev)
                    mm = P.op("tensor", lambda e, b=av_bank, V=V_sb, kb=kb, jj=jj,
                              st=state["first"], sp=last, r=reg: e.matmul(
                        b[:, 0:256], V[:, (kb * 4 + jj) * 128:(kb * 4 + jj) * 128 + 128],
                        ptsb[:, r * 256:(r + 1) * 256], start=st, stop=sp), "PE")
                    st5["ptsb_free"][reg] = mm
                    state["first"] = False
                    if last:
                        st5["tr_last"] = mm
                        if qt == NT - 1 and pair == 1:
                            kv_consumed[kvh] = mm
                        P.wait("scalar", "PE", mm)
                        for hpp in range(2):
                            h = heads[hpp]
                            ae = P.op("scalar", lambda e, b=av_bank, hpp=hpp, h=h, qt=qt: e.copy(
                                out=qa_pool[:, h * 1024 + qt * 128: h * 1024 + (qt + 1) * 128],
                                in_=b[:, hpp * 128:hpp * 128 + 128]), "AC" if hpp == 1 else None)
                        st5["av_evac"][upar] = ae
                return emit

            out = []
            for kb in range(qt + 1):
                for jj in range(4):
                    out.append(mk(kb, jj, kb == qt and jj == 3))
            return out

        REG = [0, 1, 2, 3, 6, 7]

        def prework_a(kvh, K_sb):
            P.wait("gpsimd", f"KV{kvh % 2}", kld[kvh])
            P.wait("gpsimd", "PE", st5["tr_last"])
            pk = P.op("gpsimd", lambda e, K=K_sb: e.tensor_tensor(
                out=ksink[:], in0=K, in1=K, op=ALU.mult), "PO")
            P.wait("tensor", "PO", pk)
            P.wait("tensor", "AC", st5["last_exp"])
            P.wait("tensor", "AC", max(st5["pt_evac"]))
            P.wait("tensor", "DV", st5["pb7_free"])
            kred = [0] * 8
            for kb in range(8):
                reg = REG[kb % 6]
                if kb >= 6:
                    P.wait("tensor", "DV", kred[kb - 6])
                mm = P.op("tensor", lambda e, b=pb[reg], kb=kb: e.matmul(
                    b[0:1, 0:512], onec[:], ksink[:, kb * 512:(kb + 1) * 512],
                    start=True, stop=True), "PE")
                P.wait("vector", "PE", mm)
                kred[kb] = P.op("vector", lambda e, kb=kb: e.reduce_max(
                    kmrow[0:1, kb:kb + 1], pb[REG[kb % 6]][0:1, 0:512], axis=AX.X), "DV")
            P.wait("vector", "DV", kred[7])
            km_v = P.op("vector", lambda e: e.reduce_max(
                kmrow[0:1, 8:9], kmrow[0:1, 0:8], axis=AX.X), "DV")
            P.wait("gpsimd", "PE", mm)
            pq = P.op("gpsimd", lambda e, kvh=kvh: e.tensor_tensor(
                out=ksink[:], in0=qa_pool[:, kvh * 4096:(kvh + 1) * 4096],
                in1=qa_pool[:, kvh * 4096:(kvh + 1) * 4096], op=ALU.mult), "PO")
            P.wait("tensor", "PO", pq)
            qred = [0] * 8
            for r in range(8):
                reg = REG[r % 6]
                P.wait("tensor", "DV", kred[r % 6] if r < 6 else qred[r - 6])
                if r < 2:
                    P.wait("tensor", "DV", kred[6 + r])
                mm = P.op("tensor", lambda e, b=pb[reg], r=r: e.matmul(
                    b[0:1, 0:512], onec[:], ksink[:, r * 512:(r + 1) * 512],
                    start=True, stop=True), "PE")
                P.wait("vector", "PE", mm)
                qred[r] = P.op("vector", lambda e, r=r: e.reduce_max(
                    kmrow[0:1, 16 + r:17 + r], pb[REG[r % 6]][0:1, 0:512], axis=AX.X), "DV")
            for i, reg in enumerate(REG):
                st5["region_free"][reg] = qred[i]
            st5["region_free"][0] = max(qred[6], st5["region_free"][0])
            st5["region_free"][1] = max(qred[7], st5["region_free"][1])
            return km_v, qred

        def prework_b(kvh, km_v, qred):
            chain_done = 0
            for hp in range(4):
                P.wait("vector", "DV", max(qred[2 * hp], qred[2 * hp + 1], km_v))
                v = P.op("vector", lambda e, hp=hp: e.tensor_max(
                    kmrow[0:1, 24 + hp:25 + hp], kmrow[0:1, 16 + 2 * hp:17 + 2 * hp],
                    kmrow[0:1, 17 + 2 * hp:18 + 2 * hp]), "DV")
                P.wait("vector", "DV", v)
                v = P.op("vector", lambda e, hp=hp: e.tensor_mul(
                    kmrow[0:1, 9 + hp:10 + hp], kmrow[0:1, 8:9],
                    kmrow[0:1, 24 + hp:25 + hp]), "DV")
                P.wait("scalar", "DV", v)
                aa = P.op("scalar", lambda e, hp=hp: e.activation(
                    out=kmrow[0:1, 28 + hp:29 + hp], in_=kmrow[0:1, 9 + hp:10 + hp],
                    func=AF.Sqrt, scale=1.2155), "AC")
                P.wait("vector", "AC", aa)
                v = P.op("vector", lambda e, hp=hp: e.tensor_scalar_mul(
                    kmrow[0:1, 9 + hp:10 + hp], kmrow[0:1, 28 + hp:29 + hp], -ISQ), "DV")
                P.wait("vector", "DV", v)
                v = P.op("vector", lambda e, hp=hp: e.tensor_copy(
                    out=trowb[0:1, 1 + hp:2 + hp], in_=kmrow[0:1, 9 + hp:10 + hp]), "DV")
                chain_done = v
            P.wait("tensor", "DV", chain_done)
            P.wait("tensor", "DV", st5["region_free"].pop(7, 0))
            for hp in range(4):
                mmb = P.op("tensor", lambda e, hp=hp: e.matmul(
                    pb[7][:, 2 + hp:3 + hp], oner[:], trowb[0:1, 1 + hp:2 + hp],
                    start=True, stop=True), "PE" if hp == 3 else None)
            P.wait("vector", "PE", mmb)
            P.wait("vector", "AC", st5["last_exp"])
            bc = P.op("vector", lambda e: e.tensor_copy(out=bias4[:], in_=pb[7][:, 2:6]), "DV")
            st5["pb7_free"] = bc
            st5["bias_done"] = bc

        uidx = 0
        for kvh in range(HKV):
            par = kvh % 2
            K_sb = xn_pool[:, par * 4096:(par + 1) * 4096]
            V_sb = xn_pool[:, 8192 + par * 4096: 8192 + (par + 1) * 4096]
            if kvh + 1 < HKV:
                kld[kvh + 1] = load_kv(kvh + 1, kv_consumed[kvh - 1] if kvh >= 1 else 0)
            km_v, qred = prework_a(kvh, K_sb)
            prework_b(kvh, km_v, qred)
            pending = []
            for qt in range(NT):
                for pair in range(2):
                    upar = uidx % 2
                    emit_unit(kvh, qt, pair, upar, K_sb, pending)
                    pending = trav_chunks(kvh, qt, pair, upar, V_sb)
                    uidx += 1
            for c in pending:
                c()
            pending = []
        return P.val("AC")

    att_done = phase5()
    P.barrier()

    if DEBUG_OUTPUTS:
        for h in range(HQ):
            P.op("sync", lambda e, h=h: e.dma_start(
                out=dbg_att[h], in_=qa_pool[:, h * 1024:(h + 1) * 1024]), "DBG", 16)
        P.barrier()

    if BUILD_PHASES <= 5:
        P.finalize()
        return nc, es

    def phase6():
        for s in range(4):
            P.wait("sync", f"OS{s}", P.val(f"OS{s}"))
        return emit_outpass(
            ngroups=8, stop_ic=31,
            wsrc=lambda oc, g: wo_h[oc, g],
            lhs_fn=lambda ic, t: qa_pool[:, ic * 1024 + t * 128: ic * 1024 + (t + 1) * 128],
            add_src=y1_d, store_dst=y_out)

    phase6()
    for s in range(4):
        P.wait("sync", f"OS{s}", P.val(f"OS{s}"))
    P.finalize()
    return nc, es


def _host_prep(inputs):
    x = np.asarray(inputs["L0_attn_hidden_states"], np.float32)
    wq = np.asarray(inputs["L0_attn_wq"], np.float32)
    wk = np.asarray(inputs["L0_attn_wk"], np.float32)
    wv = np.asarray(inputs["L0_attn_wv"], np.float32)
    wo = np.asarray(inputs["L0_attn_wo"], np.float32)
    worg = np.asarray(inputs["L0_ffn_W_org"], np.float32)
    ldown = np.asarray(inputs["L0_ffn_lora_down"], np.float32)
    lup = np.asarray(inputs["L0_ffn_lora_up"], np.float32)
    lkeys = np.asarray(inputs["L0_ffn_lora_keys"], np.float32)
    wqm = np.asarray(inputs["L0_ffn_Wq"], np.float32)

    wq_h = np.ascontiguousarray(
        wq.reshape(HQ, 128, 32, 128).transpose(0, 3, 2, 1).reshape(HQ, 128, D))
    wk_h = np.ascontiguousarray(
        wk.reshape(HKV, 128, 32, 128).transpose(0, 3, 2, 1).reshape(HKV, 128, D))
    wv_h = np.ascontiguousarray(
        wv.reshape(2, 512, 8, 4, 128).transpose(0, 2, 4, 3, 1).reshape(2, 8, 128, 2048))
    wo_h = np.ascontiguousarray(
        wo.reshape(8, 512, 8, 4, 128).transpose(0, 2, 4, 3, 1).reshape(8, 8, 128, 2048))
    worg_h = np.ascontiguousarray(
        worg.reshape(8, 512, 8, 4, 128).transpose(0, 2, 4, 3, 1).reshape(8, 8, 128, 2048))
    dmat = ldown.transpose(1, 0, 2).reshape(D, R)
    dmat_h = np.ascontiguousarray(
        dmat.reshape(32, 128, 4, 128).transpose(2, 1, 0, 3).reshape(4, 128, 32 * 128))
    umat = lup.reshape(R, D)
    umat_h = np.ascontiguousarray(
        umat.reshape(4, 128, 8, 512).transpose(2, 1, 0, 3).reshape(8, 128, 2048))
    wqm_h = np.ascontiguousarray(
        wqm.reshape(128, 32, 128).transpose(2, 1, 0).reshape(128, 32 * 128))
    keys_t = np.ascontiguousarray(lkeys.T)

    ident = np.eye(128, dtype=bfdt)
    onec = np.ones((128, 1), dtype=bfdt)
    oner = np.ones((1, 128), dtype=bfdt)
    one1 = np.ones((1, 1), dtype=bfdt)
    sel = np.zeros((NE, 512), dtype=np.float32)
    for rc in range(4):
        for p in range(128):
            sel[2 * rc + p // 64, rc * 128 + p] = 1.0
    sel = sel.astype(bfdt)

    inv_freq = 1.0 / (10000.0 ** (np.arange(0, DH, 2, dtype=np.float32) / DH))

    in_maps = []
    for c in range(8):
        b, j = c // 4, c % 4
        xr = x[b].reshape(NT, 4, 128, D)[:, j]
        x_core = np.ascontiguousarray(xr.reshape(TL, D))
        pos = (np.arange(NT)[:, None] * 512 + j * 128 +
               np.arange(128)[None, :]).reshape(TL).astype(np.float32)
        ang = pos[None, :] * inv_freq[:, None]
        ropec_a = np.empty((128, TL), np.float32)
        ropes_a = np.empty((128, TL), np.float32)
        ropec_a[0:64] = np.cos(ang); ropec_a[64:128] = np.cos(ang)
        ropes_a[0:64] = -np.sin(ang); ropes_a[64:128] = np.sin(ang)
        qi = np.arange(128)[:, None]
        ji = np.arange(512)[None, :]
        mask_a = np.where(j * 128 + qi >= ji, 0.0, NEG).astype(bfdt)
        in_maps.append({
            "x_in": x_core, "wq_h": wq_h, "wk_h": wk_h, "wv_h": wv_h,
            "wo_h": wo_h, "worg_h": worg_h, "dmat_h": dmat_h, "umat_h": umat_h,
            "wqm_h": wqm_h, "keys_t": keys_t, "rope_c": ropec_a, "rope_s": ropes_a,
            "mask_in": mask_a, "ident_in": ident, "onec_in": onec, "oner_in": oner,
            "one1_in": one1, "sel_in": sel,
        })
    return in_maps


def _assemble(results):
    y = np.empty((B, S, D), np.float32)
    for c in range(8):
        b, j = c // 4, c % 4
        y[b].reshape(NT, 4, 128, D)[:, j] = results[c]["y_out"].reshape(NT, 128, D)
    return y


def kernel(**inputs):
    nc, es = build_program()
    in_maps = _host_prep(inputs)
    try:
        res = run_bass_kernel_spmd(nc, in_maps, core_ids=list(range(8)))
    finally:
        es.close()
    return _assemble(res.results)


# version 07
# speedup vs baseline: 1.2369x; 1.1779x over previous
import math
import numpy as np
import ml_dtypes
import concourse.bass as bass
import concourse.mybir as mybir
from concourse.bass_utils import run_bass_kernel_spmd
import concourse.bass_utils as _bu

if not getattr(_bu, "_ldw_opt_patched", False):
    _orig_run_command = _bu.run_command

    def _run_command_ldw(cmd, **kw):
        cmd = [c.replace("--enable-ldw-opt=false", "--enable-ldw-opt=true")
               if isinstance(c, str) else c for c in cmd]
        return _orig_run_command(cmd, **kw)

    _bu.run_command = _run_command_ldw
    _bu._ldw_opt_patched = True

F32 = mybir.dt.float32
BF16 = mybir.dt.bfloat16
AF = mybir.ActivationFunctionType
ALU = mybir.AluOpType
AX = mybir.AxisListType
bfdt = ml_dtypes.bfloat16

B, S, D = 2, 4096, 4096
HQ, HKV, DH = 32, 8, 128
GRP = HQ // HKV
NT = 8
TL = 1024
R = 512
NE, KD = 8, 128
EPS = 1e-5
ISQ = 1.0 / math.sqrt(DH)
NEG = -1e9

DEBUG_OUTPUTS = False
BUILD_PHASES = 99
PHASE2_CUT = "full"


class Prog:
    ENGS = ("sync", "gpsimd", "tensor", "vector", "scalar")
    ENG_SEMS = ("PE", "DV", "AC", "PO")
    DMA_SEMS = ("CONST", "XL0", "XL1", "XL2", "XL3", "KV0", "KV1", "MN", "MS",
                "KS0", "KS1", "VS0", "VS1", "OS0", "OS1", "OS2", "OS3",
                "WP0", "WP1", "WS0", "WS1", "WS2", "WS3", "WM", "DBG")
    CC_SEMS = ("CCM", "CCK")
    ALL_SEMS = ENG_SEMS + DMA_SEMS + CC_SEMS

    def __init__(self, nc, es):
        self.nc = nc
        self.q = {e: [] for e in self.ENGS}
        self.sem = {}
        self.cnt = {}
        self._es = es

    def mksem(self, name):
        self.sem[name] = self._es.enter_context(self.nc.semaphore(name))
        self.cnt[name] = 0

    def wait(self, eng, sem, val):
        if val <= 0:
            return
        h = self.sem[sem]
        self.q[eng].append(lambda e, h=h, v=val: e.wait_ge(h, v))

    def op(self, eng, fn, inc=None, n=1):
        if inc is None:
            self.q[eng].append(fn)
            return None
        self.cnt[inc] += n
        v = self.cnt[inc]
        h = self.sem[inc]
        self.q[eng].append(lambda e, fn=fn, h=h, n=n: fn(e).then_inc(h, n))
        return v

    def val(self, sem):
        return self.cnt[sem]

    def barrier(self):
        marks = [(s, self.val(s)) for s in self.ENG_SEMS + self.DMA_SEMS]
        for eng in self.ENGS:
            for sem, v in marks:
                self.wait(eng, sem, v)

    def finalize(self):
        with self.nc.Block() as block:
            for eng in self.ENGS:
                ops = self.q[eng]

                def run(e, ops=ops):
                    for f in ops:
                        f(e)

                getattr(block, eng)(run)


def build_program():
    from contextlib import ExitStack

    nc = bass.Bass("TRN2", target_bir_lowering=False, debug=False)
    es = ExitStack()

    def dram(name, shape, dtype, kind):
        return nc.dram_tensor(name, shape, dtype, kind=kind).ap()

    x_in = dram("x_in", [TL, D], F32, "ExternalInput")
    wq_h = dram("wq_h", [HQ, 128, D], F32, "ExternalInput")
    wk_h = dram("wk_h", [HKV, 128, D], F32, "ExternalInput")
    wv_h = dram("wv_h", [2, 8, 128, 2048], F32, "ExternalInput")
    wo_h = dram("wo_h", [8, 8, 128, 2048], F32, "ExternalInput")
    worg_h = dram("worg_h", [8, 8, 128, 2048], F32, "ExternalInput")
    dmat_h = dram("dmat_h", [4, 128, 32 * 128], F32, "ExternalInput")
    umat_h = dram("umat_h", [8, 128, 2048], F32, "ExternalInput")
    wqm_h = dram("wqm_h", [128, 32 * 128], F32, "ExternalInput")
    keys_t = dram("keys_t", [KD, NE], F32, "ExternalInput")
    rope_c = dram("rope_c", [128, TL], F32, "ExternalInput")
    rope_s = dram("rope_s", [128, TL], F32, "ExternalInput")
    mask_in = dram("mask_in", [128, 512], BF16, "ExternalInput")
    ident_in = dram("ident_in", [128, 128], BF16, "ExternalInput")
    onec_in = dram("onec_in", [128, 1], BF16, "ExternalInput")
    oner_in = dram("oner_in", [1, 128], BF16, "ExternalInput")
    one1_in = dram("one1_in", [1, 1], BF16, "ExternalInput")
    sel_in = dram("sel_in", [NE, 512], BF16, "ExternalInput")

    y_out = dram("y_out", [TL, D], F32, "ExternalOutput")

    kv_ci = dram("kv_ci", [2048, 1024], BF16, "Internal")
    kv_co = dram("kv_co", [4 * 2048, 1024], BF16, "Internal")
    mn_ci = dram("mn_ci", [128, 32], F32, "Internal")
    mn_co = dram("mn_co", [128, 32], F32, "Internal")
    y1_d = dram("y1_d", [TL, D], F32, "Internal")
    RG = [[0, 1, 2, 3], [4, 5, 6, 7]]

    if DEBUG_OUTPUTS:
        dbg_xn = dram("dbg_xn", [32, 128, TL], BF16, "ExternalOutput")
        dbg_coef = dram("dbg_coef", [128, 4], F32, "ExternalOutput")
        dbg_q = dram("dbg_q", [HQ, 128, TL], BF16, "ExternalOutput")
        dbg_kv = dram("dbg_kv", [4 * 2048, 1024], BF16, "ExternalOutput")
        dbg_att = dram("dbg_att", [HQ, 128, TL], BF16, "ExternalOutput")
        dbg_y1 = dram("dbg_y1", [TL, D], F32, "ExternalOutput")

    sb = lambda name, shape, dt: es.enter_context(nc.sbuf_tensor(name, shape, dt))
    xn_pool = sb("xn_pool", [128, 32768], BF16)
    qa_pool = sb("qa_pool", [128, 32768], BF16)
    f32_pool = sb("f32_pool", [128, 8192], F32)
    wpool = sb("wpool", [128, 8192], BF16)
    ptsb = sb("ptsb", [128, 1024], BF16)
    stage = sb("stage", [128, 1536], BF16)
    ident = sb("ident", [128, 128], BF16)
    onec = sb("onec", [128, 1], BF16)
    oner = sb("oner", [1, 128], BF16)
    one1 = sb("one1", [1, 1], BF16)
    mask = sb("mask", [128, 512], BF16)
    ropec = sb("ropec", [128, TL], F32)
    ropes = sb("ropes", [128, TL], F32)
    selm = sb("selm", [NE, 512], BF16)
    keyst = sb("keyst", [KD, NE], BF16)
    ra = f32_pool[:, 0:512]
    rb = f32_pool[:, 512:1024]
    rt1 = f32_pool[:, 1024:1536]
    rt2 = f32_pool[:, 1536:2048]
    colsum = sb("colsum", [128, 32], F32)
    meansb = sb("meansb", [128, 32], F32)
    meanbf = sb("meanbf", [128, 32], BF16)
    stat = sb("stat", [128, 16], F32)
    sums4 = sb("sums4", [128, 32], F32)
    tot4 = sb("tot4", [128, 4], F32)
    rcp4 = sb("rcp4", [128, 4], F32)
    bias4 = sb("bias4", [128, 4], F32)
    scol = sb("scol", [128, 4], F32)
    trowb = sb("trowb", [1, 128], BF16)
    kmrow = sb("kmrow", [1, 32], F32)

    pb = [es.enter_context(nc.psum_tensor(f"pb{i}", [128, 512], F32)) for i in range(8)]

    P = Prog(nc, es)
    for s in Prog.ALL_SEMS:
        P.mksem(s)

    for dst, src in ((ident, ident_in), (onec, onec_in), (oner, oner_in),
                     (one1, one1_in), (mask, mask_in), (ropec, rope_c),
                     (ropes, rope_s), (selm, sel_in)):
        P.op("sync", lambda e, d=dst, s=src: e.dma_start(out=d[:], in_=s[:]), "CONST", 16)
    P.op("gpsimd", lambda e: e.dma_start(out=keyst[:], in_=keys_t[:]), "WM", 16)
    P.barrier()

    def phase1():
        tp_free = [0, 0]
        norm_done = [0] * NT
        stage_free = [0, 0]
        xld = [0] * NT
        for qt in range(NT):
            par = qt % 2
            xbuf = f32_pool[:, par * 4096:(par + 1) * 4096]
            st = lambda i, par=par: stat[:, par * 8 + i: par * 8 + i + 1]
            if qt >= 2:
                P.wait("sync", "AC", norm_done[qt - 2])
            xld[qt] = P.op("sync", lambda e, d=xbuf, q=qt: e.dma_start(
                out=d, in_=x_in[q * 128:(q + 1) * 128, :]), f"XL{par}", 16)
            P.wait("vector", f"XL{par}", xld[qt])
            sink = qa_pool[:, 8192:12288]
            P.wait("scalar", f"XL{par}", xld[qt])
            a0 = P.op("scalar", lambda e, x=xbuf, st=st, s=sink: e.activation(
                out=s, in_=x, func=AF.Square, accum_out=st(2)), "AC")
            v = P.op("vector", lambda e, x=xbuf, st=st: e.reduce_sum(
                st(0), x, axis=AX.X, negate=True), "DV")
            P.wait("vector", "DV", v)
            v = P.op("vector", lambda e, st=st: e.tensor_scalar_mul(st(1), st(0), 1.0 / D), "DV")
            P.wait("vector", "DV", v)
            v = P.op("vector", lambda e, st=st: e.tensor_mul(st(3), st(1), st(1)), "DV")
            P.wait("vector", "AC", a0)
            P.wait("vector", "DV", v)
            v = P.op("vector", lambda e, st=st: e.tensor_scalar(
                st(4), st(2), 1.0 / D, EPS, ALU.mult, ALU.add), "DV")
            P.wait("vector", "DV", v)
            v = P.op("vector", lambda e, st=st: e.tensor_sub(st(5), st(4), st(3)), "DV")
            P.wait("scalar", "DV", v)
            a = P.op("scalar", lambda e, st=st: e.activation(out=st(6), in_=st(5), func=AF.Sqrt), "AC")
            P.wait("vector", "AC", a)
            v = P.op("vector", lambda e, st=st: e.reciprocal(out=st(7), in_=st(6)), "DV")
            P.wait("vector", "DV", v)
            v = P.op("vector", lambda e, st=st: e.tensor_mul(st(3), st(1), st(7)), "DV")
            xst = qa_pool[:, par * 4096:(par + 1) * 4096]
            P.wait("scalar", "DV", v)
            P.wait("scalar", "PE", stage_free[par])
            a = P.op("scalar", lambda e, x=xbuf, o=xst, st=st: e.activation(
                out=o, in_=x, func=AF.Identity, scale=st(7), bias=st(3)), "AC")
            norm_done[qt] = a
            for g in range(8):
                bank = pb[g % 2]
                P.wait("tensor", "AC", a)
                P.wait("tensor", "DV", tp_free[g % 2])
                for u in range(4):
                    dc = g * 4 + u
                    pe_v = P.op("tensor", lambda e, b=bank, u=u, x=xst, dc=dc: e.matmul(
                        b[:, u * 128:(u + 1) * 128], x[:, dc * 128:(dc + 1) * 128],
                        ident[:], start=True, stop=True), "PE" if u == 3 else None)
                P.wait("vector", "PE", pe_v)
                for u in range(4):
                    dc = g * 4 + u
                    dst = xn_pool[:, dc * 1024 + qt * 128: dc * 1024 + (qt + 1) * 128]
                    ev = P.op("vector", lambda e, b=bank, u=u, d=dst: e.tensor_copy(
                        out=d, in_=b[:, u * 128:(u + 1) * 128]), "DV" if u == 3 else None)
                tp_free[g % 2] = ev
            stage_free[par] = pe_v
        P.wait("vector", "DV", max(tp_free[0], tp_free[1]))
        for dc in range(32):
            cv = P.op("vector", lambda e, dc=dc: e.reduce_sum(
                colsum[:, dc:dc + 1], xn_pool[:, dc * 1024:(dc + 1) * 1024], axis=AX.X),
                "DV" if dc == 31 else None)
        P.wait("sync", "DV", cv)
        stv = P.op("sync", lambda e: e.dma_start(out=mn_ci[:], in_=colsum[:]), "MS", 16)
        P.wait("gpsimd", "MS", stv)
        ar = P.op("gpsimd", lambda e: e.collective_compute(
            "AllReduce", ALU.add, RG, [mn_ci[:]], [mn_co[:]]), "CCM", 1)
        return ar

    mean_ar = phase1()
    P.barrier()

    if DEBUG_OUTPUTS:
        for dc in range(32):
            P.op("sync", lambda e, dc=dc: e.dma_start(
                out=dbg_xn[dc], in_=xn_pool[:, dc * 1024:(dc + 1) * 1024]), "DBG", 16)
        P.barrier()

    if BUILD_PHASES <= 1:
        P.finalize()
        return nc, es

    rope_state = {"p1": 0, "v4": 0, "stage_st": {}}

    def rope_evac(bank, th, dst_bf, pe_v, extra_dst_wait=None):
        C = ropec[:, th * 512:(th + 1) * 512]
        Ssg = ropes[:, th * 512:(th + 1) * 512]
        P.wait("vector", "PE", pe_v)
        P.wait("vector", "PO", rope_state["p1"])
        v1 = P.op("vector", lambda e, b=bank: e.tensor_copy(out=rb[64:128, :], in_=b[0:64, :]), "DV")
        v2 = P.op("vector", lambda e, b=bank: e.tensor_copy(out=rb[0:64, :], in_=b[64:128, :]), "DV")
        P.wait("scalar", "PE", pe_v)
        P.wait("scalar", "DV", max(v2, rope_state["v4"]))
        a1 = P.op("scalar", lambda e, b=bank: e.copy(out=ra[:], in_=b[:]), "AC")
        P.wait("vector", "AC", a1)
        v3 = P.op("vector", lambda e, C=C: e.tensor_mul(rt1[:], ra[:], C), "DV")
        P.wait("gpsimd", "DV", max(v2, rope_state["v4"]))
        p1 = P.op("gpsimd", lambda e, Ssg=Ssg: e.tensor_tensor(
            out=rt2[:], in0=rb[:], in1=Ssg, op=ALU.mult), "PO")
        rope_state["p1"] = p1
        P.wait("vector", "DV", v3)
        P.wait("vector", "PO", p1)
        if extra_dst_wait is not None:
            sem, val = extra_dst_wait
            P.wait("vector", sem, val)
        v4 = P.op("vector", lambda e, d=dst_bf: e.tensor_add(d, rt1[:], rt2[:]), "DV")
        rope_state["v4"] = v4
        return v2, v4

    def phase2():
        wfree = [0, 0]
        bank_free = [0] * 4
        kst_by_th = {0: 0, 1: 0}
        for kvh in range(HKV):
            par = kvh % 2
            wbuf = wpool[:, par * 4096:(par + 1) * 4096]
            P.wait("gpsimd", "PE", wfree[par])
            wld = P.op("gpsimd", lambda e, d=wbuf, k=kvh: e.dma_start(
                out=d, in_=wk_h[k]), f"WP{par}", 16)
            P.wait("tensor", f"WP{par}", wld)
            for th in range(2):
                bslot = (kvh * 2 + th) % 4
                bank = pb[4 + bslot]
                P.wait("tensor", "DV", bank_free[bslot])
                for ic in range(32):
                    mm = P.op("tensor", lambda e, b=bank, w=wbuf, ic=ic, th=th: e.matmul(
                        b[:], w[:, ic * 128:(ic + 1) * 128],
                        xn_pool[:, ic * 1024 + th * 512: ic * 1024 + (th + 1) * 512],
                        start=(ic == 0), stop=(ic == 31)), "PE" if ic == 31 else None)
                if th == 1:
                    wfree[par] = mm
                dst = stage[:, th * 512:(th + 1) * 512]
                if PHASE2_CUT == "kmm":
                    P.wait("vector", "PE", mm)
                    done = P.op("vector", lambda e, b=bank, d=dst: e.tensor_copy(out=d, in_=b[:]), "DV")
                    bank_free[bslot] = done
                    continue
                cons, done = rope_evac(bank, th, dst, mm,
                                       extra_dst_wait=(f"KS{th}", kst_by_th[th]))
                bank_free[bslot] = done
                if PHASE2_CUT == "krope":
                    continue
                P.wait("sync", "DV", done)
                sv = P.op("sync", lambda e, k=kvh, th=th, d=dst: e.dma_start(
                    out=kv_ci[k * 128:(k + 1) * 128, th * 512:(th + 1) * 512], in_=d), f"KS{th}", 16)
                kst_by_th[th] = sv
        if PHASE2_CUT == "k":
            return 0
        P.wait("tensor", "DV", max(bank_free))
        P.wait("scalar", "KS0", P.val("KS0"))
        P.wait("scalar", "KS1", P.val("KS1"))
        vslot_free = [P.val("PE")] * 2
        vbank_free = [0] * NT
        vst_slot = [0, 0]
        for dvh in range(2):
            for g in range(8):
                slot = g % 2
                gbuf = wpool[:, slot * 2048:(slot + 1) * 2048]
                P.wait("gpsimd", "PE", vslot_free[slot])
                wv = P.op("gpsimd", lambda e, d=gbuf, dvh=dvh, g=g: e.dma_start(
                    out=d, in_=wv_h[dvh, g]), f"WS{slot}", 16)
                P.wait("tensor", f"WS{slot}", wv)
                for gi in range(4):
                    ic = g * 4 + gi
                    for t in range(NT):
                        if ic == 0:
                            P.wait("tensor", "AC", vbank_free[t])
                        mm = P.op("tensor", lambda e, b=pb[t], ic=ic, t=t, w=gbuf, gi=gi: e.matmul(
                            b[:], xn_pool[:, ic * 1024 + t * 128: ic * 1024 + (t + 1) * 128],
                            w[:, gi * 512:(gi + 1) * 512],
                            start=(ic == 0), stop=(ic == 31)), "PE" if t == NT - 1 else None)
                vslot_free[slot] = mm
            for t in range(NT):
                sl = t % 2
                vdst = stage[:, sl * 512:(sl + 1) * 512]
                P.wait("scalar", "PE", mm)
                P.wait("scalar", f"VS{sl}", vst_slot[sl])
                av = P.op("scalar", lambda e, b=pb[t], d=vdst: e.copy(out=d, in_=b[:]), "AC")
                vbank_free[t] = av
                P.wait("sync", "AC", av)
                vst_slot[sl] = P.op("sync", lambda e, t=t, dvh=dvh, d=vdst: e.dma_start(
                    out=kv_ci[1024 + t * 128:1024 + (t + 1) * 128, dvh * 512:(dvh + 1) * 512],
                    in_=d), f"VS{sl}", 16)
        if PHASE2_CUT == "kv":
            return 0
        P.wait("gpsimd", "KS0", P.val("KS0"))
        P.wait("gpsimd", "KS1", P.val("KS1"))
        P.wait("gpsimd", "VS0", P.val("VS0"))
        P.wait("gpsimd", "VS1", P.val("VS1"))
        ag = P.op("gpsimd", lambda e: e.collective_compute(
            "AllGather", ALU.bypass, RG, [kv_ci[:]], [kv_co[:]]), "CCK", 1)
        return ag

    kv_ag = phase2()
    P.barrier()

    if BUILD_PHASES <= 2:
        P.finalize()
        return nc, es

    def phase3():
        wfree = [0, 0]
        bank_free = [0] * 4
        for h in range(HQ):
            par = h % 2
            wbuf = wpool[:, par * 4096:(par + 1) * 4096]
            P.wait("gpsimd", "PE", wfree[par])
            wld = P.op("gpsimd", lambda e, d=wbuf, h=h: e.dma_start(
                out=d, in_=wq_h[h]), f"WP{par}", 16)
            P.wait("tensor", f"WP{par}", wld)
            for th in range(2):
                bslot = (h * 2 + th) % 4
                bank = pb[4 + bslot]
                P.wait("tensor", "DV", bank_free[bslot])
                for ic in range(32):
                    mm = P.op("tensor", lambda e, b=bank, w=wbuf, ic=ic, th=th: e.matmul(
                        b[:], w[:, ic * 128:(ic + 1) * 128],
                        xn_pool[:, ic * 1024 + th * 512: ic * 1024 + (th + 1) * 512],
                        start=(ic == 0), stop=(ic == 31)), "PE" if ic == 31 else None)
                if th == 1:
                    wfree[par] = mm
                dst = qa_pool[:, h * 1024 + th * 512: h * 1024 + (th + 1) * 512]
                cons, done = rope_evac(bank, th, dst, mm)
                bank_free[bslot] = done
        return done

    q_done = phase3()
    P.barrier()

    if DEBUG_OUTPUTS:
        for h in range(HQ):
            P.op("sync", lambda e, h=h: e.dma_start(
                out=dbg_q[h], in_=qa_pool[:, h * 1024:(h + 1) * 1024]), "DBG", 16)
        P.wait("sync", "CCK", kv_ag)
        P.op("sync", lambda e: e.dma_start(out=dbg_kv[:], in_=kv_co[:]), "DBG", 16)
        P.barrier()

    if BUILD_PHASES <= 3:
        P.finalize()
        return nc, es

    def phase_coef():
        P.wait("sync", "CCM", mean_ar)
        mld = P.op("sync", lambda e: e.dma_start(out=meansb[:], in_=mn_co[:]), "MN", 16)
        P.wait("vector", "MN", mld)
        v = P.op("vector", lambda e: e.tensor_scalar_mul(meansb[:], meansb[:], 1.0 / S), "DV")
        P.wait("vector", "DV", v)
        v = P.op("vector", lambda e: e.tensor_copy(out=meanbf[:], in_=meansb[:]), "DV")
        wld = P.op("gpsimd", lambda e: e.dma_start(out=wpool[:, 0:4096], in_=wqm_h[:]), "WM", 16)
        P.wait("tensor", "DV", v)
        P.wait("tensor", "WM", wld)
        for ic in range(32):
            mm = P.op("tensor", lambda e, ic=ic: e.matmul(
                pb[7][0:1, 0:128], meanbf[:, ic:ic + 1],
                wpool[:, ic * 128:(ic + 1) * 128], start=(ic == 0), stop=(ic == 31)),
                "PE" if ic == 31 else None)
        P.wait("scalar", "PE", mm)
        a = P.op("scalar", lambda e: e.copy(out=trowb[:], in_=pb[7][0:1, 0:128]), "AC")
        P.wait("tensor", "AC", a)
        mm = P.op("tensor", lambda e: e.matmul(pb[6][:, 0:1], trowb[:], one1[:],
                                               start=True, stop=True), "PE")
        P.wait("vector", "PE", mm)
        v = P.op("vector", lambda e: e.tensor_copy(out=meanbf[:, 0:1], in_=pb[6][:, 0:1]), "DV")
        P.wait("tensor", "DV", v)
        mm = P.op("tensor", lambda e: e.matmul(pb[6][0:1, 16:24], meanbf[:, 0:1], keyst[:],
                                               start=True, stop=True), "PE")
        P.wait("vector", "PE", mm)
        v = P.op("vector", lambda e: e.tensor_scalar_mul(
            kmrow[0:1, 0:8], pb[6][0:1, 16:24], 1.0 / math.sqrt(KD)), "DV")
        P.wait("vector", "DV", v)
        v = P.op("vector", lambda e: e.reduce_max(kmrow[0:1, 8:9], kmrow[0:1, 0:8],
                                                  axis=AX.X, negate=True), "DV")
        P.wait("scalar", "DV", v)
        a = P.op("scalar", lambda e: e.activation(out=kmrow[0:1, 10:18], in_=kmrow[0:1, 0:8],
                                                  func=AF.Exp, bias=kmrow[0:1, 8:9], scale=1.0,
                                                  accum_out=kmrow[0:1, 9:10]), "AC")
        P.wait("vector", "AC", a)
        v = P.op("vector", lambda e: e.reciprocal(out=kmrow[0:1, 8:9], in_=kmrow[0:1, 9:10]), "DV")
        P.wait("vector", "DV", v)
        v = P.op("vector", lambda e: e.tensor_scalar_mul(
            kmrow[0:1, 10:18], kmrow[0:1, 10:18], kmrow[0:1, 8:9]), "DV")
        P.wait("vector", "DV", v)
        v = P.op("vector", lambda e: e.tensor_copy(out=trowb[0:1, 0:8], in_=kmrow[0:1, 10:18]), "DV")
        P.wait("tensor", "DV", v)
        mm = P.op("tensor", lambda e: e.matmul(pb[6][0:8, 8:9], trowb[0:1, 0:8], one1[:],
                                               start=True, stop=True), "PE")
        P.wait("vector", "PE", mm)
        v = P.op("vector", lambda e: e.tensor_copy(out=meanbf[0:8, 1:2], in_=pb[6][0:8, 8:9]), "DV")
        P.wait("tensor", "DV", v)
        for rc in range(4):
            mm = P.op("tensor", lambda e, rc=rc: e.matmul(
                pb[6][:, 2 + rc:3 + rc], selm[:, rc * 128:(rc + 1) * 128], meanbf[0:8, 1:2],
                start=True, stop=True), "PE" if rc == 3 else None)
        P.wait("vector", "PE", mm)
        v = P.op("vector", lambda e: e.tensor_copy(out=scol[:], in_=pb[6][:, 2:6]), "DV")
        return v

    coef_done = phase_coef()
    P.barrier()

    if DEBUG_OUTPUTS:
        P.op("sync", lambda e: e.dma_start(out=dbg_coef[:], in_=scol[:]), "DBG", 16)
        P.barrier()

    def emit_outpass(ngroups, stop_ic, wsrc, lhs_fn, add_src, store_dst,
                     init_bank_ev=None, init_slot_free=0, extra_mm_wait=None):
        wslot_free = [init_slot_free, init_slot_free]
        bank_ev = [0] * NT
        if init_bank_ev:
            for kk, vv in init_bank_ev.items():
                bank_ev[kk] = vv
        xl_free = [0] * 4
        out_free = [0] * 4
        xv = [0] * NT
        last_sv = 0
        def stage_load(oc, t):
            sl = t % 4
            xs = f32_pool[:, sl * 512:(sl + 1) * 512]
            P.wait("sync", "DV", xl_free[sl])
            xv[t] = P.op("sync", lambda e, d=xs, t=t, oc=oc, a=add_src: e.dma_start(
                out=d, in_=a[t * 128:(t + 1) * 128, oc * 512:(oc + 1) * 512]), f"XL{sl}", 16)

        for oc in range(8):
            for t in range(4):
                stage_load(oc, t)
            for g in range(ngroups):
                slot = g % 2
                gbuf = wpool[:, slot * 2048:(slot + 1) * 2048]
                P.wait("gpsimd", "PE", wslot_free[slot])
                wv = P.op("gpsimd", lambda e, d=gbuf, oc=oc, g=g: e.dma_start(
                    out=d, in_=wsrc(oc, g)), f"WS{slot}", 16)
                P.wait("tensor", f"WS{slot}", wv)
                for gi in range(4):
                    ic = g * 4 + gi
                    if ic > stop_ic:
                        break
                    if extra_mm_wait is not None and ic == extra_mm_wait[2] * 4:
                        if oc == 0:
                            P.wait("tensor", extra_mm_wait[0], extra_mm_wait[1])
                    for t in range(NT):
                        if ic == 0:
                            P.wait("tensor", "DV", bank_ev[t])
                        mm = P.op("tensor",
                                  lambda e, b=pb[t], l=lhs_fn(ic, t), w=gbuf, gi=gi, ic=ic: e.matmul(
                                      b[:], l, w[:, gi * 512:(gi + 1) * 512],
                                      start=(ic == 0), stop=(ic == stop_ic)),
                                  "PE" if t == NT - 1 else None)
                wslot_free[slot] = mm
            for t in range(NT):
                sl = t % 4
                xs = f32_pool[:, sl * 512:(sl + 1) * 512]
                od = f32_pool[:, 4096 + sl * 512: 4096 + (sl + 1) * 512]
                P.wait("vector", "PE", mm)
                P.wait("vector", f"XL{sl}", xv[t])
                P.wait("vector", f"OS{sl}", out_free[sl])
                ev = P.op("vector", lambda e, b=pb[t], x=xs, o=od: e.tensor_add(o, b[:], x), "DV")
                xl_free[sl] = ev
                bank_ev[t] = ev
                P.wait("sync", "DV", ev)
                sv = P.op("sync", lambda e, o=od, t=t, oc=oc, s=store_dst: e.dma_start(
                    out=s[t * 128:(t + 1) * 128, oc * 512:(oc + 1) * 512], in_=o), f"OS{sl}", 16)
                out_free[sl] = sv
                last_sv = sv
                if t + 4 < NT:
                    stage_load(oc, t + 4)
        return last_sv

    def phase4():
        h_sb = wpool[:, 4096:8192]
        hfree = [0, 0]
        last_rc_mm = 0
        for rc in range(4):
            P.wait("gpsimd", "PE", last_rc_mm)
            wld = P.op("gpsimd", lambda e, rc=rc: e.dma_start(
                out=wpool[:, 0:4096], in_=dmat_h[rc]), "WP0", 16)
            P.wait("tensor", "WP0", wld)
            for th in range(2):
                bank = pb[6 + th]
                P.wait("tensor", "DV", hfree[th])
                for ic in range(32):
                    mm = P.op("tensor", lambda e, b=bank, ic=ic, th=th: e.matmul(
                        b[:], wpool[:, ic * 128:(ic + 1) * 128],
                        xn_pool[:, ic * 1024 + th * 512: ic * 1024 + (th + 1) * 512],
                        start=(ic == 0), stop=(ic == 31)), "PE" if ic == 31 else None)
                last_rc_mm = mm
                P.wait("vector", "PE", mm)
                hv = P.op("vector", lambda e, b=bank, rc=rc, th=th: e.tensor_scalar_mul(
                    h_sb[:, rc * 1024 + th * 512: rc * 1024 + (th + 1) * 512],
                    b[:], scol[:, rc:rc + 1]), "DV")
                hfree[th] = hv
        h_done = hv
        return emit_outpass(
            ngroups=9, stop_ic=35,
            wsrc=lambda oc, g: worg_h[oc, g] if g < 8 else umat_h[oc],
            lhs_fn=lambda ic, t: (xn_pool[:, ic * 1024 + t * 128: ic * 1024 + (t + 1) * 128]
                                  if ic < 32 else
                                  h_sb[:, (ic - 32) * 1024 + t * 128: (ic - 32) * 1024 + (t + 1) * 128]),
            add_src=x_in, store_dst=y1_d,
            init_bank_ev={6: hfree[0], 7: hfree[1]},
            init_slot_free=last_rc_mm,
            extra_mm_wait=("DV", h_done, 8))

    y1_last = phase4()
    P.barrier()

    if DEBUG_OUTPUTS:
        P.op("sync", lambda e: e.dma_start(out=dbg_y1[:], in_=y1_d[:]), "DBG", 16)
        P.barrier()

    if BUILD_PHASES <= 4:
        P.finalize()
        return nc, es

    def phase5():
        ksink = xn_pool[:, 16384:16384 + 4096]
        P.wait("sync", "CCK", kv_ag)
        kv_consumed = [0] * HKV

        def load_kv(kvh, wait_pe_val):
            par = kvh % 2
            P.wait("sync", "PE", wait_pe_val)
            last = 0
            for jj in range(4):
                kin = kv_co[jj * 2048 + kvh * 128: jj * 2048 + (kvh + 1) * 128, :]
                kout = xn_pool[:, par * 4096:(par + 1) * 4096].rearrange(
                    "p (kb j i) -> p kb j i", kb=8, j=4)[:, :, jj, :]
                last = P.op("sync", lambda e, o=kout,
                            i=kin.rearrange("p (kb i) -> p kb i", kb=8):
                            e.dma_start(out=o, in_=i), f"KV{par}", 16)
                vin = kv_co[jj * 2048 + 1024: jj * 2048 + 2048, kvh * 128:(kvh + 1) * 128]
                vout = xn_pool[:, 8192 + par * 4096: 8192 + (par + 1) * 4096].rearrange(
                    "p (kb j dv) -> p kb j dv", kb=8, j=4)[:, :, jj, :]
                last = P.op("sync", lambda e, o=vout,
                            i=vin.rearrange("(kb p) dv -> p kb dv", p=128):
                            e.dma_start(out=o, in_=i), f"KV{par}", 16)
            return last

        kld = [0] * HKV
        kld[0] = load_kv(0, 0)

        SCB = [0, 1, 6]
        st5 = {
            "sc_rr": 0,
            "sc_exp": [0, 0, 0],
            "sums_read": [0, 0, 0, 0],
            "last_exp": 0,
            "pt_evac": [0, 0],
            "ptsb_free": [0, 0],
            "pt_rr": 0,
            "av_evac": [0, 0],
            "tr_last": 0,
            "pb7_free": 0,
            "region_free": {},
            "pbuf_free": [0, 0],
            "norm_done": {},
        }

        def praw(upar, hpp):
            off = 16384 + upar * 8192 + hpp * 4096
            return xn_pool[:, off:off + 4096]

        def emit_unit(kvh, qt, pair, upar, K_sb, pending):
            heads = (kvh * GRP + pair * 2, kvh * GRP + pair * 2 + 1)
            nsc = 2 * (qt + 1)
            ci = 0
            emitted = 0
            rcpv = [0, 0]
            exp_last = [0, 0]
            for kb in range(qt + 1):
                for hpp in range(2):
                    h = heads[hpp]
                    slot = st5["sc_rr"]
                    st5["sc_rr"] = (slot + 1) % 3
                    bank = pb[SCB[slot]]
                    P.wait("tensor", "AC", st5["sc_exp"][slot])
                    rf = st5["region_free"].pop(SCB[slot], 0)
                    P.wait("tensor", "DV", rf)
                    if kb == qt:
                        P.op("tensor", lambda e, b=bank, h=h, qt=qt, kb=kb, K=K_sb: e.matmul(
                            b[:], qa_pool[:, h * 1024 + qt * 128: h * 1024 + (qt + 1) * 128],
                            K[:, kb * 512:(kb + 1) * 512], start=True, stop=False))
                        mm = P.op("tensor", lambda e, b=bank: e.matmul(
                            b[:], ident[:], mask[:], start=False, stop=True), "PE")
                    else:
                        mm = P.op("tensor", lambda e, b=bank, h=h, qt=qt, kb=kb, K=K_sb: e.matmul(
                            b[:], qa_pool[:, h * 1024 + qt * 128: h * 1024 + (qt + 1) * 128],
                            K[:, kb * 512:(kb + 1) * 512], start=True, stop=True), "PE")
                    hp = pair * 2 + hpp
                    Pb = praw(upar, hpp)
                    P.wait("scalar", "PE", mm)
                    if kb == 0:
                        P.wait("scalar", "DV", st5["bias_done"])
                        P.wait("scalar", "DV", st5["sums_read"][hp])
                        P.wait("scalar", "PE", st5["pbuf_free"][upar])
                    a = P.op("scalar", lambda e, b=bank, P_=Pb, kb=kb, hp=hp: e.activation(
                        out=P_[:, kb * 512:(kb + 1) * 512], in_=b[:], func=AF.Exp,
                        scale=ISQ, bias=bias4[:, hp:hp + 1],
                        accum_out=sums4[:, hp * 8 + kb: hp * 8 + kb + 1]), "AC")
                    st5["sc_exp"][slot] = a
                    st5["last_exp"] = a
                    exp_last[hpp] = a
                    emitted += 1
                    want = (len(pending) * emitted) // nsc
                    while ci < want:
                        pending[ci]()
                        ci += 1
            while ci < len(pending):
                pending[ci]()
                ci += 1
            nrm = 0
            for hpp in range(2):
                hp = pair * 2 + hpp
                Pb = praw(upar, hpp)
                P.wait("vector", "AC", exp_last[hpp])
                v = P.op("vector", lambda e, hp=hp, qt=qt: e.reduce_sum(
                    tot4[:, hp:hp + 1], sums4[:, hp * 8: hp * 8 + qt + 1], axis=AX.X), "DV")
                st5["sums_read"][hp] = v
                P.wait("vector", "DV", v)
                v = P.op("vector", lambda e, hp=hp: e.reciprocal(
                    out=rcp4[:, hp:hp + 1], in_=tot4[:, hp:hp + 1]), "DV")
                P.wait("vector", "DV", v)
                nrm = P.op("vector", lambda e, P_=Pb, qt=qt, hp=hp: e.tensor_scalar_mul(
                    P_[:, 0:(qt + 1) * 512], P_[:, 0:(qt + 1) * 512], rcp4[:, hp:hp + 1]), "DV")
            st5["norm_done"][(qt, pair)] = nrm

        def trav_chunks(kvh, qt, pair, upar, V_sb):
            heads = (kvh * GRP + pair * 2, kvh * GRP + pair * 2 + 1)
            av_bank = pb[4 + upar]
            state = {"first": True}

            def mkT(kb, jj, lastT):
                def emit():
                    reg = st5["pt_rr"]
                    st5["pt_rr"] = (reg + 1) % 2
                    ptbank = pb[2 + reg]
                    if kb == 0 and jj == 0:
                        P.wait("tensor", "DV", st5["norm_done"].pop((qt, pair)))
                        P.wait("tensor", "AC", st5["av_evac"][upar])
                    P.wait("tensor", "AC", st5["pt_evac"][reg])
                    rf = st5["region_free"].pop(2 + reg, 0)
                    P.wait("tensor", "DV", rf)
                    for hpp in range(2):
                        mm = P.op("tensor",
                                  lambda e, b=ptbank, kb=kb, jj=jj, hpp=hpp: e.matmul(
                                      b[:, hpp * 128:hpp * 128 + 128],
                                      praw(upar, hpp)[:, kb * 512 + jj * 128:
                                                      kb * 512 + jj * 128 + 128],
                                      ident[:], start=True, stop=True),
                                  "PE" if hpp == 1 else None)
                    if lastT:
                        st5["pbuf_free"][upar] = mm
                    return reg, mm
                return emit

            def mkAV(kb, jj, last):
                def emit(treg_mm):
                    reg, mm = treg_mm
                    P.wait("scalar", "PE", mm)
                    P.wait("scalar", "PE", st5["ptsb_free"][reg])
                    aev = P.op("scalar", lambda e, b=pb[2 + reg], r=reg: e.copy(
                        out=ptsb[:, r * 256:(r + 1) * 256], in_=b[:, 0:256]), "AC")
                    st5["pt_evac"][reg] = aev
                    P.wait("tensor", "AC", aev)
                    mm = P.op("tensor", lambda e, b=av_bank, V=V_sb, kb=kb, jj=jj,
                              st=state["first"], sp=last, r=reg: e.matmul(
                        b[:, 0:256], V[:, (kb * 4 + jj) * 128:(kb * 4 + jj) * 128 + 128],
                        ptsb[:, r * 256:(r + 1) * 256], start=st, stop=sp), "PE")
                    st5["ptsb_free"][reg] = mm
                    state["first"] = False
                    if last:
                        st5["tr_last"] = mm
                        if qt == NT - 1 and pair == 1:
                            kv_consumed[kvh] = mm
                        P.wait("scalar", "PE", mm)
                        for hpp in range(2):
                            h = heads[hpp]
                            ae = P.op("scalar", lambda e, b=av_bank, hpp=hpp, h=h, qt=qt: e.copy(
                                out=qa_pool[:, h * 1024 + qt * 128: h * 1024 + (qt + 1) * 128],
                                in_=b[:, hpp * 128:hpp * 128 + 128]), "AC" if hpp == 1 else None)
                        st5["av_evac"][upar] = ae
                return emit

            n = (qt + 1) * 4
            parts = []
            i = 0
            for kb in range(qt + 1):
                for jj in range(4):
                    parts.append((mkT(kb, jj, i == n - 1), mkAV(kb, jj, i == n - 1)))
                    i += 1
            hold = {}

            def mk_step(idx, do_T, do_AV):
                def emit():
                    if do_T is not None:
                        hold[do_T] = parts[do_T][0]()
                    if do_AV is not None:
                        parts[do_AV][1](hold.pop(do_AV))
                return emit

            sched = [mk_step(0, 0, None)]
            for i in range(1, n):
                sched.append(mk_step(i, i, i - 1))
            sched.append(mk_step(n, None, n - 1))
            return sched

        REG = [0, 1, 2, 3, 6, 7]

        def prework_a(kvh, K_sb):
            P.wait("gpsimd", f"KV{kvh % 2}", kld[kvh])
            P.wait("gpsimd", "PE", st5["tr_last"])
            pk = P.op("gpsimd", lambda e, K=K_sb: e.tensor_tensor(
                out=ksink[:], in0=K, in1=K, op=ALU.mult), "PO")
            P.wait("tensor", "PO", pk)
            P.wait("tensor", "AC", st5["last_exp"])
            P.wait("tensor", "AC", max(st5["pt_evac"]))
            P.wait("tensor", "DV", st5["pb7_free"])
            kred = [0] * 8
            for kb in range(8):
                reg = REG[kb % 6]
                if kb >= 6:
                    P.wait("tensor", "DV", kred[kb - 6])
                mm = P.op("tensor", lambda e, b=pb[reg], kb=kb: e.matmul(
                    b[0:1, 0:512], onec[:], ksink[:, kb * 512:(kb + 1) * 512],
                    start=True, stop=True), "PE")
                P.wait("vector", "PE", mm)
                kred[kb] = P.op("vector", lambda e, kb=kb: e.reduce_max(
                    kmrow[0:1, kb:kb + 1], pb[REG[kb % 6]][0:1, 0:512], axis=AX.X), "DV")
            P.wait("vector", "DV", kred[7])
            km_v = P.op("vector", lambda e: e.reduce_max(
                kmrow[0:1, 8:9], kmrow[0:1, 0:8], axis=AX.X), "DV")
            P.wait("gpsimd", "PE", mm)
            pq = P.op("gpsimd", lambda e, kvh=kvh: e.tensor_tensor(
                out=ksink[:], in0=qa_pool[:, kvh * 4096:(kvh + 1) * 4096],
                in1=qa_pool[:, kvh * 4096:(kvh + 1) * 4096], op=ALU.mult), "PO")
            P.wait("tensor", "PO", pq)
            qred = [0] * 8
            for r in range(8):
                reg = REG[r % 6]
                P.wait("tensor", "DV", kred[r % 6] if r < 6 else qred[r - 6])
                if r < 2:
                    P.wait("tensor", "DV", kred[6 + r])
                mm = P.op("tensor", lambda e, b=pb[reg], r=r: e.matmul(
                    b[0:1, 0:512], onec[:], ksink[:, r * 512:(r + 1) * 512],
                    start=True, stop=True), "PE")
                P.wait("vector", "PE", mm)
                qred[r] = P.op("vector", lambda e, r=r: e.reduce_max(
                    kmrow[0:1, 16 + r:17 + r], pb[REG[r % 6]][0:1, 0:512], axis=AX.X), "DV")
            for i, reg in enumerate(REG):
                st5["region_free"][reg] = qred[i]
            st5["region_free"][0] = max(qred[6], st5["region_free"][0])
            st5["region_free"][1] = max(qred[7], st5["region_free"][1])
            return km_v, qred

        def prework_b(kvh, km_v, qred):
            chain_done = 0
            for hp in range(4):
                P.wait("vector", "DV", max(qred[2 * hp], qred[2 * hp + 1], km_v))
                v = P.op("vector", lambda e, hp=hp: e.tensor_max(
                    kmrow[0:1, 24 + hp:25 + hp], kmrow[0:1, 16 + 2 * hp:17 + 2 * hp],
                    kmrow[0:1, 17 + 2 * hp:18 + 2 * hp]), "DV")
                P.wait("vector", "DV", v)
                v = P.op("vector", lambda e, hp=hp: e.tensor_mul(
                    kmrow[0:1, 9 + hp:10 + hp], kmrow[0:1, 8:9],
                    kmrow[0:1, 24 + hp:25 + hp]), "DV")
                P.wait("scalar", "DV", v)
                aa = P.op("scalar", lambda e, hp=hp: e.activation(
                    out=kmrow[0:1, 28 + hp:29 + hp], in_=kmrow[0:1, 9 + hp:10 + hp],
                    func=AF.Sqrt, scale=1.2155), "AC")
                P.wait("vector", "AC", aa)
                v = P.op("vector", lambda e, hp=hp: e.tensor_scalar_mul(
                    kmrow[0:1, 9 + hp:10 + hp], kmrow[0:1, 28 + hp:29 + hp], -ISQ), "DV")
                P.wait("vector", "DV", v)
                v = P.op("vector", lambda e, hp=hp: e.tensor_copy(
                    out=trowb[0:1, 1 + hp:2 + hp], in_=kmrow[0:1, 9 + hp:10 + hp]), "DV")
                chain_done = v
            P.wait("tensor", "DV", chain_done)
            P.wait("tensor", "DV", st5["region_free"].pop(7, 0))
            for hp in range(4):
                mmb = P.op("tensor", lambda e, hp=hp: e.matmul(
                    pb[7][:, 2 + hp:3 + hp], oner[:], trowb[0:1, 1 + hp:2 + hp],
                    start=True, stop=True), "PE" if hp == 3 else None)
            P.wait("vector", "PE", mmb)
            P.wait("vector", "AC", st5["last_exp"])
            bc = P.op("vector", lambda e: e.tensor_copy(out=bias4[:], in_=pb[7][:, 2:6]), "DV")
            st5["pb7_free"] = bc
            st5["bias_done"] = bc

        uidx = 0
        for kvh in range(HKV):
            par = kvh % 2
            K_sb = xn_pool[:, par * 4096:(par + 1) * 4096]
            V_sb = xn_pool[:, 8192 + par * 4096: 8192 + (par + 1) * 4096]
            if kvh + 1 < HKV:
                kld[kvh + 1] = load_kv(kvh + 1, kv_consumed[kvh - 1] if kvh >= 1 else 0)
            km_v, qred = prework_a(kvh, K_sb)
            prework_b(kvh, km_v, qred)
            pending = []
            for qt in range(NT):
                for pair in range(2):
                    upar = uidx % 2
                    emit_unit(kvh, qt, pair, upar, K_sb, pending)
                    pending = trav_chunks(kvh, qt, pair, upar, V_sb)
                    uidx += 1
            for c in pending:
                c()
            pending = []
        return P.val("AC")

    att_done = phase5()
    P.barrier()

    if DEBUG_OUTPUTS:
        for h in range(HQ):
            P.op("sync", lambda e, h=h: e.dma_start(
                out=dbg_att[h], in_=qa_pool[:, h * 1024:(h + 1) * 1024]), "DBG", 16)
        P.barrier()

    if BUILD_PHASES <= 5:
        P.finalize()
        return nc, es

    def phase6():
        for s in range(4):
            P.wait("sync", f"OS{s}", P.val(f"OS{s}"))
        return emit_outpass(
            ngroups=8, stop_ic=31,
            wsrc=lambda oc, g: wo_h[oc, g],
            lhs_fn=lambda ic, t: qa_pool[:, ic * 1024 + t * 128: ic * 1024 + (t + 1) * 128],
            add_src=y1_d, store_dst=y_out)

    phase6()
    for s in range(4):
        P.wait("sync", f"OS{s}", P.val(f"OS{s}"))
    P.finalize()
    return nc, es


def _host_prep(inputs):
    x = np.asarray(inputs["L0_attn_hidden_states"], np.float32)
    wq = np.asarray(inputs["L0_attn_wq"], np.float32)
    wk = np.asarray(inputs["L0_attn_wk"], np.float32)
    wv = np.asarray(inputs["L0_attn_wv"], np.float32)
    wo = np.asarray(inputs["L0_attn_wo"], np.float32)
    worg = np.asarray(inputs["L0_ffn_W_org"], np.float32)
    ldown = np.asarray(inputs["L0_ffn_lora_down"], np.float32)
    lup = np.asarray(inputs["L0_ffn_lora_up"], np.float32)
    lkeys = np.asarray(inputs["L0_ffn_lora_keys"], np.float32)
    wqm = np.asarray(inputs["L0_ffn_Wq"], np.float32)

    wq_h = np.ascontiguousarray(
        wq.reshape(HQ, 128, 32, 128).transpose(0, 3, 2, 1).reshape(HQ, 128, D))
    wk_h = np.ascontiguousarray(
        wk.reshape(HKV, 128, 32, 128).transpose(0, 3, 2, 1).reshape(HKV, 128, D))
    wv_h = np.ascontiguousarray(
        wv.reshape(2, 512, 8, 4, 128).transpose(0, 2, 4, 3, 1).reshape(2, 8, 128, 2048))
    wo_h = np.ascontiguousarray(
        wo.reshape(8, 512, 8, 4, 128).transpose(0, 2, 4, 3, 1).reshape(8, 8, 128, 2048))
    worg_h = np.ascontiguousarray(
        worg.reshape(8, 512, 8, 4, 128).transpose(0, 2, 4, 3, 1).reshape(8, 8, 128, 2048))
    dmat = ldown.transpose(1, 0, 2).reshape(D, R)
    dmat_h = np.ascontiguousarray(
        dmat.reshape(32, 128, 4, 128).transpose(2, 1, 0, 3).reshape(4, 128, 32 * 128))
    umat = lup.reshape(R, D)
    umat_h = np.ascontiguousarray(
        umat.reshape(4, 128, 8, 512).transpose(2, 1, 0, 3).reshape(8, 128, 2048))
    wqm_h = np.ascontiguousarray(
        wqm.reshape(128, 32, 128).transpose(2, 1, 0).reshape(128, 32 * 128))
    keys_t = np.ascontiguousarray(lkeys.T)

    ident = np.eye(128, dtype=bfdt)
    onec = np.ones((128, 1), dtype=bfdt)
    oner = np.ones((1, 128), dtype=bfdt)
    one1 = np.ones((1, 1), dtype=bfdt)
    sel = np.zeros((NE, 512), dtype=np.float32)
    for rc in range(4):
        for p in range(128):
            sel[2 * rc + p // 64, rc * 128 + p] = 1.0
    sel = sel.astype(bfdt)

    inv_freq = 1.0 / (10000.0 ** (np.arange(0, DH, 2, dtype=np.float32) / DH))

    in_maps = []
    for c in range(8):
        b, j = c // 4, c % 4
        xr = x[b].reshape(NT, 4, 128, D)[:, j]
        x_core = np.ascontiguousarray(xr.reshape(TL, D))
        pos = (np.arange(NT)[:, None] * 512 + j * 128 +
               np.arange(128)[None, :]).reshape(TL).astype(np.float32)
        ang = pos[None, :] * inv_freq[:, None]
        ropec_a = np.empty((128, TL), np.float32)
        ropes_a = np.empty((128, TL), np.float32)
        ropec_a[0:64] = np.cos(ang); ropec_a[64:128] = np.cos(ang)
        ropes_a[0:64] = -np.sin(ang); ropes_a[64:128] = np.sin(ang)
        qi = np.arange(128)[:, None]
        ji = np.arange(512)[None, :]
        mask_a = np.where(j * 128 + qi >= ji, 0.0, NEG).astype(bfdt)
        in_maps.append({
            "x_in": x_core, "wq_h": wq_h, "wk_h": wk_h, "wv_h": wv_h,
            "wo_h": wo_h, "worg_h": worg_h, "dmat_h": dmat_h, "umat_h": umat_h,
            "wqm_h": wqm_h, "keys_t": keys_t, "rope_c": ropec_a, "rope_s": ropes_a,
            "mask_in": mask_a, "ident_in": ident, "onec_in": onec, "oner_in": oner,
            "one1_in": one1, "sel_in": sel,
        })
    return in_maps


def _assemble(results):
    y = np.empty((B, S, D), np.float32)
    for c in range(8):
        b, j = c // 4, c % 4
        y[b].reshape(NT, 4, 128, D)[:, j] = results[c]["y_out"].reshape(NT, 128, D)
    return y


def kernel(**inputs):
    nc, es = build_program()
    in_maps = _host_prep(inputs)
    try:
        res = run_bass_kernel_spmd(nc, in_maps, core_ids=list(range(8)))
    finally:
        es.close()
    return _assemble(res.results)


# version 09
# speedup vs baseline: 1.4376x; 1.1622x over previous
import math
import numpy as np
import ml_dtypes
import concourse.bass as bass
import concourse.mybir as mybir
from concourse.bass_utils import run_bass_kernel_spmd
import concourse.bass_utils as _bu

if not getattr(_bu, "_ldw_opt_patched", False):
    _orig_run_command = _bu.run_command

    def _run_command_ldw(cmd, **kw):
        cmd = [c.replace("--enable-ldw-opt=false", "--enable-ldw-opt=true")
               if isinstance(c, str) else c for c in cmd]
        return _orig_run_command(cmd, **kw)

    _bu.run_command = _run_command_ldw
    _bu._ldw_opt_patched = True

F32 = mybir.dt.float32
BF16 = mybir.dt.bfloat16
AF = mybir.ActivationFunctionType
ALU = mybir.AluOpType
AX = mybir.AxisListType
bfdt = ml_dtypes.bfloat16

B, S, D = 2, 4096, 4096
HQ, HKV, DH = 32, 8, 128
GRP = HQ // HKV
NT = 8
TL = 1024
R = 512
NE, KD = 8, 128
EPS = 1e-5
ISQ = 1.0 / math.sqrt(DH)
NEG = -1e9

DEBUG_OUTPUTS = False
BUILD_PHASES = 99
PHASE2_CUT = "full"


class Prog:
    ENGS = ("sync", "gpsimd", "tensor", "vector", "scalar")
    ENG_SEMS = ("PE", "DV", "AC", "PO")
    DMA_SEMS = ("CONST", "XL0", "XL1", "XL2", "XL3", "KV0", "KV1", "MN", "MS",
                "KS0", "KS1", "VS0", "VS1", "OS0", "OS1", "OS2", "OS3",
                "WP0", "WP1", "WS0", "WS1", "WS2", "WS3", "WM", "DBG")
    CC_SEMS = ("CCM", "CCK")
    ALL_SEMS = ENG_SEMS + DMA_SEMS + CC_SEMS

    def __init__(self, nc, es):
        self.nc = nc
        self.q = {e: [] for e in self.ENGS}
        self.sem = {}
        self.cnt = {}
        self._es = es

    def mksem(self, name):
        self.sem[name] = self._es.enter_context(self.nc.semaphore(name))
        self.cnt[name] = 0

    def wait(self, eng, sem, val):
        if val <= 0:
            return
        h = self.sem[sem]
        self.q[eng].append(lambda e, h=h, v=val: e.wait_ge(h, v))

    def op(self, eng, fn, inc=None, n=1):
        if inc is None:
            self.q[eng].append(fn)
            return None
        self.cnt[inc] += n
        v = self.cnt[inc]
        h = self.sem[inc]
        self.q[eng].append(lambda e, fn=fn, h=h, n=n: fn(e).then_inc(h, n))
        return v

    def val(self, sem):
        return self.cnt[sem]

    def barrier(self):
        marks = [(s, self.val(s)) for s in self.ENG_SEMS + self.DMA_SEMS]
        for eng in self.ENGS:
            for sem, v in marks:
                self.wait(eng, sem, v)

    def finalize(self):
        with self.nc.Block() as block:
            for eng in self.ENGS:
                ops = self.q[eng]

                def run(e, ops=ops):
                    for f in ops:
                        f(e)

                getattr(block, eng)(run)


def build_program():
    from contextlib import ExitStack

    nc = bass.Bass("TRN2", target_bir_lowering=False, debug=False)
    es = ExitStack()

    def dram(name, shape, dtype, kind):
        return nc.dram_tensor(name, shape, dtype, kind=kind).ap()

    x_in = dram("x_in", [TL, D], F32, "ExternalInput")
    wq_h = dram("wq_h", [HQ, 128, D], F32, "ExternalInput")
    wk_h = dram("wk_h", [HKV, 128, D], F32, "ExternalInput")
    wv_h = dram("wv_h", [2, 8, 128, 2048], F32, "ExternalInput")
    wo_h = dram("wo_h", [8, 8, 128, 2048], F32, "ExternalInput")
    worg_h = dram("worg_h", [8, 8, 128, 2048], F32, "ExternalInput")
    dmat_h = dram("dmat_h", [4, 128, 32 * 128], F32, "ExternalInput")
    umat_h = dram("umat_h", [8, 128, 2048], F32, "ExternalInput")
    wqm_h = dram("wqm_h", [128, 32 * 128], F32, "ExternalInput")
    keys_t = dram("keys_t", [KD, NE], F32, "ExternalInput")
    rope_c = dram("rope_c", [128, TL], F32, "ExternalInput")
    rope_s = dram("rope_s", [128, TL], F32, "ExternalInput")
    mask_in = dram("mask_in", [128, 512], BF16, "ExternalInput")
    ident_in = dram("ident_in", [128, 128], BF16, "ExternalInput")
    onec_in = dram("onec_in", [128, 1], BF16, "ExternalInput")
    oner_in = dram("oner_in", [1, 128], BF16, "ExternalInput")
    one1_in = dram("one1_in", [1, 1], BF16, "ExternalInput")
    sel_in = dram("sel_in", [NE, 512], BF16, "ExternalInput")

    y_out = dram("y_out", [TL, D], F32, "ExternalOutput")

    kv_ci = dram("kv_ci", [2048, 1024], BF16, "Internal")
    kv_co = dram("kv_co", [4 * 2048, 1024], BF16, "Internal")
    mn_ci = dram("mn_ci", [128, 32], F32, "Internal")
    mn_co = dram("mn_co", [128, 32], F32, "Internal")
    y1_d = dram("y1_d", [TL, D], F32, "Internal")
    RG = [[0, 1, 2, 3], [4, 5, 6, 7]]

    if DEBUG_OUTPUTS:
        dbg_xn = dram("dbg_xn", [32, 128, TL], BF16, "ExternalOutput")
        dbg_coef = dram("dbg_coef", [128, 4], F32, "ExternalOutput")
        dbg_q = dram("dbg_q", [HQ, 128, TL], BF16, "ExternalOutput")
        dbg_kv = dram("dbg_kv", [4 * 2048, 1024], BF16, "ExternalOutput")
        dbg_att = dram("dbg_att", [HQ, 128, TL], BF16, "ExternalOutput")
        dbg_y1 = dram("dbg_y1", [TL, D], F32, "ExternalOutput")

    sb = lambda name, shape, dt: es.enter_context(nc.sbuf_tensor(name, shape, dt))
    xn_pool = sb("xn_pool", [128, 32768], BF16)
    qa_pool = sb("qa_pool", [128, 32768], BF16)
    f32_pool = sb("f32_pool", [128, 8192], F32)
    wpool = sb("wpool", [128, 8192], BF16)
    ptsb = sb("ptsb", [128, 1024], BF16)
    stage = sb("stage", [128, 1536], BF16)
    ident = sb("ident", [128, 128], BF16)
    onec = sb("onec", [128, 1], BF16)
    oner = sb("oner", [1, 128], BF16)
    one1 = sb("one1", [1, 1], BF16)
    mask = sb("mask", [128, 512], BF16)
    ropec = sb("ropec", [128, TL], F32)
    ropes = sb("ropes", [128, TL], F32)
    selm = sb("selm", [NE, 512], BF16)
    keyst = sb("keyst", [KD, NE], BF16)
    ra = f32_pool[:, 0:512]
    rb = f32_pool[:, 512:1024]
    rt1 = f32_pool[:, 1024:1536]
    rt2 = f32_pool[:, 1536:2048]
    colsum = sb("colsum", [128, 32], F32)
    meansb = sb("meansb", [128, 32], F32)
    meanbf = sb("meanbf", [128, 32], BF16)
    stat = sb("stat", [128, 16], F32)
    sums4 = sb("sums4", [128, 32], F32)
    tot4 = sb("tot4", [128, 4], F32)
    rcp4 = sb("rcp4", [128, 4], F32)
    bias4 = sb("bias4", [128, 4], F32)
    scol = sb("scol", [128, 4], F32)
    trowb = sb("trowb", [1, 128], BF16)
    kmrow = sb("kmrow", [1, 32], F32)

    pb = [es.enter_context(nc.psum_tensor(f"pb{i}", [128, 512], F32)) for i in range(8)]

    P = Prog(nc, es)
    for s in Prog.ALL_SEMS:
        P.mksem(s)

    for dst, src in ((ident, ident_in), (onec, onec_in), (oner, oner_in),
                     (one1, one1_in), (mask, mask_in), (ropec, rope_c),
                     (ropes, rope_s), (selm, sel_in)):
        P.op("sync", lambda e, d=dst, s=src: e.dma_start(out=d[:], in_=s[:]), "CONST", 16)
    P.op("gpsimd", lambda e: e.dma_start(out=keyst[:], in_=keys_t[:]), "WM", 16)
    P.barrier()

    def phase1():
        tp_free = [0, 0]
        norm_done = [0] * NT
        stage_free = [0, 0]
        half_done = [0, 0]
        xld = [0] * NT
        for qt in range(NT):
            par = qt % 2
            xbuf = f32_pool[:, 4096:8192]
            st = lambda i, par=par: stat[:, par * 8 + i: par * 8 + i + 1]
            if qt >= 1:
                P.wait("sync", "AC", norm_done[qt - 1])
            xld[qt] = P.op("sync", lambda e, d=xbuf, q=qt: e.dma_start(
                out=d, in_=x_in[q * 128:(q + 1) * 128, :]), "XL0", 16)
            P.wait("vector", "XL0", xld[qt])
            sink = qa_pool[:, 8192:12288]
            P.wait("scalar", "XL0", xld[qt])
            a0 = P.op("scalar", lambda e, x=xbuf, st=st, s=sink: e.activation(
                out=s, in_=x, func=AF.Square, accum_out=st(2)), "AC")
            v = P.op("vector", lambda e, x=xbuf, st=st: e.reduce_sum(
                st(0), x, axis=AX.X, negate=True), "DV")
            P.wait("vector", "DV", v)
            v = P.op("vector", lambda e, st=st: e.tensor_scalar_mul(st(1), st(0), 1.0 / D), "DV")
            P.wait("vector", "DV", v)
            v = P.op("vector", lambda e, st=st: e.tensor_mul(st(3), st(1), st(1)), "DV")
            P.wait("vector", "AC", a0)
            P.wait("vector", "DV", v)
            v = P.op("vector", lambda e, st=st: e.tensor_scalar(
                st(4), st(2), 1.0 / D, EPS, ALU.mult, ALU.add), "DV")
            P.wait("vector", "DV", v)
            v = P.op("vector", lambda e, st=st: e.tensor_sub(st(5), st(4), st(3)), "DV")
            P.wait("scalar", "DV", v)
            a = P.op("scalar", lambda e, st=st: e.activation(out=st(6), in_=st(5), func=AF.Sqrt), "AC")
            P.wait("vector", "AC", a)
            v = P.op("vector", lambda e, st=st: e.reciprocal(out=st(7), in_=st(6)), "DV")
            P.wait("vector", "DV", v)
            v = P.op("vector", lambda e, st=st: e.tensor_mul(st(3), st(1), st(7)), "DV")
            xst = qa_pool[:, par * 4096:(par + 1) * 4096]
            P.wait("scalar", "DV", v)
            P.wait("scalar", "PE", stage_free[par])
            a = P.op("scalar", lambda e, x=xbuf, o=xst, st=st: e.activation(
                out=o, in_=x, func=AF.Identity, scale=st(7), bias=st(3)), "AC")
            norm_done[qt] = a
            for g in range(8):
                bank = pb[g % 2]
                P.wait("tensor", "AC", a)
                P.wait("tensor", "DV", tp_free[g % 2])
                for u in range(4):
                    dc = g * 4 + u
                    pe_v = P.op("tensor", lambda e, b=bank, u=u, x=xst, dc=dc: e.matmul(
                        b[:, u * 128:(u + 1) * 128], x[:, dc * 128:(dc + 1) * 128],
                        ident[:], start=True, stop=True), "PE" if u == 3 else None)
                P.wait("vector", "PE", pe_v)
                for u in range(4):
                    dc = g * 4 + u
                    dst = xn_pool[:, dc * 1024 + qt * 128: dc * 1024 + (qt + 1) * 128]
                    ev = P.op("vector", lambda e, b=bank, u=u, d=dst: e.tensor_copy(
                        out=d, in_=b[:, u * 128:(u + 1) * 128]), "DV" if u == 3 else None)
                tp_free[g % 2] = ev
            stage_free[par] = pe_v
            if qt == 3 or qt == 7:
                half_done[qt // 4] = max(tp_free)
        P.wait("vector", "DV", max(tp_free[0], tp_free[1]))
        for dc in range(32):
            cv = P.op("vector", lambda e, dc=dc: e.reduce_sum(
                colsum[:, dc:dc + 1], xn_pool[:, dc * 1024:(dc + 1) * 1024], axis=AX.X),
                "DV" if dc == 31 else None)
        return cv, half_done

    colsum_dv, xn_half = phase1()

    if DEBUG_OUTPUTS:
        P.wait("sync", "DV", colsum_dv)
        for dc in range(32):
            P.op("sync", lambda e, dc=dc: e.dma_start(
                out=dbg_xn[dc], in_=xn_pool[:, dc * 1024:(dc + 1) * 1024]), "DBG", 16)

    if BUILD_PHASES <= 1:
        P.finalize()
        return nc, es

    rope_state = {"p1": 0, "v4": 0, "stage_st": {}}

    def rope_evac(bank, th, dst_bf, pe_v, extra_dst_wait=None):
        C = ropec[:, th * 512:(th + 1) * 512]
        Ssg = ropes[:, th * 512:(th + 1) * 512]
        P.wait("vector", "PE", pe_v)
        P.wait("vector", "DV", rope_state["p1"])
        v1 = P.op("vector", lambda e, b=bank: e.tensor_copy(out=rb[64:128, :], in_=b[0:64, :]), "DV")
        v2 = P.op("vector", lambda e, b=bank: e.tensor_copy(out=rb[0:64, :], in_=b[64:128, :]), "DV")
        P.wait("scalar", "PE", pe_v)
        P.wait("scalar", "DV", max(v2, rope_state["v4"]))
        a1 = P.op("scalar", lambda e, b=bank: e.copy(out=ra[:], in_=b[:]), "AC")
        P.wait("vector", "AC", a1)
        v3 = P.op("vector", lambda e, C=C: e.tensor_mul(rt1[:], ra[:], C), "DV")
        P.wait("vector", "DV", v2)
        p1 = P.op("vector", lambda e, Ssg=Ssg: e.tensor_mul(rt2[:], rb[:], Ssg), "DV")
        rope_state["p1"] = p1
        P.wait("vector", "DV", max(v3, p1))
        if extra_dst_wait is not None:
            sem, val = extra_dst_wait
            P.wait("vector", sem, val)
        v4 = P.op("vector", lambda e, d=dst_bf: e.tensor_add(d, rt1[:], rt2[:]), "DV")
        rope_state["v4"] = v4
        return v2, v4

    def phase2():
        wfree = [0, 0]
        bank_free = [0] * 4
        kst_by_th = {0: 0, 1: 0}
        for kvh in range(HKV):
            par = kvh % 2
            wbuf = wpool[:, par * 4096:(par + 1) * 4096]
            P.wait("gpsimd", "PE", wfree[par])
            wld = P.op("gpsimd", lambda e, d=wbuf, k=kvh: e.dma_start(
                out=d, in_=wk_h[k]), f"WP{par}", 16)
            P.wait("tensor", f"WP{par}", wld)
            for th in range(2):
                bslot = (kvh * 2 + th) % 4
                bank = pb[4 + bslot]
                P.wait("tensor", "DV", max(bank_free[bslot], xn_half[th]))
                for ic in range(32):
                    mm = P.op("tensor", lambda e, b=bank, w=wbuf, ic=ic, th=th: e.matmul(
                        b[:], w[:, ic * 128:(ic + 1) * 128],
                        xn_pool[:, ic * 1024 + th * 512: ic * 1024 + (th + 1) * 512],
                        start=(ic == 0), stop=(ic == 31)), "PE" if ic == 31 else None)
                if th == 1:
                    wfree[par] = mm
                dst = stage[:, th * 512:(th + 1) * 512]
                if PHASE2_CUT == "kmm":
                    P.wait("vector", "PE", mm)
                    done = P.op("vector", lambda e, b=bank, d=dst: e.tensor_copy(out=d, in_=b[:]), "DV")
                    bank_free[bslot] = done
                    continue
                cons, done = rope_evac(bank, th, dst, mm,
                                       extra_dst_wait=(f"KS{th}", kst_by_th[th]))
                bank_free[bslot] = done
                if PHASE2_CUT == "krope":
                    continue
                P.wait("sync", "DV", done)
                sv = P.op("sync", lambda e, k=kvh, th=th, d=dst: e.dma_start(
                    out=kv_ci[k * 128:(k + 1) * 128, th * 512:(th + 1) * 512], in_=d), f"KS{th}", 16)
                kst_by_th[th] = sv
        if PHASE2_CUT == "k":
            return 0
        P.wait("tensor", "DV", max(bank_free))
        P.wait("scalar", "KS0", P.val("KS0"))
        P.wait("scalar", "KS1", P.val("KS1"))
        vslot_free = [P.val("PE")] * 2
        vbank_free = [0] * NT
        vst_slot = [0, 0]
        for dvh in range(2):
            for g in range(8):
                slot = g % 2
                gbuf = wpool[:, slot * 2048:(slot + 1) * 2048]
                P.wait("gpsimd", "PE", vslot_free[slot])
                wv = P.op("gpsimd", lambda e, d=gbuf, dvh=dvh, g=g: e.dma_start(
                    out=d, in_=wv_h[dvh, g]), f"WS{slot}", 16)
                P.wait("tensor", f"WS{slot}", wv)
                for gi in range(4):
                    ic = g * 4 + gi
                    for t in range(NT):
                        if ic == 0:
                            P.wait("tensor", "AC", vbank_free[t])
                        mm = P.op("tensor", lambda e, b=pb[t], ic=ic, t=t, w=gbuf, gi=gi: e.matmul(
                            b[:], xn_pool[:, ic * 1024 + t * 128: ic * 1024 + (t + 1) * 128],
                            w[:, gi * 512:(gi + 1) * 512],
                            start=(ic == 0), stop=(ic == 31)), "PE" if t == NT - 1 else None)
                vslot_free[slot] = mm
            for t in range(NT):
                sl = t % 2
                vdst = stage[:, sl * 512:(sl + 1) * 512]
                P.wait("scalar", "PE", mm)
                P.wait("scalar", f"VS{sl}", vst_slot[sl])
                av = P.op("scalar", lambda e, b=pb[t], d=vdst: e.copy(out=d, in_=b[:]), "AC")
                vbank_free[t] = av
                P.wait("sync", "AC", av)
                vst_slot[sl] = P.op("sync", lambda e, t=t, dvh=dvh, d=vdst: e.dma_start(
                    out=kv_ci[1024 + t * 128:1024 + (t + 1) * 128, dvh * 512:(dvh + 1) * 512],
                    in_=d), f"VS{sl}", 16)
        if PHASE2_CUT == "kv":
            return 0, 0
        P.wait("sync", "DV", colsum_dv)
        stv = P.op("sync", lambda e: e.dma_start(out=mn_ci[:], in_=colsum[:]), "MS", 16)
        P.wait("gpsimd", "MS", stv)
        ar = P.op("gpsimd", lambda e: e.collective_compute(
            "AllReduce", ALU.add, RG, [mn_ci[:]], [mn_co[:]]), "CCM", 1)
        P.wait("gpsimd", "KS0", P.val("KS0"))
        P.wait("gpsimd", "KS1", P.val("KS1"))
        P.wait("gpsimd", "VS0", P.val("VS0"))
        P.wait("gpsimd", "VS1", P.val("VS1"))
        ag = P.op("gpsimd", lambda e: e.collective_compute(
            "AllGather", ALU.bypass, RG, [kv_ci[:]], [kv_co[:]]), "CCK", 1)
        return ag, ar

    kv_ag, mean_ar = phase2()
    P.barrier()

    if BUILD_PHASES <= 2:
        P.finalize()
        return nc, es

    def phase3():
        wfree = [0, 0]
        bank_free = [0] * 4
        for h in range(HQ):
            par = h % 2
            wbuf = wpool[:, par * 4096:(par + 1) * 4096]
            P.wait("gpsimd", "PE", wfree[par])
            wld = P.op("gpsimd", lambda e, d=wbuf, h=h: e.dma_start(
                out=d, in_=wq_h[h]), f"WP{par}", 16)
            P.wait("tensor", f"WP{par}", wld)
            for th in range(2):
                bslot = (h * 2 + th) % 4
                bank = pb[4 + bslot]
                P.wait("tensor", "DV", bank_free[bslot])
                for ic in range(32):
                    mm = P.op("tensor", lambda e, b=bank, w=wbuf, ic=ic, th=th: e.matmul(
                        b[:], w[:, ic * 128:(ic + 1) * 128],
                        xn_pool[:, ic * 1024 + th * 512: ic * 1024 + (th + 1) * 512],
                        start=(ic == 0), stop=(ic == 31)), "PE" if ic == 31 else None)
                if th == 1:
                    wfree[par] = mm
                dst = qa_pool[:, h * 1024 + th * 512: h * 1024 + (th + 1) * 512]
                cons, done = rope_evac(bank, th, dst, mm)
                bank_free[bslot] = done
        return done

    q_done = phase3()
    P.barrier()

    if DEBUG_OUTPUTS:
        for h in range(HQ):
            P.op("sync", lambda e, h=h: e.dma_start(
                out=dbg_q[h], in_=qa_pool[:, h * 1024:(h + 1) * 1024]), "DBG", 16)
        P.wait("sync", "CCK", kv_ag)
        P.op("sync", lambda e: e.dma_start(out=dbg_kv[:], in_=kv_co[:]), "DBG", 16)
        P.barrier()

    if BUILD_PHASES <= 3:
        P.finalize()
        return nc, es

    def phase_coef():
        P.wait("sync", "CCM", mean_ar)
        mld = P.op("sync", lambda e: e.dma_start(out=meansb[:], in_=mn_co[:]), "MN", 16)
        P.wait("vector", "MN", mld)
        v = P.op("vector", lambda e: e.tensor_scalar_mul(meansb[:], meansb[:], 1.0 / S), "DV")
        P.wait("vector", "DV", v)
        v = P.op("vector", lambda e: e.tensor_copy(out=meanbf[:], in_=meansb[:]), "DV")
        wld = P.op("gpsimd", lambda e: e.dma_start(out=wpool[:, 0:4096], in_=wqm_h[:]), "WM", 16)
        P.wait("tensor", "DV", v)
        P.wait("tensor", "WM", wld)
        for ic in range(32):
            mm = P.op("tensor", lambda e, ic=ic: e.matmul(
                pb[7][0:1, 0:128], meanbf[:, ic:ic + 1],
                wpool[:, ic * 128:(ic + 1) * 128], start=(ic == 0), stop=(ic == 31)),
                "PE" if ic == 31 else None)
        P.wait("scalar", "PE", mm)
        a = P.op("scalar", lambda e: e.copy(out=trowb[:], in_=pb[7][0:1, 0:128]), "AC")
        P.wait("tensor", "AC", a)
        mm = P.op("tensor", lambda e: e.matmul(pb[6][:, 0:1], trowb[:], one1[:],
                                               start=True, stop=True), "PE")
        P.wait("vector", "PE", mm)
        v = P.op("vector", lambda e: e.tensor_copy(out=meanbf[:, 0:1], in_=pb[6][:, 0:1]), "DV")
        P.wait("tensor", "DV", v)
        mm = P.op("tensor", lambda e: e.matmul(pb[6][0:1, 16:24], meanbf[:, 0:1], keyst[:],
                                               start=True, stop=True), "PE")
        P.wait("vector", "PE", mm)
        v = P.op("vector", lambda e: e.tensor_scalar_mul(
            kmrow[0:1, 0:8], pb[6][0:1, 16:24], 1.0 / math.sqrt(KD)), "DV")
        P.wait("vector", "DV", v)
        v = P.op("vector", lambda e: e.reduce_max(kmrow[0:1, 8:9], kmrow[0:1, 0:8],
                                                  axis=AX.X, negate=True), "DV")
        P.wait("scalar", "DV", v)
        a = P.op("scalar", lambda e: e.activation(out=kmrow[0:1, 10:18], in_=kmrow[0:1, 0:8],
                                                  func=AF.Exp, bias=kmrow[0:1, 8:9], scale=1.0,
                                                  accum_out=kmrow[0:1, 9:10]), "AC")
        P.wait("vector", "AC", a)
        v = P.op("vector", lambda e: e.reciprocal(out=kmrow[0:1, 8:9], in_=kmrow[0:1, 9:10]), "DV")
        P.wait("vector", "DV", v)
        v = P.op("vector", lambda e: e.tensor_scalar_mul(
            kmrow[0:1, 10:18], kmrow[0:1, 10:18], kmrow[0:1, 8:9]), "DV")
        P.wait("vector", "DV", v)
        v = P.op("vector", lambda e: e.tensor_copy(out=trowb[0:1, 0:8], in_=kmrow[0:1, 10:18]), "DV")
        P.wait("tensor", "DV", v)
        mm = P.op("tensor", lambda e: e.matmul(pb[6][0:8, 8:9], trowb[0:1, 0:8], one1[:],
                                               start=True, stop=True), "PE")
        P.wait("vector", "PE", mm)
        v = P.op("vector", lambda e: e.tensor_copy(out=meanbf[0:8, 1:2], in_=pb[6][0:8, 8:9]), "DV")
        P.wait("tensor", "DV", v)
        for rc in range(4):
            mm = P.op("tensor", lambda e, rc=rc: e.matmul(
                pb[6][:, 2 + rc:3 + rc], selm[:, rc * 128:(rc + 1) * 128], meanbf[0:8, 1:2],
                start=True, stop=True), "PE" if rc == 3 else None)
        P.wait("vector", "PE", mm)
        v = P.op("vector", lambda e: e.tensor_copy(out=scol[:], in_=pb[6][:, 2:6]), "DV")
        return v

    coef_done = phase_coef()
    P.barrier()

    if DEBUG_OUTPUTS:
        P.op("sync", lambda e: e.dma_start(out=dbg_coef[:], in_=scol[:]), "DBG", 16)
        P.barrier()

    def emit_outpass(ngroups, stop_ic, wsrc, lhs_fn, add_src, store_dst,
                     init_bank_ev=None, init_slot_free=0, extra_mm_wait=None):
        wslot_free = [init_slot_free, init_slot_free]
        bank_ev = [0] * NT
        if init_bank_ev:
            for kk, vv in init_bank_ev.items():
                bank_ev[kk] = vv
        xl_free = [0] * 4
        out_free = [0] * 4
        xv = [0] * NT
        last_sv = 0
        def stage_load(oc, t):
            sl = t % 4
            xs = f32_pool[:, sl * 512:(sl + 1) * 512]
            P.wait("sync", "DV", xl_free[sl])
            xv[t] = P.op("sync", lambda e, d=xs, t=t, oc=oc, a=add_src: e.dma_start(
                out=d, in_=a[t * 128:(t + 1) * 128, oc * 512:(oc + 1) * 512]), f"XL{sl}", 16)

        for oc in range(8):
            for t in range(4):
                stage_load(oc, t)
            for g in range(ngroups):
                slot = g % 2
                gbuf = wpool[:, slot * 2048:(slot + 1) * 2048]
                P.wait("gpsimd", "PE", wslot_free[slot])
                wv = P.op("gpsimd", lambda e, d=gbuf, oc=oc, g=g: e.dma_start(
                    out=d, in_=wsrc(oc, g)), f"WS{slot}", 16)
                P.wait("tensor", f"WS{slot}", wv)
                for gi in range(4):
                    ic = g * 4 + gi
                    if ic > stop_ic:
                        break
                    if extra_mm_wait is not None and ic == extra_mm_wait[2] * 4:
                        if oc == 0:
                            P.wait("tensor", extra_mm_wait[0], extra_mm_wait[1])
                    for t in range(NT):
                        if ic == 0:
                            P.wait("tensor", "DV", bank_ev[t])
                        mm = P.op("tensor",
                                  lambda e, b=pb[t], l=lhs_fn(ic, t), w=gbuf, gi=gi, ic=ic: e.matmul(
                                      b[:], l, w[:, gi * 512:(gi + 1) * 512],
                                      start=(ic == 0), stop=(ic == stop_ic)),
                                  "PE" if t == NT - 1 else None)
                wslot_free[slot] = mm
            for t in range(NT):
                sl = t % 4
                xs = f32_pool[:, sl * 512:(sl + 1) * 512]
                od = f32_pool[:, 4096 + sl * 512: 4096 + (sl + 1) * 512]
                P.wait("vector", "PE", mm)
                P.wait("vector", f"XL{sl}", xv[t])
                P.wait("vector", f"OS{sl}", out_free[sl])
                ev = P.op("vector", lambda e, b=pb[t], x=xs, o=od: e.tensor_add(o, b[:], x), "DV")
                xl_free[sl] = ev
                bank_ev[t] = ev
                P.wait("sync", "DV", ev)
                sv = P.op("sync", lambda e, o=od, t=t, oc=oc, s=store_dst: e.dma_start(
                    out=s[t * 128:(t + 1) * 128, oc * 512:(oc + 1) * 512], in_=o), f"OS{sl}", 16)
                out_free[sl] = sv
                last_sv = sv
                if t + 4 < NT:
                    stage_load(oc, t + 4)
        return last_sv

    def phase4():
        h_sb = wpool[:, 4096:8192]
        hfree = [0, 0]
        last_rc_mm = 0
        for rc in range(4):
            P.wait("gpsimd", "PE", last_rc_mm)
            wld = P.op("gpsimd", lambda e, rc=rc: e.dma_start(
                out=wpool[:, 0:4096], in_=dmat_h[rc]), "WP0", 16)
            P.wait("tensor", "WP0", wld)
            for th in range(2):
                bank = pb[6 + th]
                P.wait("tensor", "DV", hfree[th])
                for ic in range(32):
                    mm = P.op("tensor", lambda e, b=bank, ic=ic, th=th: e.matmul(
                        b[:], wpool[:, ic * 128:(ic + 1) * 128],
                        xn_pool[:, ic * 1024 + th * 512: ic * 1024 + (th + 1) * 512],
                        start=(ic == 0), stop=(ic == 31)), "PE" if ic == 31 else None)
                last_rc_mm = mm
                P.wait("vector", "PE", mm)
                hv = P.op("vector", lambda e, b=bank, rc=rc, th=th: e.tensor_scalar_mul(
                    h_sb[:, rc * 1024 + th * 512: rc * 1024 + (th + 1) * 512],
                    b[:], scol[:, rc:rc + 1]), "DV")
                hfree[th] = hv
        h_done = hv
        return emit_outpass(
            ngroups=9, stop_ic=35,
            wsrc=lambda oc, g: worg_h[oc, g] if g < 8 else umat_h[oc],
            lhs_fn=lambda ic, t: (xn_pool[:, ic * 1024 + t * 128: ic * 1024 + (t + 1) * 128]
                                  if ic < 32 else
                                  h_sb[:, (ic - 32) * 1024 + t * 128: (ic - 32) * 1024 + (t + 1) * 128]),
            add_src=x_in, store_dst=y1_d,
            init_bank_ev={6: hfree[0], 7: hfree[1]},
            init_slot_free=last_rc_mm,
            extra_mm_wait=("DV", h_done, 8))

    y1_last = phase4()
    P.barrier()

    if DEBUG_OUTPUTS:
        P.op("sync", lambda e: e.dma_start(out=dbg_y1[:], in_=y1_d[:]), "DBG", 16)
        P.barrier()

    if BUILD_PHASES <= 4:
        P.finalize()
        return nc, es

    def phase5():
        ksink = xn_pool[:, 16384:16384 + 4096]
        P.wait("sync", "CCK", kv_ag)
        kv_consumed = [0] * HKV

        def load_kv(kvh, wait_pe_val):
            par = kvh % 2
            P.wait("sync", "PE", wait_pe_val)
            last = 0
            for jj in range(4):
                kin = kv_co[jj * 2048 + kvh * 128: jj * 2048 + (kvh + 1) * 128, :]
                kout = xn_pool[:, par * 4096:(par + 1) * 4096].rearrange(
                    "p (kb j i) -> p kb j i", kb=8, j=4)[:, :, jj, :]
                last = P.op("sync", lambda e, o=kout,
                            i=kin.rearrange("p (kb i) -> p kb i", kb=8):
                            e.dma_start(out=o, in_=i), f"KV{par}", 16)
                vin = kv_co[jj * 2048 + 1024: jj * 2048 + 2048, kvh * 128:(kvh + 1) * 128]
                vout = xn_pool[:, 8192 + par * 4096: 8192 + (par + 1) * 4096].rearrange(
                    "p (kb j dv) -> p kb j dv", kb=8, j=4)[:, :, jj, :]
                last = P.op("sync", lambda e, o=vout,
                            i=vin.rearrange("(kb p) dv -> p kb dv", p=128):
                            e.dma_start(out=o, in_=i), f"KV{par}", 16)
            return last

        kld = [0] * HKV
        kld[0] = load_kv(0, 0)

        SCB = [0, 1, 6, 7]
        st5 = {
            "sc_rr": 0,
            "sc_exp": [0, 0, 0, 0],
            "sums_read": [0, 0, 0, 0],
            "last_exp": 0,
            "pt_evac": [("AC", 0), ("DV", 0)],
            "ptsb_free": [0, 0],
            "pt_rr": 0,
            "av_evac": [0, 0],
            "tr_last": 0,
            "pb7_free": 0,
            "region_free": {},
            "pbuf_free": [0, 0],
            "norm_done": {},
        }

        def praw(upar, hpp):
            off = 16384 + upar * 8192 + hpp * 4096
            return xn_pool[:, off:off + 4096]

        def emit_unit(kvh, qt, pair, upar, K_sb, pending):
            heads = (kvh * GRP + pair * 2, kvh * GRP + pair * 2 + 1)
            nsc = 2 * (qt + 1)
            ci = 0
            emitted = 0
            rcpv = [0, 0]
            exp_last = [0, 0]
            for kb in range(qt + 1):
                for hpp in range(2):
                    h = heads[hpp]
                    slot = st5["sc_rr"]
                    st5["sc_rr"] = (slot + 1) % 4
                    bank = pb[SCB[slot]]
                    P.wait("tensor", "AC", st5["sc_exp"][slot])
                    rf = st5["region_free"].pop(SCB[slot], 0)
                    P.wait("tensor", "DV", rf)
                    if kb == qt:
                        P.op("tensor", lambda e, b=bank, h=h, qt=qt, kb=kb, K=K_sb: e.matmul(
                            b[:], qa_pool[:, h * 1024 + qt * 128: h * 1024 + (qt + 1) * 128],
                            K[:, kb * 512:(kb + 1) * 512], start=True, stop=False))
                        mm = P.op("tensor", lambda e, b=bank: e.matmul(
                            b[:], ident[:], mask[:], start=False, stop=True), "PE")
                    else:
                        mm = P.op("tensor", lambda e, b=bank, h=h, qt=qt, kb=kb, K=K_sb: e.matmul(
                            b[:], qa_pool[:, h * 1024 + qt * 128: h * 1024 + (qt + 1) * 128],
                            K[:, kb * 512:(kb + 1) * 512], start=True, stop=True), "PE")
                    hp = pair * 2 + hpp
                    Pb = praw(upar, hpp)
                    P.wait("scalar", "PE", mm)
                    if kb == 0:
                        P.wait("scalar", "DV", st5["bias_done"])
                        P.wait("scalar", "DV", st5["sums_read"][hp])
                        P.wait("scalar", "PE", st5["pbuf_free"][upar])
                    a = P.op("scalar", lambda e, b=bank, P_=Pb, kb=kb, hp=hp: e.activation(
                        out=P_[:, kb * 512:(kb + 1) * 512], in_=b[:], func=AF.Exp,
                        scale=ISQ, bias=bias4[:, hp:hp + 1],
                        accum_out=sums4[:, hp * 8 + kb: hp * 8 + kb + 1]), "AC")
                    st5["sc_exp"][slot] = a
                    st5["last_exp"] = a
                    exp_last[hpp] = a
                    emitted += 1
                    dly = min(4, nsc - 1)
                    want = 0 if emitted <= dly else (len(pending) * (emitted - dly)) // (nsc - dly)
                    while ci < want:
                        pending[ci]()
                        ci += 1
            while ci < len(pending):
                pending[ci]()
                ci += 1
            nrm = 0
            for hpp in range(2):
                hp = pair * 2 + hpp
                Pb = praw(upar, hpp)
                P.wait("vector", "AC", exp_last[hpp])
                v = P.op("vector", lambda e, hp=hp, qt=qt: e.reduce_sum(
                    tot4[:, hp:hp + 1], sums4[:, hp * 8: hp * 8 + qt + 1], axis=AX.X), "DV")
                st5["sums_read"][hp] = v
                P.wait("vector", "DV", v)
                v = P.op("vector", lambda e, hp=hp: e.reciprocal(
                    out=rcp4[:, hp:hp + 1], in_=tot4[:, hp:hp + 1]), "DV")
                P.wait("vector", "DV", v)
                nrm = P.op("vector", lambda e, P_=Pb, qt=qt, hp=hp: e.tensor_scalar_mul(
                    P_[:, 0:(qt + 1) * 512], P_[:, 0:(qt + 1) * 512], rcp4[:, hp:hp + 1]), "DV")
            st5["norm_done"][(qt, pair)] = nrm

        def trav_chunks(kvh, qt, pair, upar, V_sb):
            heads = (kvh * GRP + pair * 2, kvh * GRP + pair * 2 + 1)
            av_bank = pb[4 + upar]
            state = {"first": True}

            def mkT(kb, jj, lastT):
                def emit():
                    reg = st5["pt_rr"]
                    st5["pt_rr"] = (reg + 1) % 2
                    ptbank = pb[2 + reg]
                    if kb == 0 and jj == 0:
                        P.wait("tensor", "DV", st5["norm_done"].pop((qt, pair)))
                        P.wait("tensor", "DV", st5["av_evac"][upar])
                    pe_sem, pe_val = st5["pt_evac"][reg]
                    P.wait("tensor", pe_sem, pe_val)
                    rf = st5["region_free"].pop(2 + reg, 0)
                    P.wait("tensor", "DV", rf)
                    for hpp in range(2):
                        mm = P.op("tensor",
                                  lambda e, b=ptbank, kb=kb, jj=jj, hpp=hpp: e.matmul(
                                      b[:, hpp * 128:hpp * 128 + 128],
                                      praw(upar, hpp)[:, kb * 512 + jj * 128:
                                                      kb * 512 + jj * 128 + 128],
                                      ident[:], start=True, stop=True),
                                  "PE" if hpp == 1 else None)
                    if lastT:
                        st5["pbuf_free"][upar] = mm
                    return reg, mm
                return emit

            def mkAV(kb, jj, last):
                def emit(treg_mm):
                    reg, mm = treg_mm
                    eng, sem = (("scalar", "AC") if reg == 0 else ("vector", "DV"))
                    P.wait(eng, "PE", mm)
                    P.wait(eng, "PE", st5["ptsb_free"][reg])
                    if reg == 0:
                        aev = P.op("scalar", lambda e, b=pb[2 + reg], r=reg: e.copy(
                            out=ptsb[:, r * 256:(r + 1) * 256], in_=b[:, 0:256]), "AC")
                    else:
                        aev = P.op("vector", lambda e, b=pb[2 + reg], r=reg: e.tensor_copy(
                            out=ptsb[:, r * 256:(r + 1) * 256], in_=b[:, 0:256]), "DV")
                    st5["pt_evac"][reg] = (sem, aev)
                    P.wait("tensor", sem, aev)
                    mm = P.op("tensor", lambda e, b=av_bank, V=V_sb, kb=kb, jj=jj,
                              st=state["first"], sp=last, r=reg: e.matmul(
                        b[:, 0:256], V[:, (kb * 4 + jj) * 128:(kb * 4 + jj) * 128 + 128],
                        ptsb[:, r * 256:(r + 1) * 256], start=st, stop=sp), "PE")
                    st5["ptsb_free"][reg] = mm
                    state["first"] = False
                    if last:
                        st5["tr_last"] = mm
                        if qt == NT - 1 and pair == 1:
                            kv_consumed[kvh] = mm
                        P.wait("vector", "PE", mm)
                        for hpp in range(2):
                            h = heads[hpp]
                            ae = P.op("vector", lambda e, b=av_bank, hpp=hpp, h=h, qt=qt: e.tensor_copy(
                                out=qa_pool[:, h * 1024 + qt * 128: h * 1024 + (qt + 1) * 128],
                                in_=b[:, hpp * 128:hpp * 128 + 128]), "DV" if hpp == 1 else None)
                        st5["av_evac"][upar] = ae
                return emit

            n = (qt + 1) * 4
            parts = []
            i = 0
            for kb in range(qt + 1):
                for jj in range(4):
                    parts.append((mkT(kb, jj, i == n - 1), mkAV(kb, jj, i == n - 1)))
                    i += 1
            hold = {}

            def mk_step(idx, do_T, do_AV):
                def emit():
                    if do_T is not None:
                        hold[do_T] = parts[do_T][0]()
                    if do_AV is not None:
                        parts[do_AV][1](hold.pop(do_AV))
                return emit

            sched = [mk_step(0, 0, None)]
            for i in range(1, n):
                sched.append(mk_step(i, i, i - 1))
            sched.append(mk_step(n, None, n - 1))
            return sched

        REG = [0, 1, 2, 3, 6, 7]

        def prework_a(kvh, K_sb):
            P.wait("gpsimd", f"KV{kvh % 2}", kld[kvh])
            P.wait("gpsimd", "PE", st5["tr_last"])
            pk = P.op("gpsimd", lambda e, K=K_sb: e.tensor_tensor(
                out=ksink[:], in0=K, in1=K, op=ALU.mult), "PO")
            P.wait("tensor", "PO", pk)
            P.wait("tensor", "AC", st5["last_exp"])
            for pe_sem, pe_val in st5["pt_evac"]:
                P.wait("tensor", pe_sem, pe_val)
            P.wait("tensor", "DV", st5["pb7_free"])
            kred = [0] * 8
            for kb in range(8):
                reg = REG[kb % 6]
                if kb >= 6:
                    P.wait("tensor", "DV", kred[kb - 6])
                mm = P.op("tensor", lambda e, b=pb[reg], kb=kb: e.matmul(
                    b[0:1, 0:512], onec[:], ksink[:, kb * 512:(kb + 1) * 512],
                    start=True, stop=True), "PE")
                P.wait("vector", "PE", mm)
                kred[kb] = P.op("vector", lambda e, kb=kb: e.reduce_max(
                    kmrow[0:1, kb:kb + 1], pb[REG[kb % 6]][0:1, 0:512], axis=AX.X), "DV")
            P.wait("vector", "DV", kred[7])
            km_v = P.op("vector", lambda e: e.reduce_max(
                kmrow[0:1, 8:9], kmrow[0:1, 0:8], axis=AX.X), "DV")
            P.wait("gpsimd", "PE", mm)
            pq = P.op("gpsimd", lambda e, kvh=kvh: e.tensor_tensor(
                out=ksink[:], in0=qa_pool[:, kvh * 4096:(kvh + 1) * 4096],
                in1=qa_pool[:, kvh * 4096:(kvh + 1) * 4096], op=ALU.mult), "PO")
            P.wait("tensor", "PO", pq)
            qred = [0] * 8
            for r in range(8):
                reg = REG[r % 6]
                P.wait("tensor", "DV", kred[r % 6] if r < 6 else qred[r - 6])
                if r < 2:
                    P.wait("tensor", "DV", kred[6 + r])
                mm = P.op("tensor", lambda e, b=pb[reg], r=r: e.matmul(
                    b[0:1, 0:512], onec[:], ksink[:, r * 512:(r + 1) * 512],
                    start=True, stop=True), "PE")
                P.wait("vector", "PE", mm)
                qred[r] = P.op("vector", lambda e, r=r: e.reduce_max(
                    kmrow[0:1, 16 + r:17 + r], pb[REG[r % 6]][0:1, 0:512], axis=AX.X), "DV")
            for i, reg in enumerate(REG):
                st5["region_free"][reg] = qred[i]
            st5["region_free"][0] = max(qred[6], st5["region_free"][0])
            st5["region_free"][1] = max(qred[7], st5["region_free"][1])
            return km_v, qred

        def prework_b(kvh, km_v, qred):
            chain_done = 0
            for hp in range(4):
                P.wait("vector", "DV", max(qred[2 * hp], qred[2 * hp + 1], km_v))
                v = P.op("vector", lambda e, hp=hp: e.tensor_max(
                    kmrow[0:1, 24 + hp:25 + hp], kmrow[0:1, 16 + 2 * hp:17 + 2 * hp],
                    kmrow[0:1, 17 + 2 * hp:18 + 2 * hp]), "DV")
                P.wait("vector", "DV", v)
                v = P.op("vector", lambda e, hp=hp: e.tensor_mul(
                    kmrow[0:1, 9 + hp:10 + hp], kmrow[0:1, 8:9],
                    kmrow[0:1, 24 + hp:25 + hp]), "DV")
                P.wait("scalar", "DV", v)
                aa = P.op("scalar", lambda e, hp=hp: e.activation(
                    out=kmrow[0:1, 28 + hp:29 + hp], in_=kmrow[0:1, 9 + hp:10 + hp],
                    func=AF.Sqrt, scale=1.2155), "AC")
                P.wait("vector", "AC", aa)
                v = P.op("vector", lambda e, hp=hp: e.tensor_scalar_mul(
                    kmrow[0:1, 9 + hp:10 + hp], kmrow[0:1, 28 + hp:29 + hp], -ISQ), "DV")
                P.wait("vector", "DV", v)
                v = P.op("vector", lambda e, hp=hp: e.tensor_copy(
                    out=trowb[0:1, 1 + hp:2 + hp], in_=kmrow[0:1, 9 + hp:10 + hp]), "DV")
                chain_done = v
            P.wait("tensor", "DV", chain_done)
            P.wait("tensor", "DV", st5["region_free"].get(7, 0))
            for hp in range(4):
                mmb = P.op("tensor", lambda e, hp=hp: e.matmul(
                    pb[7][:, 2 + hp:3 + hp], oner[:], trowb[0:1, 1 + hp:2 + hp],
                    start=True, stop=True), "PE" if hp == 3 else None)
            P.wait("vector", "PE", mmb)
            P.wait("vector", "AC", st5["last_exp"])
            bc = P.op("vector", lambda e: e.tensor_copy(out=bias4[:], in_=pb[7][:, 2:6]), "DV")
            st5["pb7_free"] = bc
            st5["region_free"][7] = max(st5["region_free"].get(7, 0), bc)
            st5["bias_done"] = bc

        uidx = 0
        for kvh in range(HKV):
            par = kvh % 2
            K_sb = xn_pool[:, par * 4096:(par + 1) * 4096]
            V_sb = xn_pool[:, 8192 + par * 4096: 8192 + (par + 1) * 4096]
            if kvh + 1 < HKV:
                kld[kvh + 1] = load_kv(kvh + 1, kv_consumed[kvh - 1] if kvh >= 1 else 0)
            km_v, qred = prework_a(kvh, K_sb)
            prework_b(kvh, km_v, qred)
            pending = []
            for qt in range(NT):
                for pair in range(2):
                    upar = uidx % 2
                    emit_unit(kvh, qt, pair, upar, K_sb, pending)
                    pending = trav_chunks(kvh, qt, pair, upar, V_sb)
                    uidx += 1
            for c in pending:
                c()
            pending = []
        return P.val("AC")

    att_done = phase5()
    P.barrier()

    if DEBUG_OUTPUTS:
        for h in range(HQ):
            P.op("sync", lambda e, h=h: e.dma_start(
                out=dbg_att[h], in_=qa_pool[:, h * 1024:(h + 1) * 1024]), "DBG", 16)
        P.barrier()

    if BUILD_PHASES <= 5:
        P.finalize()
        return nc, es

    def phase6():
        for s in range(4):
            P.wait("sync", f"OS{s}", P.val(f"OS{s}"))
        return emit_outpass(
            ngroups=8, stop_ic=31,
            wsrc=lambda oc, g: wo_h[oc, g],
            lhs_fn=lambda ic, t: qa_pool[:, ic * 1024 + t * 128: ic * 1024 + (t + 1) * 128],
            add_src=y1_d, store_dst=y_out)

    phase6()
    for s in range(4):
        P.wait("sync", f"OS{s}", P.val(f"OS{s}"))
    P.finalize()
    return nc, es


def _host_prep(inputs):
    x = np.asarray(inputs["L0_attn_hidden_states"], np.float32)
    wq = np.asarray(inputs["L0_attn_wq"], np.float32)
    wk = np.asarray(inputs["L0_attn_wk"], np.float32)
    wv = np.asarray(inputs["L0_attn_wv"], np.float32)
    wo = np.asarray(inputs["L0_attn_wo"], np.float32)
    worg = np.asarray(inputs["L0_ffn_W_org"], np.float32)
    ldown = np.asarray(inputs["L0_ffn_lora_down"], np.float32)
    lup = np.asarray(inputs["L0_ffn_lora_up"], np.float32)
    lkeys = np.asarray(inputs["L0_ffn_lora_keys"], np.float32)
    wqm = np.asarray(inputs["L0_ffn_Wq"], np.float32)

    wq_h = np.ascontiguousarray(
        wq.reshape(HQ, 128, 32, 128).transpose(0, 3, 2, 1).reshape(HQ, 128, D))
    wk_h = np.ascontiguousarray(
        wk.reshape(HKV, 128, 32, 128).transpose(0, 3, 2, 1).reshape(HKV, 128, D))
    wv_h = np.ascontiguousarray(
        wv.reshape(2, 512, 8, 4, 128).transpose(0, 2, 4, 3, 1).reshape(2, 8, 128, 2048))
    wo_h = np.ascontiguousarray(
        wo.reshape(8, 512, 8, 4, 128).transpose(0, 2, 4, 3, 1).reshape(8, 8, 128, 2048))
    worg_h = np.ascontiguousarray(
        worg.reshape(8, 512, 8, 4, 128).transpose(0, 2, 4, 3, 1).reshape(8, 8, 128, 2048))
    dmat = ldown.transpose(1, 0, 2).reshape(D, R)
    dmat_h = np.ascontiguousarray(
        dmat.reshape(32, 128, 4, 128).transpose(2, 1, 0, 3).reshape(4, 128, 32 * 128))
    umat = lup.reshape(R, D)
    umat_h = np.ascontiguousarray(
        umat.reshape(4, 128, 8, 512).transpose(2, 1, 0, 3).reshape(8, 128, 2048))
    wqm_h = np.ascontiguousarray(
        wqm.reshape(128, 32, 128).transpose(2, 1, 0).reshape(128, 32 * 128))
    keys_t = np.ascontiguousarray(lkeys.T)

    ident = np.eye(128, dtype=bfdt)
    onec = np.ones((128, 1), dtype=bfdt)
    oner = np.ones((1, 128), dtype=bfdt)
    one1 = np.ones((1, 1), dtype=bfdt)
    sel = np.zeros((NE, 512), dtype=np.float32)
    for rc in range(4):
        for p in range(128):
            sel[2 * rc + p // 64, rc * 128 + p] = 1.0
    sel = sel.astype(bfdt)

    inv_freq = 1.0 / (10000.0 ** (np.arange(0, DH, 2, dtype=np.float32) / DH))

    in_maps = []
    for c in range(8):
        b, j = c // 4, c % 4
        xr = x[b].reshape(NT, 4, 128, D)[:, j]
        x_core = np.ascontiguousarray(xr.reshape(TL, D))
        pos = (np.arange(NT)[:, None] * 512 + j * 128 +
               np.arange(128)[None, :]).reshape(TL).astype(np.float32)
        ang = pos[None, :] * inv_freq[:, None]
        ropec_a = np.empty((128, TL), np.float32)
        ropes_a = np.empty((128, TL), np.float32)
        ropec_a[0:64] = np.cos(ang); ropec_a[64:128] = np.cos(ang)
        ropes_a[0:64] = -np.sin(ang); ropes_a[64:128] = np.sin(ang)
        qi = np.arange(128)[:, None]
        ji = np.arange(512)[None, :]
        mask_a = np.where(j * 128 + qi >= ji, 0.0, NEG).astype(bfdt)
        in_maps.append({
            "x_in": x_core, "wq_h": wq_h, "wk_h": wk_h, "wv_h": wv_h,
            "wo_h": wo_h, "worg_h": worg_h, "dmat_h": dmat_h, "umat_h": umat_h,
            "wqm_h": wqm_h, "keys_t": keys_t, "rope_c": ropec_a, "rope_s": ropes_a,
            "mask_in": mask_a, "ident_in": ident, "onec_in": onec, "oner_in": oner,
            "one1_in": one1, "sel_in": sel,
        })
    return in_maps


def _assemble(results):
    y = np.empty((B, S, D), np.float32)
    for c in range(8):
        b, j = c // 4, c % 4
        y[b].reshape(NT, 4, 128, D)[:, j] = results[c]["y_out"].reshape(NT, 128, D)
    return y


def kernel(**inputs):
    nc, es = build_program()
    in_maps = _host_prep(inputs)
    try:
        res = run_bass_kernel_spmd(nc, in_maps, core_ids=list(range(8)))
    finally:
        es.close()
    return _assemble(res.results)


# version 11
# speedup vs baseline: 1.8273x; 1.2711x over previous
import math
import numpy as np
import ml_dtypes
import concourse.bass as bass
import concourse.mybir as mybir
from concourse.bass_utils import run_bass_kernel_spmd
import concourse.bass_utils as _bu

if not getattr(_bu, "_ldw_opt_patched", False):
    _orig_run_command = _bu.run_command

    def _run_command_ldw(cmd, **kw):
        cmd = [c.replace("--enable-ldw-opt=false", "--enable-ldw-opt=true")
               if isinstance(c, str) else c for c in cmd]
        return _orig_run_command(cmd, **kw)

    _bu.run_command = _run_command_ldw
    _bu._ldw_opt_patched = True

F32 = mybir.dt.float32
BF16 = mybir.dt.bfloat16
AF = mybir.ActivationFunctionType
ALU = mybir.AluOpType
AX = mybir.AxisListType
bfdt = ml_dtypes.bfloat16

B, S, D = 2, 4096, 4096
HQ, HKV, DH = 32, 8, 128
GRP = HQ // HKV
NT = 8
TL = 1024
R = 512
NE, KD = 8, 128
EPS = 1e-5
ISQ = 1.0 / math.sqrt(DH)
NEG = -1e9

DEBUG_OUTPUTS = False
BUILD_PHASES = 99
PHASE2_CUT = "full"


class Prog:
    ENGS = ("sync", "gpsimd", "tensor", "vector", "scalar")
    ENG_SEMS = ("PE", "DV", "AC", "PO")
    DMA_SEMS = ("CONST", "XL0", "XL1", "XL2", "XL3", "KV0", "KV1", "MN", "MS",
                "KS0", "KS1", "VS0", "VS1", "OS0", "OS1", "OS2", "OS3",
                "WP0", "WP1", "WS0", "WS1", "WS2", "WS3", "WM", "DBG")
    CC_SEMS = ("CCM", "CCK")
    ALL_SEMS = ENG_SEMS + DMA_SEMS + CC_SEMS

    def __init__(self, nc, es):
        self.nc = nc
        self.q = {e: [] for e in self.ENGS}
        self.sem = {}
        self.cnt = {}
        self._es = es

    def mksem(self, name):
        self.sem[name] = self._es.enter_context(self.nc.semaphore(name))
        self.cnt[name] = 0

    def wait(self, eng, sem, val):
        if val <= 0:
            return
        h = self.sem[sem]
        self.q[eng].append(lambda e, h=h, v=val: e.wait_ge(h, v))

    def op(self, eng, fn, inc=None, n=1):
        if inc is None:
            self.q[eng].append(fn)
            return None
        self.cnt[inc] += n
        v = self.cnt[inc]
        h = self.sem[inc]
        self.q[eng].append(lambda e, fn=fn, h=h, n=n: fn(e).then_inc(h, n))
        return v

    def val(self, sem):
        return self.cnt[sem]

    def barrier(self):
        marks = [(s, self.val(s)) for s in self.ENG_SEMS + self.DMA_SEMS]
        for eng in self.ENGS:
            for sem, v in marks:
                self.wait(eng, sem, v)

    def finalize(self):
        with self.nc.Block() as block:
            for eng in self.ENGS:
                ops = self.q[eng]

                def run(e, ops=ops):
                    for f in ops:
                        f(e)

                getattr(block, eng)(run)


def build_program():
    from contextlib import ExitStack

    nc = bass.Bass("TRN2", target_bir_lowering=False, debug=False)
    es = ExitStack()

    def dram(name, shape, dtype, kind):
        return nc.dram_tensor(name, shape, dtype, kind=kind).ap()

    x_in = dram("x_in", [TL, D], F32, "ExternalInput")
    wq_h = dram("wq_h", [HQ, 128, D], F32, "ExternalInput")
    wk_h = dram("wk_h", [HKV, 128, D], F32, "ExternalInput")
    wv_h = dram("wv_h", [2, 8, 128, 2048], F32, "ExternalInput")
    wo_h = dram("wo_h", [8, 8, 128, 2048], F32, "ExternalInput")
    worg_h = dram("worg_h", [8, 8, 128, 2048], F32, "ExternalInput")
    dmat_h = dram("dmat_h", [4, 128, 32 * 128], F32, "ExternalInput")
    umat_h = dram("umat_h", [8, 128, 2048], F32, "ExternalInput")
    wqm_h = dram("wqm_h", [128, 32 * 128], F32, "ExternalInput")
    keys_t = dram("keys_t", [KD, NE], F32, "ExternalInput")
    rope_c = dram("rope_c", [128, TL], F32, "ExternalInput")
    rope_s = dram("rope_s", [128, TL], F32, "ExternalInput")
    mask_in = dram("mask_in", [128, 512], BF16, "ExternalInput")
    ident_in = dram("ident_in", [128, 128], BF16, "ExternalInput")
    onec_in = dram("onec_in", [128, 1], BF16, "ExternalInput")
    oner_in = dram("oner_in", [1, 128], BF16, "ExternalInput")
    one1_in = dram("one1_in", [1, 1], BF16, "ExternalInput")
    sel_in = dram("sel_in", [NE, 512], BF16, "ExternalInput")

    y_out = dram("y_out", [TL, D], F32, "ExternalOutput")

    kv_ci = dram("kv_ci", [2048, 1024], BF16, "Internal")
    kv_co = dram("kv_co", [4 * 2048, 1024], BF16, "Internal")
    mn_ci = dram("mn_ci", [128, 32], F32, "Internal")
    mn_co = dram("mn_co", [128, 32], F32, "Internal")
    y1_d = dram("y1_d", [TL, D], F32, "Internal")
    RG = [[0, 1, 2, 3], [4, 5, 6, 7]]

    if DEBUG_OUTPUTS:
        dbg_xn = dram("dbg_xn", [32, 128, TL], BF16, "ExternalOutput")
        dbg_coef = dram("dbg_coef", [128, 4], F32, "ExternalOutput")
        dbg_q = dram("dbg_q", [HQ, 128, TL], BF16, "ExternalOutput")
        dbg_kv = dram("dbg_kv", [4 * 2048, 1024], BF16, "ExternalOutput")
        dbg_att = dram("dbg_att", [HQ, 128, TL], BF16, "ExternalOutput")
        dbg_y1 = dram("dbg_y1", [TL, D], F32, "ExternalOutput")

    sb = lambda name, shape, dt: es.enter_context(nc.sbuf_tensor(name, shape, dt))
    xn_pool = sb("xn_pool", [128, 32768], BF16)
    qa_pool = sb("qa_pool", [128, 32768], BF16)
    f32_pool = sb("f32_pool", [128, 8192], F32)
    wpool = sb("wpool", [128, 8192], BF16)
    ptsb = sb("ptsb", [128, 1024], BF16)
    stage = sb("stage", [128, 1536], BF16)
    ident = sb("ident", [128, 128], BF16)
    onec = sb("onec", [128, 1], BF16)
    oner = sb("oner", [1, 128], BF16)
    one1 = sb("one1", [1, 1], BF16)
    mask = sb("mask", [128, 512], BF16)
    ropec = sb("ropec", [128, TL], F32)
    ropes = sb("ropes", [128, TL], F32)
    selm = sb("selm", [NE, 512], BF16)
    keyst = sb("keyst", [KD, NE], BF16)
    ra = f32_pool[:, 0:512]
    rb = f32_pool[:, 512:1024]
    rt1 = f32_pool[:, 1024:1536]
    rt2 = f32_pool[:, 1536:2048]
    colsum = sb("colsum", [128, 32], F32)
    meansb = sb("meansb", [128, 32], F32)
    meanbf = sb("meanbf", [128, 32], BF16)
    stat = sb("stat", [128, 16], F32)
    sums4 = sb("sums4", [128, 32], F32)
    tot4 = sb("tot4", [128, 4], F32)
    rcp4 = sb("rcp4", [128, 4], F32)
    bias4 = sb("bias4", [128, 4], F32)
    scol = sb("scol", [128, 4], F32)
    trowb = sb("trowb", [1, 128], BF16)
    kmrow = sb("kmrow", [1, 32], F32)

    pb = [es.enter_context(nc.psum_tensor(f"pb{i}", [128, 512], F32)) for i in range(8)]

    P = Prog(nc, es)
    for s in Prog.ALL_SEMS:
        P.mksem(s)

    for dst, src in ((ident, ident_in), (onec, onec_in), (oner, oner_in),
                     (one1, one1_in), (mask, mask_in), (ropec, rope_c),
                     (ropes, rope_s), (selm, sel_in)):
        P.op("sync", lambda e, d=dst, s=src: e.dma_start(out=d[:], in_=s[:]), "CONST", 16)
    P.op("gpsimd", lambda e: e.dma_start(out=keyst[:], in_=keys_t[:]), "WM", 16)
    P.barrier()

    def phase1():
        tp_free = [0, 0]
        norm_done = [0] * NT
        stage_free = [0, 0]
        half_done = [0, 0]
        xld = [0] * NT
        for qt in range(NT):
            par = qt % 2
            xbuf = f32_pool[:, 4096:8192]
            st = lambda i, par=par: stat[:, par * 8 + i: par * 8 + i + 1]
            if qt >= 1:
                P.wait("sync", "AC", norm_done[qt - 1])
            xld[qt] = P.op("sync", lambda e, d=xbuf, q=qt: e.dma_start(
                out=d, in_=x_in[q * 128:(q + 1) * 128, :]), "XL0", 16)
            P.wait("vector", "XL0", xld[qt])
            sink = qa_pool[:, 8192:12288]
            P.wait("scalar", "XL0", xld[qt])
            a0 = P.op("scalar", lambda e, x=xbuf, st=st, s=sink: e.activation(
                out=s, in_=x, func=AF.Square, accum_out=st(2)), "AC")
            v = P.op("vector", lambda e, x=xbuf, st=st: e.reduce_sum(
                st(0), x, axis=AX.X, negate=True), "DV")
            P.wait("vector", "DV", v)
            v = P.op("vector", lambda e, st=st: e.tensor_scalar_mul(st(1), st(0), 1.0 / D), "DV")
            P.wait("vector", "DV", v)
            v = P.op("vector", lambda e, st=st: e.tensor_mul(st(3), st(1), st(1)), "DV")
            P.wait("vector", "AC", a0)
            P.wait("vector", "DV", v)
            v = P.op("vector", lambda e, st=st: e.tensor_scalar(
                st(4), st(2), 1.0 / D, EPS, ALU.mult, ALU.add), "DV")
            P.wait("vector", "DV", v)
            v = P.op("vector", lambda e, st=st: e.tensor_sub(st(5), st(4), st(3)), "DV")
            P.wait("scalar", "DV", v)
            a = P.op("scalar", lambda e, st=st: e.activation(out=st(6), in_=st(5), func=AF.Sqrt), "AC")
            P.wait("vector", "AC", a)
            v = P.op("vector", lambda e, st=st: e.reciprocal(out=st(7), in_=st(6)), "DV")
            P.wait("vector", "DV", v)
            v = P.op("vector", lambda e, st=st: e.tensor_mul(st(3), st(1), st(7)), "DV")
            xst = qa_pool[:, par * 4096:(par + 1) * 4096]
            P.wait("scalar", "DV", v)
            P.wait("scalar", "PE", stage_free[par])
            a = P.op("scalar", lambda e, x=xbuf, o=xst, st=st: e.activation(
                out=o, in_=x, func=AF.Identity, scale=st(7), bias=st(3)), "AC")
            norm_done[qt] = a
            for g in range(8):
                bank = pb[g % 2]
                P.wait("tensor", "AC", a)
                P.wait("tensor", "DV", tp_free[g % 2])
                for u in range(4):
                    dc = g * 4 + u
                    pe_v = P.op("tensor", lambda e, b=bank, u=u, x=xst, dc=dc: e.matmul(
                        b[:, u * 128:(u + 1) * 128], x[:, dc * 128:(dc + 1) * 128],
                        ident[:], start=True, stop=True), "PE" if u == 3 else None)
                P.wait("vector", "PE", pe_v)
                for u in range(4):
                    dc = g * 4 + u
                    dst = xn_pool[:, dc * 1024 + qt * 128: dc * 1024 + (qt + 1) * 128]
                    ev = P.op("vector", lambda e, b=bank, u=u, d=dst: e.tensor_copy(
                        out=d, in_=b[:, u * 128:(u + 1) * 128]), "DV" if u == 3 else None)
                tp_free[g % 2] = ev
            stage_free[par] = pe_v
            if qt == 3 or qt == 7:
                half_done[qt // 4] = max(tp_free)
        P.wait("vector", "DV", max(tp_free[0], tp_free[1]))
        for dc in range(32):
            cv = P.op("vector", lambda e, dc=dc: e.reduce_sum(
                colsum[:, dc:dc + 1], xn_pool[:, dc * 1024:(dc + 1) * 1024], axis=AX.X),
                "DV" if dc == 31 else None)
        return cv, half_done

    colsum_dv, xn_half = phase1()

    if DEBUG_OUTPUTS:
        P.wait("sync", "DV", colsum_dv)
        for dc in range(32):
            P.op("sync", lambda e, dc=dc: e.dma_start(
                out=dbg_xn[dc], in_=xn_pool[:, dc * 1024:(dc + 1) * 1024]), "DBG", 16)

    if BUILD_PHASES <= 1:
        P.finalize()
        return nc, es

    rope_state = {"p1": 0, "v4": 0, "stage_st": {}}

    def rope_evac(bank, th, dst_bf, pe_v, extra_dst_wait=None):
        C = ropec[:, th * 512:(th + 1) * 512]
        Ssg = ropes[:, th * 512:(th + 1) * 512]
        P.wait("vector", "PE", pe_v)
        P.wait("vector", "DV", rope_state["p1"])
        v1 = P.op("vector", lambda e, b=bank: e.tensor_copy(out=rb[64:128, :], in_=b[0:64, :]), "DV")
        v2 = P.op("vector", lambda e, b=bank: e.tensor_copy(out=rb[0:64, :], in_=b[64:128, :]), "DV")
        P.wait("scalar", "PE", pe_v)
        P.wait("scalar", "DV", max(v2, rope_state["v4"]))
        a1 = P.op("scalar", lambda e, b=bank: e.copy(out=ra[:], in_=b[:]), "AC")
        P.wait("vector", "AC", a1)
        v3 = P.op("vector", lambda e, C=C: e.tensor_mul(rt1[:], ra[:], C), "DV")
        P.wait("vector", "DV", v2)
        p1 = P.op("vector", lambda e, Ssg=Ssg: e.tensor_mul(rt2[:], rb[:], Ssg), "DV")
        rope_state["p1"] = p1
        P.wait("vector", "DV", max(v3, p1))
        if extra_dst_wait is not None:
            sem, val = extra_dst_wait
            P.wait("vector", sem, val)
        v4 = P.op("vector", lambda e, d=dst_bf: e.tensor_add(d, rt1[:], rt2[:]), "DV")
        rope_state["v4"] = v4
        return v2, v4

    def phase2():
        wfree = [0, 0]
        bank_free = [0] * 4
        kst_by_th = {0: 0, 1: 0}
        for kvh in range(HKV):
            par = kvh % 2
            wbuf = wpool[:, par * 4096:(par + 1) * 4096]
            P.wait("gpsimd", "PE", wfree[par])
            wld = P.op("gpsimd", lambda e, d=wbuf, k=kvh: e.dma_start(
                out=d, in_=wk_h[k]), f"WP{par}", 16)
            P.wait("tensor", f"WP{par}", wld)
            for th in range(2):
                bslot = (kvh * 2 + th) % 4
                bank = pb[4 + bslot]
                P.wait("tensor", "DV", max(bank_free[bslot], xn_half[th]))
                for ic in range(32):
                    mm = P.op("tensor", lambda e, b=bank, w=wbuf, ic=ic, th=th: e.matmul(
                        b[:], w[:, ic * 128:(ic + 1) * 128],
                        xn_pool[:, ic * 1024 + th * 512: ic * 1024 + (th + 1) * 512],
                        start=(ic == 0), stop=(ic == 31)), "PE" if ic == 31 else None)
                if th == 1:
                    wfree[par] = mm
                dst = stage[:, th * 512:(th + 1) * 512]
                if PHASE2_CUT == "kmm":
                    P.wait("vector", "PE", mm)
                    done = P.op("vector", lambda e, b=bank, d=dst: e.tensor_copy(out=d, in_=b[:]), "DV")
                    bank_free[bslot] = done
                    continue
                cons, done = rope_evac(bank, th, dst, mm,
                                       extra_dst_wait=(f"KS{th}", kst_by_th[th]))
                bank_free[bslot] = done
                if PHASE2_CUT == "krope":
                    continue
                P.wait("sync", "DV", done)
                sv = P.op("sync", lambda e, k=kvh, th=th, d=dst: e.dma_start(
                    out=kv_ci[k * 128:(k + 1) * 128, th * 512:(th + 1) * 512], in_=d), f"KS{th}", 16)
                kst_by_th[th] = sv
        if PHASE2_CUT == "k":
            return 0
        P.wait("tensor", "DV", max(bank_free))
        P.wait("scalar", "KS0", P.val("KS0"))
        P.wait("scalar", "KS1", P.val("KS1"))
        vslot_free = [P.val("PE")] * 2
        vbank_free = [0] * NT
        vst_slot = [0, 0]
        for dvh in range(2):
            for g in range(8):
                slot = g % 2
                gbuf = wpool[:, slot * 2048:(slot + 1) * 2048]
                P.wait("gpsimd", "PE", vslot_free[slot])
                wv = P.op("gpsimd", lambda e, d=gbuf, dvh=dvh, g=g: e.dma_start(
                    out=d, in_=wv_h[dvh, g]), f"WS{slot}", 16)
                P.wait("tensor", f"WS{slot}", wv)
                for gi in range(4):
                    ic = g * 4 + gi
                    for t in range(NT):
                        if ic == 0:
                            P.wait("tensor", "AC", vbank_free[t])
                        mm = P.op("tensor", lambda e, b=pb[t], ic=ic, t=t, w=gbuf, gi=gi: e.matmul(
                            b[:], xn_pool[:, ic * 1024 + t * 128: ic * 1024 + (t + 1) * 128],
                            w[:, gi * 512:(gi + 1) * 512],
                            start=(ic == 0), stop=(ic == 31)), "PE" if t == NT - 1 else None)
                vslot_free[slot] = mm
            for t in range(NT):
                sl = t % 2
                vdst = stage[:, sl * 512:(sl + 1) * 512]
                P.wait("scalar", "PE", mm)
                P.wait("scalar", f"VS{sl}", vst_slot[sl])
                av = P.op("scalar", lambda e, b=pb[t], d=vdst: e.copy(out=d, in_=b[:]), "AC")
                vbank_free[t] = av
                P.wait("sync", "AC", av)
                vst_slot[sl] = P.op("sync", lambda e, t=t, dvh=dvh, d=vdst: e.dma_start(
                    out=kv_ci[1024 + t * 128:1024 + (t + 1) * 128, dvh * 512:(dvh + 1) * 512],
                    in_=d), f"VS{sl}", 16)
        if PHASE2_CUT == "kv":
            return 0, 0
        P.wait("sync", "DV", colsum_dv)
        stv = P.op("sync", lambda e: e.dma_start(out=mn_ci[:], in_=colsum[:]), "MS", 16)
        P.wait("gpsimd", "MS", stv)
        ar = P.op("gpsimd", lambda e: e.collective_compute(
            "AllReduce", ALU.add, RG, [mn_ci[:]], [mn_co[:]]), "CCM", 1)
        P.wait("gpsimd", "KS0", P.val("KS0"))
        P.wait("gpsimd", "KS1", P.val("KS1"))
        P.wait("gpsimd", "VS0", P.val("VS0"))
        P.wait("gpsimd", "VS1", P.val("VS1"))
        ag = P.op("gpsimd", lambda e: e.collective_compute(
            "AllGather", ALU.bypass, RG, [kv_ci[:]], [kv_co[:]]), "CCK", 1)
        return ag, ar

    kv_ag, mean_ar = phase2()
    P.barrier()

    if BUILD_PHASES <= 2:
        P.finalize()
        return nc, es

    def phase3():
        wfree = [0, 0]
        bank_free = [0] * 4
        for h in range(HQ):
            par = h % 2
            wbuf = wpool[:, par * 4096:(par + 1) * 4096]
            P.wait("gpsimd", "PE", wfree[par])
            wld = P.op("gpsimd", lambda e, d=wbuf, h=h: e.dma_start(
                out=d, in_=wq_h[h]), f"WP{par}", 16)
            P.wait("tensor", f"WP{par}", wld)
            for th in range(2):
                bslot = (h * 2 + th) % 4
                bank = pb[4 + bslot]
                P.wait("tensor", "DV", bank_free[bslot])
                for ic in range(32):
                    mm = P.op("tensor", lambda e, b=bank, w=wbuf, ic=ic, th=th: e.matmul(
                        b[:], w[:, ic * 128:(ic + 1) * 128],
                        xn_pool[:, ic * 1024 + th * 512: ic * 1024 + (th + 1) * 512],
                        start=(ic == 0), stop=(ic == 31)), "PE" if ic == 31 else None)
                if th == 1:
                    wfree[par] = mm
                dst = qa_pool[:, h * 1024 + th * 512: h * 1024 + (th + 1) * 512]
                cons, done = rope_evac(bank, th, dst, mm)
                bank_free[bslot] = done
        return done

    q_done = phase3()
    P.barrier()

    if DEBUG_OUTPUTS:
        for h in range(HQ):
            P.op("sync", lambda e, h=h: e.dma_start(
                out=dbg_q[h], in_=qa_pool[:, h * 1024:(h + 1) * 1024]), "DBG", 16)
        P.wait("sync", "CCK", kv_ag)
        P.op("sync", lambda e: e.dma_start(out=dbg_kv[:], in_=kv_co[:]), "DBG", 16)
        P.barrier()

    if BUILD_PHASES <= 3:
        P.finalize()
        return nc, es

    def phase_coef():
        P.wait("sync", "CCM", mean_ar)
        mld = P.op("sync", lambda e: e.dma_start(out=meansb[:], in_=mn_co[:]), "MN", 16)
        P.wait("vector", "MN", mld)
        v = P.op("vector", lambda e: e.tensor_scalar_mul(meansb[:], meansb[:], 1.0 / S), "DV")
        P.wait("vector", "DV", v)
        v = P.op("vector", lambda e: e.tensor_copy(out=meanbf[:], in_=meansb[:]), "DV")
        wld = P.op("gpsimd", lambda e: e.dma_start(out=wpool[:, 0:4096], in_=wqm_h[:]), "WM", 16)
        P.wait("tensor", "DV", v)
        P.wait("tensor", "WM", wld)
        for ic in range(32):
            mm = P.op("tensor", lambda e, ic=ic: e.matmul(
                pb[7][0:1, 0:128], meanbf[:, ic:ic + 1],
                wpool[:, ic * 128:(ic + 1) * 128], start=(ic == 0), stop=(ic == 31)),
                "PE" if ic == 31 else None)
        P.wait("scalar", "PE", mm)
        a = P.op("scalar", lambda e: e.copy(out=trowb[:], in_=pb[7][0:1, 0:128]), "AC")
        P.wait("tensor", "AC", a)
        mm = P.op("tensor", lambda e: e.matmul(pb[6][:, 0:1], trowb[:], one1[:],
                                               start=True, stop=True), "PE")
        P.wait("vector", "PE", mm)
        v = P.op("vector", lambda e: e.tensor_copy(out=meanbf[:, 0:1], in_=pb[6][:, 0:1]), "DV")
        P.wait("tensor", "DV", v)
        mm = P.op("tensor", lambda e: e.matmul(pb[6][0:1, 16:24], meanbf[:, 0:1], keyst[:],
                                               start=True, stop=True), "PE")
        P.wait("vector", "PE", mm)
        v = P.op("vector", lambda e: e.tensor_scalar_mul(
            kmrow[0:1, 0:8], pb[6][0:1, 16:24], 1.0 / math.sqrt(KD)), "DV")
        P.wait("vector", "DV", v)
        v = P.op("vector", lambda e: e.reduce_max(kmrow[0:1, 8:9], kmrow[0:1, 0:8],
                                                  axis=AX.X, negate=True), "DV")
        P.wait("scalar", "DV", v)
        a = P.op("scalar", lambda e: e.activation(out=kmrow[0:1, 10:18], in_=kmrow[0:1, 0:8],
                                                  func=AF.Exp, bias=kmrow[0:1, 8:9], scale=1.0,
                                                  accum_out=kmrow[0:1, 9:10]), "AC")
        P.wait("vector", "AC", a)
        v = P.op("vector", lambda e: e.reciprocal(out=kmrow[0:1, 8:9], in_=kmrow[0:1, 9:10]), "DV")
        P.wait("vector", "DV", v)
        v = P.op("vector", lambda e: e.tensor_scalar_mul(
            kmrow[0:1, 10:18], kmrow[0:1, 10:18], kmrow[0:1, 8:9]), "DV")
        P.wait("vector", "DV", v)
        v = P.op("vector", lambda e: e.tensor_copy(out=trowb[0:1, 0:8], in_=kmrow[0:1, 10:18]), "DV")
        P.wait("tensor", "DV", v)
        mm = P.op("tensor", lambda e: e.matmul(pb[6][0:8, 8:9], trowb[0:1, 0:8], one1[:],
                                               start=True, stop=True), "PE")
        P.wait("vector", "PE", mm)
        v = P.op("vector", lambda e: e.tensor_copy(out=meanbf[0:8, 1:2], in_=pb[6][0:8, 8:9]), "DV")
        P.wait("tensor", "DV", v)
        for rc in range(4):
            mm = P.op("tensor", lambda e, rc=rc: e.matmul(
                pb[6][:, 2 + rc:3 + rc], selm[:, rc * 128:(rc + 1) * 128], meanbf[0:8, 1:2],
                start=True, stop=True), "PE" if rc == 3 else None)
        P.wait("vector", "PE", mm)
        v = P.op("vector", lambda e: e.tensor_copy(out=scol[:], in_=pb[6][:, 2:6]), "DV")
        return v

    coef_done = phase_coef()
    P.barrier()

    if DEBUG_OUTPUTS:
        P.op("sync", lambda e: e.dma_start(out=dbg_coef[:], in_=scol[:]), "DBG", 16)
        P.barrier()

    def emit_outpass(ngroups, stop_ic, wsrc, lhs_fn, add_src, store_dst,
                     init_bank_ev=None, init_slot_free=0, extra_mm_wait=None):
        wslot_free = [init_slot_free, init_slot_free]
        bank_ev = [0] * NT
        if init_bank_ev:
            for kk, vv in init_bank_ev.items():
                bank_ev[kk] = vv
        xl_free = [0] * 4
        out_free = [0] * 4
        xv = [0] * NT
        last_sv = 0
        def stage_load(oc, t):
            sl = t % 4
            xs = f32_pool[:, sl * 512:(sl + 1) * 512]
            P.wait("sync", "DV", xl_free[sl])
            xv[t] = P.op("sync", lambda e, d=xs, t=t, oc=oc, a=add_src: e.dma_start(
                out=d, in_=a[t * 128:(t + 1) * 128, oc * 512:(oc + 1) * 512]), f"XL{sl}", 16)

        for oc in range(8):
            for t in range(4):
                stage_load(oc, t)
            for g in range(ngroups):
                slot = g % 2
                gbuf = wpool[:, slot * 2048:(slot + 1) * 2048]
                P.wait("gpsimd", "PE", wslot_free[slot])
                wv = P.op("gpsimd", lambda e, d=gbuf, oc=oc, g=g: e.dma_start(
                    out=d, in_=wsrc(oc, g)), f"WS{slot}", 16)
                P.wait("tensor", f"WS{slot}", wv)
                for gi in range(4):
                    ic = g * 4 + gi
                    if ic > stop_ic:
                        break
                    if extra_mm_wait is not None and ic == extra_mm_wait[2] * 4:
                        if oc == 0:
                            P.wait("tensor", extra_mm_wait[0], extra_mm_wait[1])
                    for t in range(NT):
                        if ic == 0:
                            P.wait("tensor", "DV", bank_ev[t])
                        mm = P.op("tensor",
                                  lambda e, b=pb[t], l=lhs_fn(ic, t), w=gbuf, gi=gi, ic=ic: e.matmul(
                                      b[:], l, w[:, gi * 512:(gi + 1) * 512],
                                      start=(ic == 0), stop=(ic == stop_ic)),
                                  "PE" if t == NT - 1 else None)
                wslot_free[slot] = mm
            for t in range(NT):
                sl = t % 4
                xs = f32_pool[:, sl * 512:(sl + 1) * 512]
                od = f32_pool[:, 4096 + sl * 512: 4096 + (sl + 1) * 512]
                P.wait("vector", "PE", mm)
                P.wait("vector", f"XL{sl}", xv[t])
                P.wait("vector", f"OS{sl}", out_free[sl])
                ev = P.op("vector", lambda e, b=pb[t], x=xs, o=od: e.tensor_add(o, b[:], x), "DV")
                xl_free[sl] = ev
                bank_ev[t] = ev
                P.wait("sync", "DV", ev)
                sv = P.op("sync", lambda e, o=od, t=t, oc=oc, s=store_dst: e.dma_start(
                    out=s[t * 128:(t + 1) * 128, oc * 512:(oc + 1) * 512], in_=o), f"OS{sl}", 16)
                out_free[sl] = sv
                last_sv = sv
                if t + 4 < NT:
                    stage_load(oc, t + 4)
        return last_sv

    def phase4():
        h_sb = wpool[:, 4096:8192]
        hfree = [0, 0]
        last_rc_mm = 0
        for rc in range(4):
            P.wait("gpsimd", "PE", last_rc_mm)
            wld = P.op("gpsimd", lambda e, rc=rc: e.dma_start(
                out=wpool[:, 0:4096], in_=dmat_h[rc]), "WP0", 16)
            P.wait("tensor", "WP0", wld)
            for th in range(2):
                bank = pb[6 + th]
                P.wait("tensor", "DV", hfree[th])
                for ic in range(32):
                    mm = P.op("tensor", lambda e, b=bank, ic=ic, th=th: e.matmul(
                        b[:], wpool[:, ic * 128:(ic + 1) * 128],
                        xn_pool[:, ic * 1024 + th * 512: ic * 1024 + (th + 1) * 512],
                        start=(ic == 0), stop=(ic == 31)), "PE" if ic == 31 else None)
                last_rc_mm = mm
                P.wait("vector", "PE", mm)
                hv = P.op("vector", lambda e, b=bank, rc=rc, th=th: e.tensor_scalar_mul(
                    h_sb[:, rc * 1024 + th * 512: rc * 1024 + (th + 1) * 512],
                    b[:], scol[:, rc:rc + 1]), "DV")
                hfree[th] = hv
        h_done = hv
        return emit_outpass(
            ngroups=9, stop_ic=35,
            wsrc=lambda oc, g: worg_h[oc, g] if g < 8 else umat_h[oc],
            lhs_fn=lambda ic, t: (xn_pool[:, ic * 1024 + t * 128: ic * 1024 + (t + 1) * 128]
                                  if ic < 32 else
                                  h_sb[:, (ic - 32) * 1024 + t * 128: (ic - 32) * 1024 + (t + 1) * 128]),
            add_src=x_in, store_dst=y1_d,
            init_bank_ev={6: hfree[0], 7: hfree[1]},
            init_slot_free=last_rc_mm,
            extra_mm_wait=("DV", h_done, 8))

    y1_last = phase4()
    P.barrier()

    if DEBUG_OUTPUTS:
        P.op("sync", lambda e: e.dma_start(out=dbg_y1[:], in_=y1_d[:]), "DBG", 16)
        P.barrier()

    if BUILD_PHASES <= 4:
        P.finalize()
        return nc, es

    def phase5():
        ksink = xn_pool[:, 16384:16384 + 4096]
        P.wait("sync", "CCK", kv_ag)
        kv_consumed = [0] * HKV

        def load_kv(kvh, wait_pe_val):
            par = kvh % 2
            P.wait("sync", "PE", wait_pe_val)
            last = 0
            for jj in range(4):
                kin = kv_co[jj * 2048 + kvh * 128: jj * 2048 + (kvh + 1) * 128, :]
                kout = xn_pool[:, par * 4096:(par + 1) * 4096].rearrange(
                    "p (kb j i) -> p kb j i", kb=8, j=4)[:, :, jj, :]
                last = P.op("sync", lambda e, o=kout,
                            i=kin.rearrange("p (kb i) -> p kb i", kb=8):
                            e.dma_start(out=o, in_=i), f"KV{par}", 16)
                vin = kv_co[jj * 2048 + 1024: jj * 2048 + 2048, kvh * 128:(kvh + 1) * 128]
                vout = xn_pool[:, 8192 + par * 4096: 8192 + (par + 1) * 4096].rearrange(
                    "p (kb j dv) -> p kb j dv", kb=8, j=4)[:, :, jj, :]
                last = P.op("sync", lambda e, o=vout,
                            i=vin.rearrange("(kb p) dv -> p kb dv", p=128):
                            e.dma_start(out=o, in_=i), f"KV{par}", 16)
            return last

        kld = [0] * HKV
        kld[0] = load_kv(0, 0)

        SCB = [0, 1, 6]
        PTB = [2, 3, 7]
        st5 = {
            "sc_rr": 0,
            "sc_exp": [0, 0, 0],
            "sums_read": [0, 0, 0, 0],
            "last_exp": 0,
            "pt_evac": [("DV", 0), ("DV", 0), ("DV", 0)],
            "ptsb_free": [0, 0, 0],
            "pt_rr": 0,
            "av_evac": [0, 0],
            "tr_last": 0,
            "pb7_free": 0,
            "region_free": {},
            "pbuf_free": [0, 0],
            "norm_done": {},
        }

        def praw(upar, hpp):
            off = 16384 + upar * 8192 + hpp * 4096
            return xn_pool[:, off:off + 4096]

        def emit_unit(kvh, qt, pair, upar, K_sb, pending):
            heads = (kvh * GRP + pair * 2, kvh * GRP + pair * 2 + 1)
            nsc = 2 * (qt + 1)
            ci = 0
            emitted = 0
            rcpv = [0, 0]
            exp_last = [0, 0]
            for kb in range(qt + 1):
                for hpp in range(2):
                    h = heads[hpp]
                    slot = st5["sc_rr"]
                    st5["sc_rr"] = (slot + 1) % 3
                    bank = pb[SCB[slot]]
                    P.wait("tensor", "AC", st5["sc_exp"][slot])
                    rf = st5["region_free"].pop(SCB[slot], 0)
                    P.wait("tensor", "DV", rf)
                    if kb == qt:
                        P.op("tensor", lambda e, b=bank, h=h, qt=qt, kb=kb, K=K_sb: e.matmul(
                            b[:], qa_pool[:, h * 1024 + qt * 128: h * 1024 + (qt + 1) * 128],
                            K[:, kb * 512:(kb + 1) * 512], start=True, stop=False))
                        mm = P.op("tensor", lambda e, b=bank: e.matmul(
                            b[:], ident[:], mask[:], start=False, stop=True), "PE")
                    else:
                        mm = P.op("tensor", lambda e, b=bank, h=h, qt=qt, kb=kb, K=K_sb: e.matmul(
                            b[:], qa_pool[:, h * 1024 + qt * 128: h * 1024 + (qt + 1) * 128],
                            K[:, kb * 512:(kb + 1) * 512], start=True, stop=True), "PE")
                    hp = pair * 2 + hpp
                    Pb = praw(upar, hpp)
                    P.wait("scalar", "PE", mm)
                    if kb == 0:
                        P.wait("scalar", "DV", st5["bias_done"])
                        P.wait("scalar", "PE", st5["pbuf_free"][upar])
                    if kb == 0:
                        P.wait("scalar", "DV", st5["sums_read"][hp])
                    a = P.op("scalar", lambda e, b=bank, P_=Pb, kb=kb, hp=hp: e.activation(
                        out=P_[:, kb * 512:(kb + 1) * 512], in_=b[:], func=AF.Exp,
                        scale=ISQ, bias=bias4[:, hp:hp + 1],
                        accum_out=sums4[:, hp * 8 + kb: hp * 8 + kb + 1]), "AC")
                    st5["sc_exp"][slot] = a
                    st5["last_exp"] = a
                    exp_last[hpp] = a
                    emitted += 1
                    dly = min(4, nsc - 1)
                    want = 0 if emitted <= dly else (len(pending) * (emitted - dly)) // (nsc - dly)
                    while ci < want:
                        pending[ci]()
                        ci += 1
            while ci < len(pending):
                pending[ci]()
                ci += 1
            nrm = 0
            for hpp in range(2):
                hp = pair * 2 + hpp
                Pb = praw(upar, hpp)
                P.wait("vector", "AC", exp_last[hpp])
                v = P.op("vector", lambda e, hp=hp, qt=qt: e.reduce_sum(
                    tot4[:, hp:hp + 1], sums4[:, hp * 8: hp * 8 + qt + 1], axis=AX.X), "DV")
                st5["sums_read"][hp] = v
                P.wait("vector", "DV", v)
                v = P.op("vector", lambda e, hp=hp: e.reciprocal(
                    out=rcp4[:, hp:hp + 1], in_=tot4[:, hp:hp + 1]), "DV")
                P.wait("vector", "DV", v)
                nrm = P.op("vector", lambda e, P_=Pb, qt=qt, hp=hp: e.tensor_scalar_mul(
                    P_[:, 0:(qt + 1) * 512], P_[:, 0:(qt + 1) * 512], rcp4[:, hp:hp + 1]), "DV")
            st5["norm_done"][(qt, pair)] = nrm

        def trav_chunks(kvh, qt, pair, upar, V_sb):
            heads = (kvh * GRP + pair * 2, kvh * GRP + pair * 2 + 1)
            av_bank = pb[4 + upar]
            state = {"first": True}

            def mkT(kb, jj, lastT):
                def emit():
                    reg = st5["pt_rr"]
                    st5["pt_rr"] = (reg + 1) % 3
                    ptbank = pb[PTB[reg]]
                    if kb == 0 and jj == 0:
                        P.wait("tensor", "DV", st5["norm_done"].pop((qt, pair)))
                        P.wait("tensor", "DV", st5["av_evac"][upar])
                    pe_sem, pe_val = st5["pt_evac"][reg]
                    P.wait("tensor", pe_sem, pe_val)
                    rf = st5["region_free"].pop(PTB[reg], 0)
                    P.wait("tensor", "DV", rf)
                    for hpp in range(2):
                        mm = P.op("tensor",
                                  lambda e, b=ptbank, kb=kb, jj=jj, hpp=hpp: e.matmul(
                                      b[:, hpp * 128:hpp * 128 + 128],
                                      praw(upar, hpp)[:, kb * 512 + jj * 128:
                                                      kb * 512 + jj * 128 + 128],
                                      ident[:], start=True, stop=True),
                                  "PE" if hpp == 1 else None)
                    if lastT:
                        st5["pbuf_free"][upar] = mm
                    return reg, mm
                return emit

            def mkAV(kb, jj, last):
                def emit(treg_mm):
                    reg, mm = treg_mm
                    eng, sem = ("vector", "DV")
                    P.wait(eng, "PE", mm)
                    P.wait(eng, "PE", st5["ptsb_free"][reg])
                    aev = P.op("vector", lambda e, b=pb[PTB[reg]], r=reg: e.tensor_copy(
                        out=ptsb[:, r * 256:(r + 1) * 256], in_=b[:, 0:256]), "DV")
                    st5["pt_evac"][reg] = (sem, aev)
                    P.wait("tensor", sem, aev)
                    mm = P.op("tensor", lambda e, b=av_bank, V=V_sb, kb=kb, jj=jj,
                              st=state["first"], sp=last, r=reg: e.matmul(
                        b[:, 0:256], V[:, (kb * 4 + jj) * 128:(kb * 4 + jj) * 128 + 128],
                        ptsb[:, r * 256:(r + 1) * 256], start=st, stop=sp), "PE")
                    st5["ptsb_free"][reg] = mm
                    state["first"] = False
                    if last:
                        st5["tr_last"] = mm
                        if qt == NT - 1 and pair == 1:
                            kv_consumed[kvh] = mm
                        P.wait("vector", "PE", mm)
                        for hpp in range(2):
                            h = heads[hpp]
                            ae = P.op("vector", lambda e, b=av_bank, hpp=hpp, h=h, qt=qt: e.tensor_copy(
                                out=qa_pool[:, h * 1024 + qt * 128: h * 1024 + (qt + 1) * 128],
                                in_=b[:, hpp * 128:hpp * 128 + 128]), "DV" if hpp == 1 else None)
                        st5["av_evac"][upar] = ae
                return emit

            n = (qt + 1) * 4
            parts = []
            i = 0
            for kb in range(qt + 1):
                for jj in range(4):
                    parts.append((mkT(kb, jj, i == n - 1), mkAV(kb, jj, i == n - 1)))
                    i += 1
            D = min(3, n)
            hold = {}

            def mk_step(do_T, do_AV):
                def emit():
                    if do_AV is not None:
                        parts[do_AV][1](hold.pop(do_AV))
                    if do_T is not None:
                        hold[do_T] = parts[do_T][0]()
                return emit

            sched = [mk_step(i, None) for i in range(D)]
            for i in range(D, n):
                sched.append(mk_step(i, i - D))
            for i in range(n - D, n):
                sched.append(mk_step(None, i))
            return sched

        REG = [0, 1, 2, 3, 6, 7]

        def prework_a(kvh, K_sb):
            P.wait("gpsimd", f"KV{kvh % 2}", kld[kvh])
            P.wait("gpsimd", "PE", st5["tr_last"])
            pk = P.op("gpsimd", lambda e, K=K_sb: e.tensor_tensor(
                out=ksink[:], in0=K, in1=K, op=ALU.mult), "PO")
            P.wait("tensor", "PO", pk)
            P.wait("tensor", "AC", st5["last_exp"])
            for pe_sem, pe_val in st5["pt_evac"]:
                P.wait("tensor", pe_sem, pe_val)
            P.wait("tensor", "DV", st5["pb7_free"])
            kred = [0] * 8
            for kb in range(8):
                reg = REG[kb % 6]
                if kb >= 6:
                    P.wait("tensor", "DV", kred[kb - 6])
                mm = P.op("tensor", lambda e, b=pb[reg], kb=kb: e.matmul(
                    b[0:1, 0:512], onec[:], ksink[:, kb * 512:(kb + 1) * 512],
                    start=True, stop=True), "PE")
                P.wait("vector", "PE", mm)
                kred[kb] = P.op("vector", lambda e, kb=kb: e.reduce_max(
                    kmrow[0:1, kb:kb + 1], pb[REG[kb % 6]][0:1, 0:512], axis=AX.X), "DV")
            P.wait("vector", "DV", kred[7])
            km_v = P.op("vector", lambda e: e.reduce_max(
                kmrow[0:1, 8:9], kmrow[0:1, 0:8], axis=AX.X), "DV")
            P.wait("gpsimd", "PE", mm)
            pq = P.op("gpsimd", lambda e, kvh=kvh: e.tensor_tensor(
                out=ksink[:], in0=qa_pool[:, kvh * 4096:(kvh + 1) * 4096],
                in1=qa_pool[:, kvh * 4096:(kvh + 1) * 4096], op=ALU.mult), "PO")
            P.wait("tensor", "PO", pq)
            qred = [0] * 8
            for r in range(8):
                reg = REG[r % 6]
                P.wait("tensor", "DV", kred[r % 6] if r < 6 else qred[r - 6])
                if r < 2:
                    P.wait("tensor", "DV", kred[6 + r])
                mm = P.op("tensor", lambda e, b=pb[reg], r=r: e.matmul(
                    b[0:1, 0:512], onec[:], ksink[:, r * 512:(r + 1) * 512],
                    start=True, stop=True), "PE")
                P.wait("vector", "PE", mm)
                qred[r] = P.op("vector", lambda e, r=r: e.reduce_max(
                    kmrow[0:1, 16 + r:17 + r], pb[REG[r % 6]][0:1, 0:512], axis=AX.X), "DV")
            for i, reg in enumerate(REG):
                st5["region_free"][reg] = qred[i]
            st5["region_free"][0] = max(qred[6], st5["region_free"][0])
            st5["region_free"][1] = max(qred[7], st5["region_free"][1])
            return km_v, qred

        def prework_b(kvh, km_v, qred):
            chain_done = 0
            for hp in range(4):
                P.wait("vector", "DV", max(qred[2 * hp], qred[2 * hp + 1], km_v))
                v = P.op("vector", lambda e, hp=hp: e.tensor_max(
                    kmrow[0:1, 24 + hp:25 + hp], kmrow[0:1, 16 + 2 * hp:17 + 2 * hp],
                    kmrow[0:1, 17 + 2 * hp:18 + 2 * hp]), "DV")
                P.wait("vector", "DV", v)
                v = P.op("vector", lambda e, hp=hp: e.tensor_mul(
                    kmrow[0:1, 9 + hp:10 + hp], kmrow[0:1, 8:9],
                    kmrow[0:1, 24 + hp:25 + hp]), "DV")
                P.wait("scalar", "DV", v)
                aa = P.op("scalar", lambda e, hp=hp: e.activation(
                    out=kmrow[0:1, 28 + hp:29 + hp], in_=kmrow[0:1, 9 + hp:10 + hp],
                    func=AF.Sqrt, scale=1.2155), "AC")
                P.wait("vector", "AC", aa)
                v = P.op("vector", lambda e, hp=hp: e.tensor_scalar_mul(
                    kmrow[0:1, 9 + hp:10 + hp], kmrow[0:1, 28 + hp:29 + hp], -ISQ), "DV")
                P.wait("vector", "DV", v)
                v = P.op("vector", lambda e, hp=hp: e.tensor_copy(
                    out=trowb[0:1, 1 + hp:2 + hp], in_=kmrow[0:1, 9 + hp:10 + hp]), "DV")
                chain_done = v
            P.wait("tensor", "DV", chain_done)
            P.wait("tensor", "DV", st5["region_free"].get(7, 0))
            pe_sem, pe_val = st5["pt_evac"][2]
            P.wait("tensor", pe_sem, pe_val)
            for hp in range(4):
                mmb = P.op("tensor", lambda e, hp=hp: e.matmul(
                    pb[7][:, 2 + hp:3 + hp], oner[:], trowb[0:1, 1 + hp:2 + hp],
                    start=True, stop=True), "PE" if hp == 3 else None)
            P.wait("vector", "PE", mmb)
            P.wait("vector", "AC", st5["last_exp"])
            bc = P.op("vector", lambda e: e.tensor_copy(out=bias4[:], in_=pb[7][:, 2:6]), "DV")
            st5["pb7_free"] = bc
            st5["region_free"][7] = max(st5["region_free"].get(7, 0), bc)
            st5["bias_done"] = bc

        uidx = 0
        for kvh in range(HKV):
            par = kvh % 2
            K_sb = xn_pool[:, par * 4096:(par + 1) * 4096]
            V_sb = xn_pool[:, 8192 + par * 4096: 8192 + (par + 1) * 4096]
            if kvh + 1 < HKV:
                kld[kvh + 1] = load_kv(kvh + 1, kv_consumed[kvh - 1] if kvh >= 1 else 0)
            km_v, qred = prework_a(kvh, K_sb)
            prework_b(kvh, km_v, qred)
            pending = []
            for qt in range(NT):
                for pair in range(2):
                    upar = uidx % 2
                    emit_unit(kvh, qt, pair, upar, K_sb, pending)
                    pending = trav_chunks(kvh, qt, pair, upar, V_sb)
                    uidx += 1
            for c in pending:
                c()
            pending = []
        return P.val("AC")

    att_done = phase5()
    P.barrier()

    if DEBUG_OUTPUTS:
        for h in range(HQ):
            P.op("sync", lambda e, h=h: e.dma_start(
                out=dbg_att[h], in_=qa_pool[:, h * 1024:(h + 1) * 1024]), "DBG", 16)
        P.barrier()

    if BUILD_PHASES <= 5:
        P.finalize()
        return nc, es

    def phase6():
        for s in range(4):
            P.wait("sync", f"OS{s}", P.val(f"OS{s}"))
        return emit_outpass(
            ngroups=8, stop_ic=31,
            wsrc=lambda oc, g: wo_h[oc, g],
            lhs_fn=lambda ic, t: qa_pool[:, ic * 1024 + t * 128: ic * 1024 + (t + 1) * 128],
            add_src=y1_d, store_dst=y_out)

    phase6()
    for s in range(4):
        P.wait("sync", f"OS{s}", P.val(f"OS{s}"))
    P.finalize()
    return nc, es


def _host_prep(inputs):
    x = np.asarray(inputs["L0_attn_hidden_states"], np.float32)
    wq = np.asarray(inputs["L0_attn_wq"], np.float32)
    wk = np.asarray(inputs["L0_attn_wk"], np.float32)
    wv = np.asarray(inputs["L0_attn_wv"], np.float32)
    wo = np.asarray(inputs["L0_attn_wo"], np.float32)
    worg = np.asarray(inputs["L0_ffn_W_org"], np.float32)
    ldown = np.asarray(inputs["L0_ffn_lora_down"], np.float32)
    lup = np.asarray(inputs["L0_ffn_lora_up"], np.float32)
    lkeys = np.asarray(inputs["L0_ffn_lora_keys"], np.float32)
    wqm = np.asarray(inputs["L0_ffn_Wq"], np.float32)

    wq_h = np.ascontiguousarray(
        wq.reshape(HQ, 128, 32, 128).transpose(0, 3, 2, 1).reshape(HQ, 128, D))
    wk_h = np.ascontiguousarray(
        wk.reshape(HKV, 128, 32, 128).transpose(0, 3, 2, 1).reshape(HKV, 128, D))
    wv_h = np.ascontiguousarray(
        wv.reshape(2, 512, 8, 4, 128).transpose(0, 2, 4, 3, 1).reshape(2, 8, 128, 2048))
    wo_h = np.ascontiguousarray(
        wo.reshape(8, 512, 8, 4, 128).transpose(0, 2, 4, 3, 1).reshape(8, 8, 128, 2048))
    worg_h = np.ascontiguousarray(
        worg.reshape(8, 512, 8, 4, 128).transpose(0, 2, 4, 3, 1).reshape(8, 8, 128, 2048))
    dmat = ldown.transpose(1, 0, 2).reshape(D, R)
    dmat_h = np.ascontiguousarray(
        dmat.reshape(32, 128, 4, 128).transpose(2, 1, 0, 3).reshape(4, 128, 32 * 128))
    umat = lup.reshape(R, D)
    umat_h = np.ascontiguousarray(
        umat.reshape(4, 128, 8, 512).transpose(2, 1, 0, 3).reshape(8, 128, 2048))
    wqm_h = np.ascontiguousarray(
        wqm.reshape(128, 32, 128).transpose(2, 1, 0).reshape(128, 32 * 128))
    keys_t = np.ascontiguousarray(lkeys.T)

    ident = np.eye(128, dtype=bfdt)
    onec = np.ones((128, 1), dtype=bfdt)
    oner = np.ones((1, 128), dtype=bfdt)
    one1 = np.ones((1, 1), dtype=bfdt)
    sel = np.zeros((NE, 512), dtype=np.float32)
    for rc in range(4):
        for p in range(128):
            sel[2 * rc + p // 64, rc * 128 + p] = 1.0
    sel = sel.astype(bfdt)

    inv_freq = 1.0 / (10000.0 ** (np.arange(0, DH, 2, dtype=np.float32) / DH))

    in_maps = []
    for c in range(8):
        b, j = c // 4, c % 4
        xr = x[b].reshape(NT, 4, 128, D)[:, j]
        x_core = np.ascontiguousarray(xr.reshape(TL, D))
        pos = (np.arange(NT)[:, None] * 512 + j * 128 +
               np.arange(128)[None, :]).reshape(TL).astype(np.float32)
        ang = pos[None, :] * inv_freq[:, None]
        ropec_a = np.empty((128, TL), np.float32)
        ropes_a = np.empty((128, TL), np.float32)
        ropec_a[0:64] = np.cos(ang); ropec_a[64:128] = np.cos(ang)
        ropes_a[0:64] = -np.sin(ang); ropes_a[64:128] = np.sin(ang)
        qi = np.arange(128)[:, None]
        ji = np.arange(512)[None, :]
        mask_a = np.where(j * 128 + qi >= ji, 0.0, NEG).astype(bfdt)
        in_maps.append({
            "x_in": x_core, "wq_h": wq_h, "wk_h": wk_h, "wv_h": wv_h,
            "wo_h": wo_h, "worg_h": worg_h, "dmat_h": dmat_h, "umat_h": umat_h,
            "wqm_h": wqm_h, "keys_t": keys_t, "rope_c": ropec_a, "rope_s": ropes_a,
            "mask_in": mask_a, "ident_in": ident, "onec_in": onec, "oner_in": oner,
            "one1_in": one1, "sel_in": sel,
        })
    return in_maps


def _assemble(results):
    y = np.empty((B, S, D), np.float32)
    for c in range(8):
        b, j = c // 4, c % 4
        y[b].reshape(NT, 4, 128, D)[:, j] = results[c]["y_out"].reshape(NT, 128, D)
    return y


def kernel(**inputs):
    nc, es = build_program()
    in_maps = _host_prep(inputs)
    try:
        res = run_bass_kernel_spmd(nc, in_maps, core_ids=list(range(8)))
    finally:
        es.close()
    return _assemble(res.results)


# version 12 (final)
# speedup vs baseline: 1.8276x; 1.0001x over previous
import math
import numpy as np
import ml_dtypes
import concourse.bass as bass
import concourse.mybir as mybir
from concourse.bass_utils import run_bass_kernel_spmd
import concourse.bass_utils as _bu

if not getattr(_bu, "_ldw_opt_patched", False):
    _orig_run_command = _bu.run_command

    def _run_command_ldw(cmd, **kw):
        cmd = [c.replace("--enable-ldw-opt=false", "--enable-ldw-opt=true")
               if isinstance(c, str) else c for c in cmd]
        return _orig_run_command(cmd, **kw)

    _bu.run_command = _run_command_ldw
    _bu._ldw_opt_patched = True

F32 = mybir.dt.float32
BF16 = mybir.dt.bfloat16
AF = mybir.ActivationFunctionType
ALU = mybir.AluOpType
AX = mybir.AxisListType
bfdt = ml_dtypes.bfloat16

B, S, D = 2, 4096, 4096
HQ, HKV, DH = 32, 8, 128
GRP = HQ // HKV
NT = 8
TL = 1024
R = 512
NE, KD = 8, 128
EPS = 1e-5
ISQ = 1.0 / math.sqrt(DH)
NEG = -1e9

DEBUG_OUTPUTS = False
BUILD_PHASES = 99
PHASE2_CUT = "full"


class Prog:
    ENGS = ("sync", "gpsimd", "tensor", "vector", "scalar")
    ENG_SEMS = ("PE", "DV", "AC", "PO")
    DMA_SEMS = ("CONST", "XL0", "XL1", "XL2", "XL3", "KV0", "KV1", "MN", "MS",
                "KS0", "KS1", "VS0", "VS1", "OS0", "OS1", "OS2", "OS3",
                "WP0", "WP1", "WS0", "WS1", "WS2", "WS3", "WM", "DBG")
    CC_SEMS = ("CCM", "CCK")
    ALL_SEMS = ENG_SEMS + DMA_SEMS + CC_SEMS

    def __init__(self, nc, es):
        self.nc = nc
        self.q = {e: [] for e in self.ENGS}
        self.sem = {}
        self.cnt = {}
        self._es = es

    def mksem(self, name):
        self.sem[name] = self._es.enter_context(self.nc.semaphore(name))
        self.cnt[name] = 0

    def wait(self, eng, sem, val):
        if val <= 0:
            return
        h = self.sem[sem]
        self.q[eng].append(lambda e, h=h, v=val: e.wait_ge(h, v))

    def op(self, eng, fn, inc=None, n=1):
        if inc is None:
            self.q[eng].append(fn)
            return None
        self.cnt[inc] += n
        v = self.cnt[inc]
        h = self.sem[inc]
        self.q[eng].append(lambda e, fn=fn, h=h, n=n: fn(e).then_inc(h, n))
        return v

    def val(self, sem):
        return self.cnt[sem]

    def barrier(self):
        marks = [(s, self.val(s)) for s in self.ENG_SEMS + self.DMA_SEMS]
        for eng in self.ENGS:
            for sem, v in marks:
                self.wait(eng, sem, v)

    def finalize(self):
        with self.nc.Block() as block:
            for eng in self.ENGS:
                ops = self.q[eng]

                def run(e, ops=ops):
                    for f in ops:
                        f(e)

                getattr(block, eng)(run)


def build_program():
    from contextlib import ExitStack

    nc = bass.Bass("TRN2", target_bir_lowering=False, debug=False)
    es = ExitStack()

    def dram(name, shape, dtype, kind):
        return nc.dram_tensor(name, shape, dtype, kind=kind).ap()

    x_in = dram("x_in", [TL, D], F32, "ExternalInput")
    wq_h = dram("wq_h", [HQ, 128, D], F32, "ExternalInput")
    wk_h = dram("wk_h", [HKV, 128, D], F32, "ExternalInput")
    wv_h = dram("wv_h", [2, 8, 128, 2048], F32, "ExternalInput")
    wo_h = dram("wo_h", [8, 8, 128, 2048], F32, "ExternalInput")
    worg_h = dram("worg_h", [8, 8, 128, 2048], F32, "ExternalInput")
    dmat_h = dram("dmat_h", [4, 128, 32 * 128], F32, "ExternalInput")
    umat_h = dram("umat_h", [8, 128, 2048], F32, "ExternalInput")
    wqm_h = dram("wqm_h", [128, 32 * 128], F32, "ExternalInput")
    keys_t = dram("keys_t", [KD, NE], F32, "ExternalInput")
    rope_c = dram("rope_c", [128, TL], F32, "ExternalInput")
    rope_s = dram("rope_s", [128, TL], F32, "ExternalInput")
    mask_in = dram("mask_in", [128, 512], BF16, "ExternalInput")
    ident_in = dram("ident_in", [128, 128], BF16, "ExternalInput")
    onec_in = dram("onec_in", [128, 1], BF16, "ExternalInput")
    oner_in = dram("oner_in", [1, 128], BF16, "ExternalInput")
    one1_in = dram("one1_in", [1, 1], BF16, "ExternalInput")
    sel_in = dram("sel_in", [NE, 512], BF16, "ExternalInput")

    y_out = dram("y_out", [TL, D], F32, "ExternalOutput")

    kv_ci = dram("kv_ci", [2048, 1024], BF16, "Internal")
    kv_co = dram("kv_co", [4 * 2048, 1024], BF16, "Internal")
    mn_ci = dram("mn_ci", [128, 32], F32, "Internal")
    mn_co = dram("mn_co", [128, 32], F32, "Internal")
    y1_d = dram("y1_d", [TL, D], F32, "Internal")
    RG = [[0, 1, 2, 3], [4, 5, 6, 7]]

    if DEBUG_OUTPUTS:
        dbg_xn = dram("dbg_xn", [32, 128, TL], BF16, "ExternalOutput")
        dbg_coef = dram("dbg_coef", [128, 4], F32, "ExternalOutput")
        dbg_q = dram("dbg_q", [HQ, 128, TL], BF16, "ExternalOutput")
        dbg_kv = dram("dbg_kv", [4 * 2048, 1024], BF16, "ExternalOutput")
        dbg_att = dram("dbg_att", [HQ, 128, TL], BF16, "ExternalOutput")
        dbg_y1 = dram("dbg_y1", [TL, D], F32, "ExternalOutput")

    sb = lambda name, shape, dt: es.enter_context(nc.sbuf_tensor(name, shape, dt))
    xn_pool = sb("xn_pool", [128, 32768], BF16)
    qa_pool = sb("qa_pool", [128, 32768], BF16)
    f32_pool = sb("f32_pool", [128, 8192], F32)
    wpool = sb("wpool", [128, 8192], BF16)
    ptsb = sb("ptsb", [128, 1024], BF16)
    stage = sb("stage", [128, 1536], BF16)
    ident = sb("ident", [128, 128], BF16)
    onec = sb("onec", [128, 1], BF16)
    oner = sb("oner", [1, 128], BF16)
    one1 = sb("one1", [1, 1], BF16)
    mask = sb("mask", [128, 512], BF16)
    ropec = sb("ropec", [128, TL], F32)
    ropes = sb("ropes", [128, TL], F32)
    selm = sb("selm", [NE, 512], BF16)
    keyst = sb("keyst", [KD, NE], BF16)
    ra = f32_pool[:, 0:512]
    rb = f32_pool[:, 512:1024]
    rt1 = f32_pool[:, 1024:1536]
    rt2 = f32_pool[:, 1536:2048]
    colsum = sb("colsum", [128, 32], F32)
    meansb = sb("meansb", [128, 32], F32)
    meanbf = sb("meanbf", [128, 32], BF16)
    stat = sb("stat", [128, 16], F32)
    sums4 = sb("sums4", [128, 32], F32)
    tot4 = sb("tot4", [128, 4], F32)
    rcp4 = sb("rcp4", [128, 4], F32)
    bias4 = sb("bias4", [128, 4], F32)
    scol = sb("scol", [128, 4], F32)
    trowb = sb("trowb", [1, 128], BF16)
    kmrow = sb("kmrow", [1, 32], F32)

    pb = [es.enter_context(nc.psum_tensor(f"pb{i}", [128, 512], F32)) for i in range(8)]

    P = Prog(nc, es)
    for s in Prog.ALL_SEMS:
        P.mksem(s)

    for dst, src in ((ident, ident_in), (onec, onec_in), (oner, oner_in),
                     (one1, one1_in), (mask, mask_in), (ropec, rope_c),
                     (ropes, rope_s), (selm, sel_in)):
        P.op("sync", lambda e, d=dst, s=src: e.dma_start(out=d[:], in_=s[:]), "CONST", 16)
    P.op("gpsimd", lambda e: e.dma_start(out=keyst[:], in_=keys_t[:]), "WM", 16)
    P.barrier()

    def phase1():
        tp_free = [0, 0]
        norm_done = [0] * NT
        stage_free = [0, 0]
        half_done = [0, 0]
        xld = [0] * NT
        for qt in range(NT):
            par = qt % 2
            xbuf = f32_pool[:, 4096:8192]
            st = lambda i, par=par: stat[:, par * 8 + i: par * 8 + i + 1]
            if qt >= 1:
                P.wait("sync", "AC", norm_done[qt - 1])
            xld[qt] = P.op("sync", lambda e, d=xbuf, q=qt: e.dma_start(
                out=d, in_=x_in[q * 128:(q + 1) * 128, :]), "XL0", 16)
            P.wait("vector", "XL0", xld[qt])
            sink = qa_pool[:, 8192:12288]
            P.wait("scalar", "XL0", xld[qt])
            a0 = P.op("scalar", lambda e, x=xbuf, st=st, s=sink: e.activation(
                out=s, in_=x, func=AF.Square, accum_out=st(2)), "AC")
            v = P.op("vector", lambda e, x=xbuf, st=st: e.reduce_sum(
                st(0), x, axis=AX.X, negate=True), "DV")
            P.wait("vector", "DV", v)
            v = P.op("vector", lambda e, st=st: e.tensor_scalar_mul(st(1), st(0), 1.0 / D), "DV")
            P.wait("vector", "DV", v)
            v = P.op("vector", lambda e, st=st: e.tensor_mul(st(3), st(1), st(1)), "DV")
            P.wait("vector", "AC", a0)
            P.wait("vector", "DV", v)
            v = P.op("vector", lambda e, st=st: e.tensor_scalar(
                st(4), st(2), 1.0 / D, EPS, ALU.mult, ALU.add), "DV")
            P.wait("vector", "DV", v)
            v = P.op("vector", lambda e, st=st: e.tensor_sub(st(5), st(4), st(3)), "DV")
            P.wait("scalar", "DV", v)
            a = P.op("scalar", lambda e, st=st: e.activation(out=st(6), in_=st(5), func=AF.Sqrt), "AC")
            P.wait("vector", "AC", a)
            v = P.op("vector", lambda e, st=st: e.reciprocal(out=st(7), in_=st(6)), "DV")
            P.wait("vector", "DV", v)
            v = P.op("vector", lambda e, st=st: e.tensor_mul(st(3), st(1), st(7)), "DV")
            xst = qa_pool[:, par * 4096:(par + 1) * 4096]
            P.wait("scalar", "DV", v)
            P.wait("scalar", "PE", stage_free[par])
            a = P.op("scalar", lambda e, x=xbuf, o=xst, st=st: e.activation(
                out=o, in_=x, func=AF.Identity, scale=st(7), bias=st(3)), "AC")
            norm_done[qt] = a
            for g in range(8):
                bank = pb[g % 2]
                P.wait("tensor", "AC", a)
                P.wait("tensor", "DV", tp_free[g % 2])
                for u in range(4):
                    dc = g * 4 + u
                    pe_v = P.op("tensor", lambda e, b=bank, u=u, x=xst, dc=dc: e.matmul(
                        b[:, u * 128:(u + 1) * 128], x[:, dc * 128:(dc + 1) * 128],
                        ident[:], start=True, stop=True), "PE" if u == 3 else None)
                P.wait("vector", "PE", pe_v)
                for u in range(4):
                    dc = g * 4 + u
                    dst = xn_pool[:, dc * 1024 + qt * 128: dc * 1024 + (qt + 1) * 128]
                    ev = P.op("vector", lambda e, b=bank, u=u, d=dst: e.tensor_copy(
                        out=d, in_=b[:, u * 128:(u + 1) * 128]), "DV" if u == 3 else None)
                tp_free[g % 2] = ev
            stage_free[par] = pe_v
            if qt == 3 or qt == 7:
                half_done[qt // 4] = max(tp_free)
        P.wait("vector", "DV", max(tp_free[0], tp_free[1]))
        for dc in range(32):
            cv = P.op("vector", lambda e, dc=dc: e.reduce_sum(
                colsum[:, dc:dc + 1], xn_pool[:, dc * 1024:(dc + 1) * 1024], axis=AX.X),
                "DV" if dc == 31 else None)
        return cv, half_done

    colsum_dv, xn_half = phase1()

    if DEBUG_OUTPUTS:
        P.wait("sync", "DV", colsum_dv)
        for dc in range(32):
            P.op("sync", lambda e, dc=dc: e.dma_start(
                out=dbg_xn[dc], in_=xn_pool[:, dc * 1024:(dc + 1) * 1024]), "DBG", 16)

    if BUILD_PHASES <= 1:
        P.finalize()
        return nc, es

    rope_state = {"p1": 0, "v4": 0, "stage_st": {}}

    def rope_evac(bank, th, dst_bf, pe_v, extra_dst_wait=None):
        C = ropec[:, th * 512:(th + 1) * 512]
        Ssg = ropes[:, th * 512:(th + 1) * 512]
        P.wait("vector", "PE", pe_v)
        P.wait("vector", "DV", rope_state["p1"])
        v1 = P.op("vector", lambda e, b=bank: e.tensor_copy(out=rb[64:128, :], in_=b[0:64, :]), "DV")
        v2 = P.op("vector", lambda e, b=bank: e.tensor_copy(out=rb[0:64, :], in_=b[64:128, :]), "DV")
        P.wait("scalar", "PE", pe_v)
        P.wait("scalar", "DV", max(v2, rope_state["v4"]))
        a1 = P.op("scalar", lambda e, b=bank: e.copy(out=ra[:], in_=b[:]), "AC")
        P.wait("vector", "AC", a1)
        v3 = P.op("vector", lambda e, C=C: e.tensor_mul(rt1[:], ra[:], C), "DV")
        P.wait("vector", "DV", v2)
        p1 = P.op("vector", lambda e, Ssg=Ssg: e.tensor_mul(rt2[:], rb[:], Ssg), "DV")
        rope_state["p1"] = p1
        P.wait("vector", "DV", max(v3, p1))
        if extra_dst_wait is not None:
            sem, val = extra_dst_wait
            P.wait("vector", sem, val)
        v4 = P.op("vector", lambda e, d=dst_bf: e.tensor_add(d, rt1[:], rt2[:]), "DV")
        rope_state["v4"] = v4
        return v2, v4

    def phase2():
        wfree = [0, 0]
        bank_free = [0] * 4
        kst_by_th = {0: 0, 1: 0}
        for kvh in range(HKV):
            par = kvh % 2
            wbuf = wpool[:, par * 4096:(par + 1) * 4096]
            P.wait("gpsimd", "PE", wfree[par])
            wld = P.op("gpsimd", lambda e, d=wbuf, k=kvh: e.dma_start(
                out=d, in_=wk_h[k]), f"WP{par}", 16)
            P.wait("tensor", f"WP{par}", wld)
            for th in range(2):
                bslot = (kvh * 2 + th) % 4
                bank = pb[4 + bslot]
                P.wait("tensor", "DV", max(bank_free[bslot], xn_half[th]))
                for ic in range(32):
                    mm = P.op("tensor", lambda e, b=bank, w=wbuf, ic=ic, th=th: e.matmul(
                        b[:], w[:, ic * 128:(ic + 1) * 128],
                        xn_pool[:, ic * 1024 + th * 512: ic * 1024 + (th + 1) * 512],
                        start=(ic == 0), stop=(ic == 31)), "PE" if ic == 31 else None)
                if th == 1:
                    wfree[par] = mm
                dst = stage[:, th * 512:(th + 1) * 512]
                if PHASE2_CUT == "kmm":
                    P.wait("vector", "PE", mm)
                    done = P.op("vector", lambda e, b=bank, d=dst: e.tensor_copy(out=d, in_=b[:]), "DV")
                    bank_free[bslot] = done
                    continue
                cons, done = rope_evac(bank, th, dst, mm,
                                       extra_dst_wait=(f"KS{th}", kst_by_th[th]))
                bank_free[bslot] = done
                if PHASE2_CUT == "krope":
                    continue
                P.wait("sync", "DV", done)
                sv = P.op("sync", lambda e, k=kvh, th=th, d=dst: e.dma_start(
                    out=kv_ci[k * 128:(k + 1) * 128, th * 512:(th + 1) * 512], in_=d), f"KS{th}", 16)
                kst_by_th[th] = sv
        if PHASE2_CUT == "k":
            return 0
        P.wait("tensor", "DV", max(bank_free))
        P.wait("scalar", "KS0", P.val("KS0"))
        P.wait("scalar", "KS1", P.val("KS1"))
        vslot_free = [P.val("PE")] * 2
        vbank_free = [0] * NT
        vst_slot = [0, 0]
        for dvh in range(2):
            for g in range(8):
                slot = g % 2
                gbuf = wpool[:, slot * 2048:(slot + 1) * 2048]
                P.wait("gpsimd", "PE", vslot_free[slot])
                wv = P.op("gpsimd", lambda e, d=gbuf, dvh=dvh, g=g: e.dma_start(
                    out=d, in_=wv_h[dvh, g]), f"WS{slot}", 16)
                P.wait("tensor", f"WS{slot}", wv)
                for gi in range(4):
                    ic = g * 4 + gi
                    for t in range(NT):
                        if ic == 0:
                            P.wait("tensor", "AC", vbank_free[t])
                        mm = P.op("tensor", lambda e, b=pb[t], ic=ic, t=t, w=gbuf, gi=gi: e.matmul(
                            b[:], xn_pool[:, ic * 1024 + t * 128: ic * 1024 + (t + 1) * 128],
                            w[:, gi * 512:(gi + 1) * 512],
                            start=(ic == 0), stop=(ic == 31)), "PE" if t == NT - 1 else None)
                vslot_free[slot] = mm
            for t in range(NT):
                sl = t % 2
                vdst = stage[:, sl * 512:(sl + 1) * 512]
                P.wait("scalar", "PE", mm)
                P.wait("scalar", f"VS{sl}", vst_slot[sl])
                av = P.op("scalar", lambda e, b=pb[t], d=vdst: e.copy(out=d, in_=b[:]), "AC")
                vbank_free[t] = av
                P.wait("sync", "AC", av)
                vst_slot[sl] = P.op("sync", lambda e, t=t, dvh=dvh, d=vdst: e.dma_start(
                    out=kv_ci[1024 + t * 128:1024 + (t + 1) * 128, dvh * 512:(dvh + 1) * 512],
                    in_=d), f"VS{sl}", 16)
        if PHASE2_CUT == "kv":
            return 0, 0
        P.wait("sync", "DV", colsum_dv)
        stv = P.op("sync", lambda e: e.dma_start(out=mn_ci[:], in_=colsum[:]), "MS", 16)
        P.wait("gpsimd", "MS", stv)
        ar = P.op("gpsimd", lambda e: e.collective_compute(
            "AllReduce", ALU.add, RG, [mn_ci[:]], [mn_co[:]]), "CCM", 1)
        P.wait("gpsimd", "KS0", P.val("KS0"))
        P.wait("gpsimd", "KS1", P.val("KS1"))
        P.wait("gpsimd", "VS0", P.val("VS0"))
        P.wait("gpsimd", "VS1", P.val("VS1"))
        ag = P.op("gpsimd", lambda e: e.collective_compute(
            "AllGather", ALU.bypass, RG, [kv_ci[:]], [kv_co[:]]), "CCK", 1)
        return ag, ar

    kv_ag, mean_ar = phase2()
    P.barrier()

    if BUILD_PHASES <= 2:
        P.finalize()
        return nc, es

    def phase3():
        wfree = [0, 0]
        bank_free = [0] * 4
        for h in range(HQ):
            par = h % 2
            wbuf = wpool[:, par * 4096:(par + 1) * 4096]
            P.wait("gpsimd", "PE", wfree[par])
            wld = P.op("gpsimd", lambda e, d=wbuf, h=h: e.dma_start(
                out=d, in_=wq_h[h]), f"WP{par}", 16)
            P.wait("tensor", f"WP{par}", wld)
            for th in range(2):
                bslot = (h * 2 + th) % 4
                bank = pb[4 + bslot]
                P.wait("tensor", "DV", bank_free[bslot])
                for ic in range(32):
                    mm = P.op("tensor", lambda e, b=bank, w=wbuf, ic=ic, th=th: e.matmul(
                        b[:], w[:, ic * 128:(ic + 1) * 128],
                        xn_pool[:, ic * 1024 + th * 512: ic * 1024 + (th + 1) * 512],
                        start=(ic == 0), stop=(ic == 31)), "PE" if ic == 31 else None)
                if th == 1:
                    wfree[par] = mm
                dst = qa_pool[:, h * 1024 + th * 512: h * 1024 + (th + 1) * 512]
                cons, done = rope_evac(bank, th, dst, mm)
                bank_free[bslot] = done
        return done

    q_done = phase3()
    P.barrier()

    if DEBUG_OUTPUTS:
        for h in range(HQ):
            P.op("sync", lambda e, h=h: e.dma_start(
                out=dbg_q[h], in_=qa_pool[:, h * 1024:(h + 1) * 1024]), "DBG", 16)
        P.wait("sync", "CCK", kv_ag)
        P.op("sync", lambda e: e.dma_start(out=dbg_kv[:], in_=kv_co[:]), "DBG", 16)
        P.barrier()

    if BUILD_PHASES <= 3:
        P.finalize()
        return nc, es

    def phase_coef():
        P.wait("sync", "CCM", mean_ar)
        mld = P.op("sync", lambda e: e.dma_start(out=meansb[:], in_=mn_co[:]), "MN", 16)
        P.wait("vector", "MN", mld)
        v = P.op("vector", lambda e: e.tensor_scalar_mul(meansb[:], meansb[:], 1.0 / S), "DV")
        P.wait("vector", "DV", v)
        v = P.op("vector", lambda e: e.tensor_copy(out=meanbf[:], in_=meansb[:]), "DV")
        wld = P.op("gpsimd", lambda e: e.dma_start(out=wpool[:, 0:4096], in_=wqm_h[:]), "WM", 16)
        P.wait("tensor", "DV", v)
        P.wait("tensor", "WM", wld)
        for ic in range(32):
            mm = P.op("tensor", lambda e, ic=ic: e.matmul(
                pb[7][0:1, 0:128], meanbf[:, ic:ic + 1],
                wpool[:, ic * 128:(ic + 1) * 128], start=(ic == 0), stop=(ic == 31)),
                "PE" if ic == 31 else None)
        P.wait("scalar", "PE", mm)
        a = P.op("scalar", lambda e: e.copy(out=trowb[:], in_=pb[7][0:1, 0:128]), "AC")
        P.wait("tensor", "AC", a)
        mm = P.op("tensor", lambda e: e.matmul(pb[6][:, 0:1], trowb[:], one1[:],
                                               start=True, stop=True), "PE")
        P.wait("vector", "PE", mm)
        v = P.op("vector", lambda e: e.tensor_copy(out=meanbf[:, 0:1], in_=pb[6][:, 0:1]), "DV")
        P.wait("tensor", "DV", v)
        mm = P.op("tensor", lambda e: e.matmul(pb[6][0:1, 16:24], meanbf[:, 0:1], keyst[:],
                                               start=True, stop=True), "PE")
        P.wait("vector", "PE", mm)
        v = P.op("vector", lambda e: e.tensor_scalar_mul(
            kmrow[0:1, 0:8], pb[6][0:1, 16:24], 1.0 / math.sqrt(KD)), "DV")
        P.wait("vector", "DV", v)
        v = P.op("vector", lambda e: e.reduce_max(kmrow[0:1, 8:9], kmrow[0:1, 0:8],
                                                  axis=AX.X, negate=True), "DV")
        P.wait("scalar", "DV", v)
        a = P.op("scalar", lambda e: e.activation(out=kmrow[0:1, 10:18], in_=kmrow[0:1, 0:8],
                                                  func=AF.Exp, bias=kmrow[0:1, 8:9], scale=1.0,
                                                  accum_out=kmrow[0:1, 9:10]), "AC")
        P.wait("vector", "AC", a)
        v = P.op("vector", lambda e: e.reciprocal(out=kmrow[0:1, 8:9], in_=kmrow[0:1, 9:10]), "DV")
        P.wait("vector", "DV", v)
        v = P.op("vector", lambda e: e.tensor_scalar_mul(
            kmrow[0:1, 10:18], kmrow[0:1, 10:18], kmrow[0:1, 8:9]), "DV")
        P.wait("vector", "DV", v)
        v = P.op("vector", lambda e: e.tensor_copy(out=trowb[0:1, 0:8], in_=kmrow[0:1, 10:18]), "DV")
        P.wait("tensor", "DV", v)
        mm = P.op("tensor", lambda e: e.matmul(pb[6][0:8, 8:9], trowb[0:1, 0:8], one1[:],
                                               start=True, stop=True), "PE")
        P.wait("vector", "PE", mm)
        v = P.op("vector", lambda e: e.tensor_copy(out=meanbf[0:8, 1:2], in_=pb[6][0:8, 8:9]), "DV")
        P.wait("tensor", "DV", v)
        for rc in range(4):
            mm = P.op("tensor", lambda e, rc=rc: e.matmul(
                pb[6][:, 2 + rc:3 + rc], selm[:, rc * 128:(rc + 1) * 128], meanbf[0:8, 1:2],
                start=True, stop=True), "PE" if rc == 3 else None)
        P.wait("vector", "PE", mm)
        v = P.op("vector", lambda e: e.tensor_copy(out=scol[:], in_=pb[6][:, 2:6]), "DV")
        return v

    coef_done = phase_coef()
    P.barrier()

    if DEBUG_OUTPUTS:
        P.op("sync", lambda e: e.dma_start(out=dbg_coef[:], in_=scol[:]), "DBG", 16)
        P.barrier()

    def emit_outpass(ngroups, stop_ic, wsrc, lhs_fn, add_src, store_dst,
                     init_bank_ev=None, init_slot_free=0, extra_mm_wait=None):
        wslot_free = [init_slot_free, init_slot_free]
        bank_ev = [0] * NT
        if init_bank_ev:
            for kk, vv in init_bank_ev.items():
                bank_ev[kk] = vv
        xl_free = [0] * 4
        out_free = [0] * 4
        xv = [0] * NT
        last_sv = 0
        def stage_load(oc, t):
            sl = t % 4
            xs = f32_pool[:, sl * 512:(sl + 1) * 512]
            P.wait("sync", "DV", xl_free[sl])
            xv[t] = P.op("sync", lambda e, d=xs, t=t, oc=oc, a=add_src: e.dma_start(
                out=d, in_=a[t * 128:(t + 1) * 128, oc * 512:(oc + 1) * 512]), f"XL{sl}", 16)

        for oc in range(8):
            for t in range(4):
                stage_load(oc, t)
            for g in range(ngroups):
                slot = g % 2
                gbuf = wpool[:, slot * 2048:(slot + 1) * 2048]
                P.wait("gpsimd", "PE", wslot_free[slot])
                wv = P.op("gpsimd", lambda e, d=gbuf, oc=oc, g=g: e.dma_start(
                    out=d, in_=wsrc(oc, g)), f"WS{slot}", 16)
                P.wait("tensor", f"WS{slot}", wv)
                for gi in range(4):
                    ic = g * 4 + gi
                    if ic > stop_ic:
                        break
                    if extra_mm_wait is not None and ic == extra_mm_wait[2] * 4:
                        if oc == 0:
                            P.wait("tensor", extra_mm_wait[0], extra_mm_wait[1])
                    for t in range(NT):
                        if ic == 0:
                            P.wait("tensor", "DV", bank_ev[t])
                        mm = P.op("tensor",
                                  lambda e, b=pb[t], l=lhs_fn(ic, t), w=gbuf, gi=gi, ic=ic: e.matmul(
                                      b[:], l, w[:, gi * 512:(gi + 1) * 512],
                                      start=(ic == 0), stop=(ic == stop_ic)),
                                  "PE" if t == NT - 1 else None)
                wslot_free[slot] = mm
            for t in range(NT):
                sl = t % 4
                xs = f32_pool[:, sl * 512:(sl + 1) * 512]
                od = f32_pool[:, 4096 + sl * 512: 4096 + (sl + 1) * 512]
                P.wait("vector", "PE", mm)
                P.wait("vector", f"XL{sl}", xv[t])
                P.wait("vector", f"OS{sl}", out_free[sl])
                ev = P.op("vector", lambda e, b=pb[t], x=xs, o=od: e.tensor_add(o, b[:], x), "DV")
                xl_free[sl] = ev
                bank_ev[t] = ev
                P.wait("sync", "DV", ev)
                sv = P.op("sync", lambda e, o=od, t=t, oc=oc, s=store_dst: e.dma_start(
                    out=s[t * 128:(t + 1) * 128, oc * 512:(oc + 1) * 512], in_=o), f"OS{sl}", 16)
                out_free[sl] = sv
                last_sv = sv
                if t + 4 < NT:
                    stage_load(oc, t + 4)
        return last_sv

    def phase4():
        h_sb = wpool[:, 4096:8192]
        hfree = [0, 0]
        last_rc_mm = 0
        for rc in range(4):
            P.wait("gpsimd", "PE", last_rc_mm)
            wld = P.op("gpsimd", lambda e, rc=rc: e.dma_start(
                out=wpool[:, 0:4096], in_=dmat_h[rc]), "WP0", 16)
            P.wait("tensor", "WP0", wld)
            for th in range(2):
                bank = pb[6 + th]
                P.wait("tensor", "DV", hfree[th])
                for ic in range(32):
                    mm = P.op("tensor", lambda e, b=bank, ic=ic, th=th: e.matmul(
                        b[:], wpool[:, ic * 128:(ic + 1) * 128],
                        xn_pool[:, ic * 1024 + th * 512: ic * 1024 + (th + 1) * 512],
                        start=(ic == 0), stop=(ic == 31)), "PE" if ic == 31 else None)
                last_rc_mm = mm
                P.wait("vector", "PE", mm)
                hv = P.op("vector", lambda e, b=bank, rc=rc, th=th: e.tensor_scalar_mul(
                    h_sb[:, rc * 1024 + th * 512: rc * 1024 + (th + 1) * 512],
                    b[:], scol[:, rc:rc + 1]), "DV")
                hfree[th] = hv
        h_done = hv
        return emit_outpass(
            ngroups=9, stop_ic=35,
            wsrc=lambda oc, g: worg_h[oc, g] if g < 8 else umat_h[oc],
            lhs_fn=lambda ic, t: (xn_pool[:, ic * 1024 + t * 128: ic * 1024 + (t + 1) * 128]
                                  if ic < 32 else
                                  h_sb[:, (ic - 32) * 1024 + t * 128: (ic - 32) * 1024 + (t + 1) * 128]),
            add_src=x_in, store_dst=y1_d,
            init_bank_ev={6: hfree[0], 7: hfree[1]},
            init_slot_free=last_rc_mm,
            extra_mm_wait=("DV", h_done, 8))

    y1_last = phase4()
    P.barrier()

    if DEBUG_OUTPUTS:
        P.op("sync", lambda e: e.dma_start(out=dbg_y1[:], in_=y1_d[:]), "DBG", 16)
        P.barrier()

    if BUILD_PHASES <= 4:
        P.finalize()
        return nc, es

    def phase5():
        ksink = xn_pool[:, 16384:16384 + 4096]
        P.wait("sync", "CCK", kv_ag)
        kv_consumed = [0] * HKV

        def load_kv(kvh, wait_pe_val):
            par = kvh % 2
            P.wait("sync", "PE", wait_pe_val)
            last = 0
            for jj in range(4):
                kin = kv_co[jj * 2048 + kvh * 128: jj * 2048 + (kvh + 1) * 128, :]
                kout = xn_pool[:, par * 4096:(par + 1) * 4096].rearrange(
                    "p (kb j i) -> p kb j i", kb=8, j=4)[:, :, jj, :]
                last = P.op("sync", lambda e, o=kout,
                            i=kin.rearrange("p (kb i) -> p kb i", kb=8):
                            e.dma_start(out=o, in_=i), f"KV{par}", 16)
                vin = kv_co[jj * 2048 + 1024: jj * 2048 + 2048, kvh * 128:(kvh + 1) * 128]
                vout = xn_pool[:, 8192 + par * 4096: 8192 + (par + 1) * 4096].rearrange(
                    "p (kb j dv) -> p kb j dv", kb=8, j=4)[:, :, jj, :]
                last = P.op("sync", lambda e, o=vout,
                            i=vin.rearrange("(kb p) dv -> p kb dv", p=128):
                            e.dma_start(out=o, in_=i), f"KV{par}", 16)
            return last

        kld = [0] * HKV
        kld[0] = load_kv(0, 0)

        SCB = [0, 1, 6]
        PTB = [2, 3, 7]
        st5 = {
            "sc_rr": 0,
            "sc_exp": [0, 0, 0],
            "sums_read": [0, 0, 0, 0],
            "last_exp": 0,
            "pt_evac": [("DV", 0), ("DV", 0), ("DV", 0)],
            "ptsb_free": [0, 0, 0],
            "pt_rr": 0,
            "av_evac": [0, 0],
            "tr_last": 0,
            "pb7_free": 0,
            "region_free": {},
            "pbuf_free": [0, 0],
            "norm_done": {},
        }

        def praw(upar, hpp):
            off = 16384 + upar * 8192 + hpp * 4096
            return xn_pool[:, off:off + 4096]

        def emit_unit(kvh, qt, pair, upar, K_sb, pending):
            heads = (kvh * GRP + pair * 2, kvh * GRP + pair * 2 + 1)
            nsc = 2 * (qt + 1)
            ci = 0
            emitted = 0
            rcpv = [0, 0]
            exp_last = [0, 0]
            for kb in range(qt + 1):
                for hpp in range(2):
                    h = heads[hpp]
                    slot = st5["sc_rr"]
                    st5["sc_rr"] = (slot + 1) % 3
                    bank = pb[SCB[slot]]
                    P.wait("tensor", "AC", st5["sc_exp"][slot])
                    rf = st5["region_free"].pop(SCB[slot], 0)
                    P.wait("tensor", "DV", rf)
                    if kb == qt:
                        P.op("tensor", lambda e, b=bank, h=h, qt=qt, kb=kb, K=K_sb: e.matmul(
                            b[:], qa_pool[:, h * 1024 + qt * 128: h * 1024 + (qt + 1) * 128],
                            K[:, kb * 512:(kb + 1) * 512], start=True, stop=False))
                        mm = P.op("tensor", lambda e, b=bank: e.matmul(
                            b[:], ident[:], mask[:], start=False, stop=True), "PE")
                    else:
                        mm = P.op("tensor", lambda e, b=bank, h=h, qt=qt, kb=kb, K=K_sb: e.matmul(
                            b[:], qa_pool[:, h * 1024 + qt * 128: h * 1024 + (qt + 1) * 128],
                            K[:, kb * 512:(kb + 1) * 512], start=True, stop=True), "PE")
                    hp = pair * 2 + hpp
                    Pb = praw(upar, hpp)
                    P.wait("scalar", "PE", mm)
                    if kb == 0:
                        P.wait("scalar", "DV", st5["bias_done"])
                        P.wait("scalar", "PE", st5["pbuf_free"][upar])
                    if kb == 0:
                        P.wait("scalar", "DV", st5["sums_read"][hp])
                    a = P.op("scalar", lambda e, b=bank, P_=Pb, kb=kb, hp=hp: e.activation(
                        out=P_[:, kb * 512:(kb + 1) * 512], in_=b[:], func=AF.Exp,
                        scale=ISQ, bias=bias4[:, hp:hp + 1],
                        accum_out=sums4[:, hp * 8 + kb: hp * 8 + kb + 1]), "AC")
                    st5["sc_exp"][slot] = a
                    st5["last_exp"] = a
                    exp_last[hpp] = a
                    emitted += 1
                    dly = min(4, nsc - 1)
                    want = 0 if emitted <= dly else (len(pending) * (emitted - dly)) // (nsc - dly)
                    while ci < want:
                        pending[ci]()
                        ci += 1
            while ci < len(pending):
                pending[ci]()
                ci += 1
            nrm = 0
            for hpp in range(2):
                hp = pair * 2 + hpp
                Pb = praw(upar, hpp)
                P.wait("vector", "AC", exp_last[hpp])
                v = P.op("vector", lambda e, hp=hp, qt=qt: e.reduce_sum(
                    tot4[:, hp:hp + 1], sums4[:, hp * 8: hp * 8 + qt + 1], axis=AX.X), "DV")
                st5["sums_read"][hp] = v
                P.wait("vector", "DV", v)
                v = P.op("vector", lambda e, hp=hp: e.reciprocal(
                    out=rcp4[:, hp:hp + 1], in_=tot4[:, hp:hp + 1]), "DV")
                P.wait("vector", "DV", v)
                nrm = P.op("vector", lambda e, P_=Pb, qt=qt, hp=hp: e.tensor_scalar_mul(
                    P_[:, 0:(qt + 1) * 512], P_[:, 0:(qt + 1) * 512], rcp4[:, hp:hp + 1]), "DV")
            st5["norm_done"][(qt, pair)] = nrm

        def trav_chunks(kvh, qt, pair, upar, V_sb):
            heads = (kvh * GRP + pair * 2, kvh * GRP + pair * 2 + 1)
            av_bank = pb[4 + upar]
            state = {"first": True}

            def mkT(kb, jj, lastT):
                def emit():
                    reg = st5["pt_rr"]
                    st5["pt_rr"] = (reg + 1) % 3
                    ptbank = pb[PTB[reg]]
                    if kb == 0 and jj == 0:
                        P.wait("tensor", "DV", st5["norm_done"].pop((qt, pair)))
                        P.wait("tensor", "DV", st5["av_evac"][upar])
                        P.wait("tensor", "DV", st5["region_free"].pop(4 + upar, 0))
                    pe_sem, pe_val = st5["pt_evac"][reg]
                    P.wait("tensor", pe_sem, pe_val)
                    rf = st5["region_free"].pop(PTB[reg], 0)
                    P.wait("tensor", "DV", rf)
                    for hpp in range(2):
                        mm = P.op("tensor",
                                  lambda e, b=ptbank, kb=kb, jj=jj, hpp=hpp: e.matmul(
                                      b[:, hpp * 128:hpp * 128 + 128],
                                      praw(upar, hpp)[:, kb * 512 + jj * 128:
                                                      kb * 512 + jj * 128 + 128],
                                      ident[:], start=True, stop=True),
                                  "PE" if hpp == 1 else None)
                    if lastT:
                        st5["pbuf_free"][upar] = mm
                    return reg, mm
                return emit

            def mkAV(kb, jj, last):
                def emit(treg_mm):
                    reg, mm = treg_mm
                    eng, sem = ("vector", "DV")
                    P.wait(eng, "PE", mm)
                    P.wait(eng, "PE", st5["ptsb_free"][reg])
                    aev = P.op("vector", lambda e, b=pb[PTB[reg]], r=reg: e.tensor_copy(
                        out=ptsb[:, r * 256:(r + 1) * 256], in_=b[:, 0:256]), "DV")
                    st5["pt_evac"][reg] = (sem, aev)
                    P.wait("tensor", sem, aev)
                    mm = P.op("tensor", lambda e, b=av_bank, V=V_sb, kb=kb, jj=jj,
                              st=state["first"], sp=last, r=reg: e.matmul(
                        b[:, 0:256], V[:, (kb * 4 + jj) * 128:(kb * 4 + jj) * 128 + 128],
                        ptsb[:, r * 256:(r + 1) * 256], start=st, stop=sp), "PE")
                    st5["ptsb_free"][reg] = mm
                    state["first"] = False
                    if last:
                        st5["tr_last"] = mm
                        if qt == NT - 1 and pair == 1:
                            kv_consumed[kvh] = mm
                        P.wait("vector", "PE", mm)
                        for hpp in range(2):
                            h = heads[hpp]
                            ae = P.op("vector", lambda e, b=av_bank, hpp=hpp, h=h, qt=qt: e.tensor_copy(
                                out=qa_pool[:, h * 1024 + qt * 128: h * 1024 + (qt + 1) * 128],
                                in_=b[:, hpp * 128:hpp * 128 + 128]), "DV" if hpp == 1 else None)
                        st5["av_evac"][upar] = ae
                return emit

            n = (qt + 1) * 4
            parts = []
            i = 0
            for kb in range(qt + 1):
                for jj in range(4):
                    parts.append((mkT(kb, jj, i == n - 1), mkAV(kb, jj, i == n - 1)))
                    i += 1
            D = min(3, n)
            hold = {}

            def mk_step(do_T, do_AV):
                def emit():
                    if do_AV is not None:
                        parts[do_AV][1](hold.pop(do_AV))
                    if do_T is not None:
                        hold[do_T] = parts[do_T][0]()
                return emit

            sched = [mk_step(i, None) for i in range(D)]
            for i in range(D, n):
                sched.append(mk_step(i, i - D))
            for i in range(n - D, n):
                sched.append(mk_step(None, i))
            return sched

        REG = [0, 1, 6, 4]

        def prework_a(kvh, K_sb, pending):
            n = len(pending)
            ci = [0]

            def drain(upto):
                while ci[0] < upto:
                    pending[ci[0]]()
                    ci[0] += 1

            par = kvh % 2
            P.wait("gpsimd", f"KV{par}", kld[kvh])
            P.wait("gpsimd", "PE", st5["pbuf_free"][0])
            pk = P.op("gpsimd", lambda e, K=K_sb: e.tensor_tensor(
                out=ksink[:], in0=K, in1=K, op=ALU.mult), "PO")
            drain(n // 3)
            P.wait("tensor", "PO", pk)
            P.wait("tensor", "AC", st5["last_exp"])
            P.wait("tensor", "DV", st5["av_evac"][0])
            kred = [0] * 8
            for kb in range(8):
                reg = REG[kb % 4]
                if kb >= 4:
                    P.wait("tensor", "DV", kred[kb - 4])
                mm = P.op("tensor", lambda e, b=pb[reg], kb=kb: e.matmul(
                    b[0:1, 0:512], onec[:], ksink[:, kb * 512:(kb + 1) * 512],
                    start=True, stop=True), "PE")
                P.wait("vector", "PE", mm)
                kred[kb] = P.op("vector", lambda e, kb=kb: e.reduce_max(
                    kmrow[0:1, kb:kb + 1], pb[REG[kb % 4]][0:1, 0:512], axis=AX.X), "DV")
            P.wait("vector", "DV", kred[7])
            km_v = P.op("vector", lambda e: e.reduce_max(
                kmrow[0:1, 8:9], kmrow[0:1, 0:8], axis=AX.X), "DV")
            P.wait("gpsimd", "PE", mm)
            pq = P.op("gpsimd", lambda e, kvh=kvh: e.tensor_tensor(
                out=ksink[:], in0=qa_pool[:, kvh * 4096:(kvh + 1) * 4096],
                in1=qa_pool[:, kvh * 4096:(kvh + 1) * 4096], op=ALU.mult), "PO")
            drain(2 * n // 3)
            P.wait("tensor", "PO", pq)
            qred = [0] * 8
            for r in range(8):
                reg = REG[r % 4]
                P.wait("tensor", "DV", kred[4 + r] if r < 4 else qred[r - 4])
                mm = P.op("tensor", lambda e, b=pb[reg], r=r: e.matmul(
                    b[0:1, 0:512], onec[:], ksink[:, r * 512:(r + 1) * 512],
                    start=True, stop=True), "PE")
                P.wait("vector", "PE", mm)
                qred[r] = P.op("vector", lambda e, r=r: e.reduce_max(
                    kmrow[0:1, 16 + r:17 + r], pb[REG[r % 4]][0:1, 0:512], axis=AX.X), "DV")
            drain(n)
            for i in range(4):
                st5["region_free"][REG[i]] = qred[4 + i]
            return km_v, qred

        def prework_a_old(kvh, K_sb):
            P.wait("gpsimd", f"KV{kvh % 2}", kld[kvh])
            P.wait("gpsimd", "PE", st5["tr_last"])
            pk = P.op("gpsimd", lambda e, K=K_sb: e.tensor_tensor(
                out=ksink[:], in0=K, in1=K, op=ALU.mult), "PO")
            P.wait("tensor", "PO", pk)
            P.wait("tensor", "AC", st5["last_exp"])
            for pe_sem, pe_val in st5["pt_evac"]:
                P.wait("tensor", pe_sem, pe_val)
            P.wait("tensor", "DV", st5["pb7_free"])
            kred = [0] * 8
            for kb in range(8):
                reg = REG[kb % 6]
                if kb >= 6:
                    P.wait("tensor", "DV", kred[kb - 6])
                mm = P.op("tensor", lambda e, b=pb[reg], kb=kb: e.matmul(
                    b[0:1, 0:512], onec[:], ksink[:, kb * 512:(kb + 1) * 512],
                    start=True, stop=True), "PE")
                P.wait("vector", "PE", mm)
                kred[kb] = P.op("vector", lambda e, kb=kb: e.reduce_max(
                    kmrow[0:1, kb:kb + 1], pb[REG[kb % 6]][0:1, 0:512], axis=AX.X), "DV")
            P.wait("vector", "DV", kred[7])
            km_v = P.op("vector", lambda e: e.reduce_max(
                kmrow[0:1, 8:9], kmrow[0:1, 0:8], axis=AX.X), "DV")
            P.wait("gpsimd", "PE", mm)
            pq = P.op("gpsimd", lambda e, kvh=kvh: e.tensor_tensor(
                out=ksink[:], in0=qa_pool[:, kvh * 4096:(kvh + 1) * 4096],
                in1=qa_pool[:, kvh * 4096:(kvh + 1) * 4096], op=ALU.mult), "PO")
            P.wait("tensor", "PO", pq)
            qred = [0] * 8
            for r in range(8):
                reg = REG[r % 6]
                P.wait("tensor", "DV", kred[r % 6] if r < 6 else qred[r - 6])
                if r < 2:
                    P.wait("tensor", "DV", kred[6 + r])
                mm = P.op("tensor", lambda e, b=pb[reg], r=r: e.matmul(
                    b[0:1, 0:512], onec[:], ksink[:, r * 512:(r + 1) * 512],
                    start=True, stop=True), "PE")
                P.wait("vector", "PE", mm)
                qred[r] = P.op("vector", lambda e, r=r: e.reduce_max(
                    kmrow[0:1, 16 + r:17 + r], pb[REG[r % 6]][0:1, 0:512], axis=AX.X), "DV")
            return km_v, qred

        def prework_b(kvh, km_v, qred):
            chain_done = 0
            for hp in range(4):
                P.wait("vector", "DV", max(qred[2 * hp], qred[2 * hp + 1], km_v))
                v = P.op("vector", lambda e, hp=hp: e.tensor_max(
                    kmrow[0:1, 24 + hp:25 + hp], kmrow[0:1, 16 + 2 * hp:17 + 2 * hp],
                    kmrow[0:1, 17 + 2 * hp:18 + 2 * hp]), "DV")
                P.wait("vector", "DV", v)
                v = P.op("vector", lambda e, hp=hp: e.tensor_mul(
                    kmrow[0:1, 9 + hp:10 + hp], kmrow[0:1, 8:9],
                    kmrow[0:1, 24 + hp:25 + hp]), "DV")
                P.wait("scalar", "DV", v)
                aa = P.op("scalar", lambda e, hp=hp: e.activation(
                    out=kmrow[0:1, 28 + hp:29 + hp], in_=kmrow[0:1, 9 + hp:10 + hp],
                    func=AF.Sqrt, scale=1.2155), "AC")
                P.wait("vector", "AC", aa)
                v = P.op("vector", lambda e, hp=hp: e.tensor_scalar_mul(
                    kmrow[0:1, 9 + hp:10 + hp], kmrow[0:1, 28 + hp:29 + hp], -ISQ), "DV")
                P.wait("vector", "DV", v)
                v = P.op("vector", lambda e, hp=hp: e.tensor_copy(
                    out=trowb[0:1, 1 + hp:2 + hp], in_=kmrow[0:1, 9 + hp:10 + hp]), "DV")
                chain_done = v
            P.wait("tensor", "DV", chain_done)
            P.wait("tensor", "DV", st5["region_free"].get(7, 0))
            pe_sem, pe_val = st5["pt_evac"][2]
            P.wait("tensor", pe_sem, pe_val)
            for hp in range(4):
                mmb = P.op("tensor", lambda e, hp=hp: e.matmul(
                    pb[7][:, 2 + hp:3 + hp], oner[:], trowb[0:1, 1 + hp:2 + hp],
                    start=True, stop=True), "PE" if hp == 3 else None)
            P.wait("vector", "PE", mmb)
            P.wait("vector", "AC", st5["last_exp"])
            bc = P.op("vector", lambda e: e.tensor_copy(out=bias4[:], in_=pb[7][:, 2:6]), "DV")
            st5["pb7_free"] = bc
            st5["region_free"][7] = max(st5["region_free"].get(7, 0), bc)
            st5["bias_done"] = bc

        uidx = 0
        km_v, qred = prework_a(0, xn_pool[:, 0:4096], [])
        prework_b(0, km_v, qred)
        for kvh in range(HKV):
            par = kvh % 2
            K_sb = xn_pool[:, par * 4096:(par + 1) * 4096]
            V_sb = xn_pool[:, 8192 + par * 4096: 8192 + (par + 1) * 4096]
            if kvh + 1 < HKV:
                kld[kvh + 1] = load_kv(kvh + 1, kv_consumed[kvh - 1] if kvh >= 1 else 0)
            pending = []
            for qt in range(NT):
                for pair in range(2):
                    upar = uidx % 2
                    emit_unit(kvh, qt, pair, upar, K_sb, pending)
                    pending = trav_chunks(kvh, qt, pair, upar, V_sb)
                    uidx += 1
            if kvh + 1 < HKV:
                nk = (kvh + 1) % 2
                km_v, qred = prework_a(
                    kvh + 1, xn_pool[:, nk * 4096:(nk + 1) * 4096], pending)
                prework_b(kvh + 1, km_v, qred)
            else:
                for c in pending:
                    c()
            pending = []
        return P.val("AC")

    att_done = phase5()
    P.barrier()

    if DEBUG_OUTPUTS:
        for h in range(HQ):
            P.op("sync", lambda e, h=h: e.dma_start(
                out=dbg_att[h], in_=qa_pool[:, h * 1024:(h + 1) * 1024]), "DBG", 16)
        P.barrier()

    if BUILD_PHASES <= 5:
        P.finalize()
        return nc, es

    def phase6():
        for s in range(4):
            P.wait("sync", f"OS{s}", P.val(f"OS{s}"))
        return emit_outpass(
            ngroups=8, stop_ic=31,
            wsrc=lambda oc, g: wo_h[oc, g],
            lhs_fn=lambda ic, t: qa_pool[:, ic * 1024 + t * 128: ic * 1024 + (t + 1) * 128],
            add_src=y1_d, store_dst=y_out)

    phase6()
    for s in range(4):
        P.wait("sync", f"OS{s}", P.val(f"OS{s}"))
    P.finalize()
    return nc, es


def _host_prep(inputs):
    x = np.asarray(inputs["L0_attn_hidden_states"], np.float32)
    wq = np.asarray(inputs["L0_attn_wq"], np.float32)
    wk = np.asarray(inputs["L0_attn_wk"], np.float32)
    wv = np.asarray(inputs["L0_attn_wv"], np.float32)
    wo = np.asarray(inputs["L0_attn_wo"], np.float32)
    worg = np.asarray(inputs["L0_ffn_W_org"], np.float32)
    ldown = np.asarray(inputs["L0_ffn_lora_down"], np.float32)
    lup = np.asarray(inputs["L0_ffn_lora_up"], np.float32)
    lkeys = np.asarray(inputs["L0_ffn_lora_keys"], np.float32)
    wqm = np.asarray(inputs["L0_ffn_Wq"], np.float32)

    wq_h = np.ascontiguousarray(
        wq.reshape(HQ, 128, 32, 128).transpose(0, 3, 2, 1).reshape(HQ, 128, D))
    wk_h = np.ascontiguousarray(
        wk.reshape(HKV, 128, 32, 128).transpose(0, 3, 2, 1).reshape(HKV, 128, D))
    wv_h = np.ascontiguousarray(
        wv.reshape(2, 512, 8, 4, 128).transpose(0, 2, 4, 3, 1).reshape(2, 8, 128, 2048))
    wo_h = np.ascontiguousarray(
        wo.reshape(8, 512, 8, 4, 128).transpose(0, 2, 4, 3, 1).reshape(8, 8, 128, 2048))
    worg_h = np.ascontiguousarray(
        worg.reshape(8, 512, 8, 4, 128).transpose(0, 2, 4, 3, 1).reshape(8, 8, 128, 2048))
    dmat = ldown.transpose(1, 0, 2).reshape(D, R)
    dmat_h = np.ascontiguousarray(
        dmat.reshape(32, 128, 4, 128).transpose(2, 1, 0, 3).reshape(4, 128, 32 * 128))
    umat = lup.reshape(R, D)
    umat_h = np.ascontiguousarray(
        umat.reshape(4, 128, 8, 512).transpose(2, 1, 0, 3).reshape(8, 128, 2048))
    wqm_h = np.ascontiguousarray(
        wqm.reshape(128, 32, 128).transpose(2, 1, 0).reshape(128, 32 * 128))
    keys_t = np.ascontiguousarray(lkeys.T)

    ident = np.eye(128, dtype=bfdt)
    onec = np.ones((128, 1), dtype=bfdt)
    oner = np.ones((1, 128), dtype=bfdt)
    one1 = np.ones((1, 1), dtype=bfdt)
    sel = np.zeros((NE, 512), dtype=np.float32)
    for rc in range(4):
        for p in range(128):
            sel[2 * rc + p // 64, rc * 128 + p] = 1.0
    sel = sel.astype(bfdt)

    inv_freq = 1.0 / (10000.0 ** (np.arange(0, DH, 2, dtype=np.float32) / DH))

    in_maps = []
    for c in range(8):
        b, j = c // 4, c % 4
        xr = x[b].reshape(NT, 4, 128, D)[:, j]
        x_core = np.ascontiguousarray(xr.reshape(TL, D))
        pos = (np.arange(NT)[:, None] * 512 + j * 128 +
               np.arange(128)[None, :]).reshape(TL).astype(np.float32)
        ang = pos[None, :] * inv_freq[:, None]
        ropec_a = np.empty((128, TL), np.float32)
        ropes_a = np.empty((128, TL), np.float32)
        ropec_a[0:64] = np.cos(ang); ropec_a[64:128] = np.cos(ang)
        ropes_a[0:64] = -np.sin(ang); ropes_a[64:128] = np.sin(ang)
        qi = np.arange(128)[:, None]
        ji = np.arange(512)[None, :]
        mask_a = np.where(j * 128 + qi >= ji, 0.0, NEG).astype(bfdt)
        in_maps.append({
            "x_in": x_core, "wq_h": wq_h, "wk_h": wk_h, "wv_h": wv_h,
            "wo_h": wo_h, "worg_h": worg_h, "dmat_h": dmat_h, "umat_h": umat_h,
            "wqm_h": wqm_h, "keys_t": keys_t, "rope_c": ropec_a, "rope_s": ropes_a,
            "mask_in": mask_a, "ident_in": ident, "onec_in": onec, "oner_in": oner,
            "one1_in": one1, "sel_in": sel,
        })
    return in_maps


def _assemble(results):
    y = np.empty((B, S, D), np.float32)
    for c in range(8):
        b, j = c // 4, c % 4
        y[b].reshape(NT, 4, 128, D)[:, j] = results[c]["y_out"].reshape(NT, 128, D)
    return y


def kernel(**inputs):
    nc, es = build_program()
    in_maps = _host_prep(inputs)
    try:
        res = run_bass_kernel_spmd(nc, in_maps, core_ids=list(range(8)))
    finally:
        es.close()
    return _assemble(res.results)
